# version 08
# baseline (speedup 1.0000x reference)
import numpy as np
import contextlib
import concourse.bass as bass
import concourse.mybir as mybir
from concourse.bass_utils import run_bass_kernel_spmd
from concourse.replica_groups import maybe_share_collective_output_space

F32 = mybir.dt.float32
BF16 = mybir.dt.bfloat16
I32 = mybir.dt.int32
ALU = mybir.AluOpType
AF = mybir.ActivationFunctionType
AX = mybir.AxisListType

NC = 8
TC = 1024
HALO = 3
TH = TC + HALO
D = 2048
DI = 4096
NH = 32
PD = 128
KC = D // PD
NCH = TC // PD
CH = PD
PROJ = 8480
NB = D // 16
LN2 = float(np.log(2.0))
RG = [list(range(NC))]
ENGINES = ("t", "v", "s", "g", "y")
DEBUG = False


def build_nc():
    nc = bass.Bass("TRN2", target_bir_lowering=False)

    xs = nc.dram_tensor("xs", [TH, D], F32, kind="ExternalInput")
    ns_in = nc.dram_tensor("ns_in", [D], F32, kind="ExternalInput")
    w1 = nc.dram_tensor("w1", [D, PROJ], F32, kind="ExternalInput")
    convw = nc.dram_tensor("convw", [4, DI], F32, kind="ExternalInput")
    convb = nc.dram_tensor("convb", [DI], F32, kind="ExternalInput")
    alog = nc.dram_tensor("alog", [NH], F32, kind="ExternalInput")
    dtb = nc.dram_tensor("dtb", [NH], F32, kind="ExternalInput")
    dvec = nc.dram_tensor("dvec", [DI], F32, kind="ExternalInput")
    w2 = nc.dram_tensor("w2", [DI, D], F32, kind="ExternalInput")
    wffs = nc.dram_tensor("wffs", [2 * PD, D], F32, kind="ExternalInput")
    ffb = nc.dram_tensor("ffb", [D], F32, kind="ExternalInput")
    xts = nc.dram_tensor("xts", [1, 1], F32, kind="ExternalInput")
    pmask = nc.dram_tensor("pmask", [1, NC], F32, kind="ExternalInput")
    out = nc.dram_tensor("out", [TC, D], F32, kind="ExternalOutput")
    dbg_t = {}

    def dbg_out(name, shape, dt):
        if DEBUG and name not in dbg_t:
            dbg_t[name] = nc.dram_tensor("dbg_" + name, shape, dt, kind="ExternalOutput")
        return dbg_t.get(name)

    sgd = nc.dram_tensor("sgd", [DI, TC], BF16, kind="Internal")
    y3d = nc.dram_tensor("y3d", [DI, TC], BF16, kind="Internal")
    cc_st_in = nc.dram_tensor("cc_st_in", [PD, DI + 1], F32, kind="Internal")
    cc_st_out = nc.dram_tensor(
        "cc_st_out", [NC, PD, DI + 1], F32, kind="Internal",
        addr_space=maybe_share_collective_output_space("AllGather", RG))
    cc_wq_in = nc.dram_tensor("cc_wq_in", [PD, KC * 256], BF16, kind="Internal")
    cc_wq_out = nc.dram_tensor(
        "cc_wq_out", [NC, PD, KC * 256], BF16, kind="Internal",
        addr_space=maybe_share_collective_output_space("AllGather", RG))
    cc_am_in = nc.dram_tensor("cc_am_in", [1, 8], F32, kind="Internal")
    cc_am_out = nc.dram_tensor(
        "cc_am_out", [1, 8], F32, kind="Internal",
        addr_space=maybe_share_collective_output_space("AllReduce", RG))

    ops = {e: [] for e in ENGINES}
    sems = {}
    cnt = {}
    es = contextlib.ExitStack()

    def mksem(name):
        sems[name] = es.enter_context(nc.semaphore(name))
        cnt[name] = 0
    for e in ENGINES:
        mksem("c_" + e)
    for s_ in ("sLD", "sST", "sW1", "sW2", "sSP", "sSG", "sY3", "sW1f", "sOUT",
               "sDT", "sXT", "sFF"):
        mksem(s_)

    def emit(e, fn):
        ops[e].append(fn)

    def wait(e, name, v=None):
        h = sems[name]
        tgt = cnt[name] if v is None else v
        emit(e, lambda E, h=h, tgt=tgt: E.wait_ge(h, tgt))

    def _mk(e, fn, drain):
        name = "c_" + e
        h = sems[name]
        cnt[name] += 1
        def run(E, h=h, fn=fn, drain=drain):
            if drain:
                E.drain()
            r = fn(E)
            return r.then_inc(h, 1)
        emit(e, run)

    def Tc(fn):
        _mk("t", fn, False)

    def Vc(fn):
        _mk("v", fn, True)

    def Sc(fn):
        _mk("s", fn, True)

    def Gc(fn):
        _mk("g", fn, True)

    def dbg_dump(name, src_ap, shape, dt):
        if not DEBUG:
            return
        t = dbg_out(name, shape, dt)
        dma("y", t.ap(), src_ap, "sST")

    def dma(e, dst, src, sem):
        h = sems[sem]
        pre = cnt[sem]
        cnt[sem] += 16
        def run(E, h=h, dst=dst, src=src, pre=pre):
            E.wait_ge(h, pre)
            return E.dma_start(dst, src,
                               allow_slow_non_contiguous=True).then_inc(h, 16)
        emit(e, run)

    def tnop_inc(sem):
        h = sems[sem]
        cnt[sem] += 1
        emit("t", lambda E, h=h: E.nop().then_inc(h, 1))

    _guards = {}

    def alloc(name, shape, dt, side=None):
        g = nc.sbuf_tensor(name, shape, dt, side=side) if side else nc.sbuf_tensor(name, shape, dt)
        h = g.__enter__()
        _guards[id(h)] = g
        return h

    def palloc(name, shape, dt):
        g = nc.psum_tensor(name, shape, dt)
        h = g.__enter__()
        _guards[id(h)] = g
        return h

    def free(h):
        _guards.pop(id(h)).__exit__(None, None, None)

    def barrier():
        names = ["c_t", "c_v", "c_s", "c_g", "sLD", "sST", "sW1", "sW2",
                 "sSP", "sSG", "sY3", "sOUT", "sDT", "sXT", "sFF"]
        snap = [(n, cnt[n]) for n in names]
        for e in ENGINES:
            for (n, v) in snap:
                if v > 0:
                    h2 = sems[n]
                    emit(e, lambda E, h2=h2, v=v: E.wait_ge(h2, v))

    ones_col = alloc("ones_col", [1, PD], BF16)
    ones_colf = alloc("ones_colf", [1, PD], F32)
    ident16 = alloc("ident16", [PD, PD], BF16)
    identf = alloc("identf", [PD, PD], F32)
    tri01 = alloc("tri01", [PD, PD], F32)
    ns_col = alloc("ns_col", [PD, KC], F32)
    cw_col = alloc("cw_col", [PD, NH * 4], F32)
    cb_col = alloc("cb_col", [PD, NH], F32)
    d16 = alloc("d16", [PD, NH], BF16)
    dtb_col = alloc("dtb_col", [NH, 1], F32)
    alog_col = alloc("alog_col", [NH, 1], F32)
    acol = alloc("acol", [NH, 1], F32)
    tsx_col = alloc("tsx_col", [PD, 1], F32)
    tsw_col = alloc("tsw_col", [PD, 1], F32)
    cconst = alloc("cconst", [PD, 8], F32)
    pm_row = alloc("pm_row", [1, NC], F32)
    stats = alloc("stats", [PD, 9 * 8], F32)
    statrow = alloc("statrow", [1, NCH * PD], F32)
    ones_row = alloc("ones_row", [1, 512], F32)
    dt_fm = alloc("dt_fm", [NH, TC], F32, side="right")
    cum_fm = alloc("cum_fm", [NH, TC], F32, side="right")
    df_fm = alloc("df_fm", [NH, TC], F32, side="right")
    clog = alloc("clog", [NH, NCH], F32)
    ld_seg = alloc("ld_seg", [NH, 1], F32)
    dt_tm = alloc("dt_tm", [PD, NCH * NH], F32)
    dte_tm = alloc("dte_tm", [PD, NCH * NH], F32)
    wcd_tm = alloc("wcd_tm", [PD, NCH * NH], F32)
    cum_tm = alloc("cum_tm", [PD, NCH * NH], F32)
    dc_cols = alloc("dc_cols", [PD, NCH * NH], F32)
    wm_cols = alloc("wm_cols", [PD, NC * NH], F32)
    srow = alloc("srow", [1, 512], F32)
    b16 = alloc("b16", [PD, TC], BF16, side="right")
    c16 = alloc("c16", [PD, TC], BF16, side="right")
    btm16 = alloc("btm16", [PD, NCH * PD], BF16, side="right")
    b_fm = alloc("b_fm", [PD, TC], F32, side="right")
    c_fm = alloc("c_fm", [PD, TC], F32, side="right")
    dtraw = alloc("dtraw", [NH, TC], F32, side="right")
    wam = alloc("wam", [PD, 4], F32)
    wam_r = alloc("wam_r", [1, PD], F32)
    ldg = alloc("ldg", [NH, NC + 1], F32)
    pmk = alloc("pmk", [NH, NC], F32)
    sacc = alloc("sacc", [PD, 4], F32)

    QB = {}

    _qgen = [0]

    def alloc_quant():
        g = _qgen[0]
        _qgen[0] += 1
        QB["qs1"] = alloc("qs1_%d" % g, [PD, D], F32)
        QB["qs2"] = alloc("qs2_%d" % g, [PD, D], F32)
        QB["qs3"] = alloc("qs3_%d" % g, [PD, D], F32)
        QB["qsi"] = alloc("qsi_%d" % g, [PD, D], I32)
        QB["qb1"] = alloc("qb1_%d" % g, [PD, NB], F32)
        QB["qb2"] = alloc("qb2_%d" % g, [PD, NB], F32)
        QB["qb3"] = alloc("qb3_%d" % g, [PD, NB], F32)
        QB["qbi"] = alloc("qbi_%d" % g, [PD, NB], I32)
        QB["qdq16"] = alloc("qdq16_%d" % g, [PD, D], BF16)

    def free_quant():
        for k in ("qdq16", "qbi", "qb3", "qb2", "qb1", "qsi", "qs3", "qs2", "qs1"):
            free(QB[k])

    pt32 = palloc("pt32", [PD, 512], F32)
    pt16 = palloc("pt16", [PD, 1024], BF16)

    def exp16(a):
        return a.rearrange("p (b k) -> p b k", k=16)

    def bc16(a, nb=NB):
        return a.unsqueeze(2).broadcast_to([PD, nb, 16])

    def rup(v, out_ap, in_ap, it_ap, mk_ap):
        v.tensor_scalar(it_ap, in_ap, 0.0, None, ALU.add)
        v.drain()
        v.scalar_tensor_tensor(mk_ap, in_ap, -0.5, it_ap, ALU.add, ALU.is_ge)
        v.drain()
        return v.tensor_tensor(out_ap, it_ap, mk_ap, ALU.add)

    def g0(E):
        E.memset(ones_col[:, :], 1.0)
        E.memset(ones_colf[:, :], 1.0)
        E.memset(identf[:, :], 1.0)
        E.drain()
        E.affine_select(identf[:, :], identf[:, :], [[1, PD]], ALU.is_equal, 0.0,
                        base=0, channel_multiplier=-1)
        E.drain()
        E.tensor_copy(ident16[:, :], identf[:, :])
        E.memset(tri01[:, :], 1.0)
        E.drain()
        E.affine_select(tri01[:, :], tri01[:, :], [[1, PD]], ALU.is_ge, 0.0,
                        base=0, channel_multiplier=-1)
        E.memset(cconst[:, 0:1], 19.0 * LN2)
        E.memset(cconst[:, 1:2], -19.0 * LN2)
        return E.memset(ones_row[:, :], 1.0)
    Gc(g0)

    for dst, src in [
        (ns_col[:, :], ns_in.ap().rearrange("(kc p) -> p kc", p=PD)),
        (cw_col[:, 0:NH * 4:4], bass.AP(convw, 0 * DI, [[1, PD], [PD, NH]])),
        (cw_col[:, 1:NH * 4:4], bass.AP(convw, 1 * DI, [[1, PD], [PD, NH]])),
        (cw_col[:, 2:NH * 4:4], bass.AP(convw, 2 * DI, [[1, PD], [PD, NH]])),
        (cw_col[:, 3:NH * 4:4], bass.AP(convw, 3 * DI, [[1, PD], [PD, NH]])),
        (cb_col[:, :], convb.ap().rearrange("(hc p) -> p hc", p=PD)),
        (dtb_col[:, :], dtb.ap().unsqueeze(1)),
        (alog_col[:, :], alog.ap().unsqueeze(1)),
        (pm_row[:, :], pmask[:, :]),
        (tsx_col[0:1, 0:1], xts[:, :]),
    ]:
        dma("y", dst, src, "sLD")
    dcolf = alloc("dcolf", [PD, NH], F32)
    wfbuf = alloc("wfbuf", [PD, D], F32)
    tmp_s = alloc("tmp_s", [NH, TC], F32)
    tmp_s2 = alloc("tmp_s2", [NH, TC], F32)
    dma("y", dcolf[:, :], dvec.ap().rearrange("(hc p) -> p hc", p=PD), "sLD")

    wait("s", "sLD")
    Sc(lambda E: E.activation(acol[:, :], alog_col[:, :], AF.Exp))
    wait("v", "c_s")
    wait("v", "sLD")
    def v0(E):
        E.tensor_scalar(acol[:, :], acol[:, :], -1.0, None, ALU.mult)
        E.drain()
        return E.tensor_copy(d16[:, :], dcolf[:, :])
    Vc(v0)

    dma("y", wfbuf[:, :], wffs[0:PD, :], "sLD")
    wait("v", "sLD")
    Vc(lambda E: E.tensor_reduce(wam[:, 0:1], wfbuf[:, :], AX.X, ALU.max,
                                 apply_absolute_value=True))
    wait("y", "c_v")
    dma("y", wfbuf[:, :], wffs[PD:2 * PD, :], "sLD")
    wait("v", "sLD")
    def vwa2(E):
        E.tensor_reduce(wam[:, 1:2], wfbuf[:, :], AX.X, ALU.max,
                        apply_absolute_value=True)
        E.drain()
        return E.tensor_reduce(wam[:, 2:3], wam[:, 0:2], AX.X, ALU.max)
    Vc(vwa2)
    wait("t", "c_v")
    Tc(lambda E: E.transpose(pt32[0:1, 0:PD], wam[:, 2:3], identf[:, :]))
    wait("v", "c_t")
    def vwa3(E):
        E.tensor_copy(wam_r[0:1, 0:PD], pt32[0:1, 0:PD])
        E.drain()
        return E.tensor_reduce(wam_r[0:1, 0:1], wam_r[0:1, 0:PD], AX.X, ALU.max)
    Vc(vwa3)
    wait("y", "c_v")
    dma("y", cc_am_in[0:1, 0:8], wam_r[0:1, 0:8], "sST")
    wait("g", "sST")
    Gc(lambda E: E.collective_compute("AllReduce", ALU.max, RG,
                                      [cc_am_in[:, :]], [cc_am_out[:, :]]))
    wait("y", "c_g")
    dma("y", wam_r[0:1, 0:1], cc_am_out[0:1, 0:1], "sLD")

    h16 = alloc("h16", [PD, KC * TH], BF16)
    xnr16 = alloc("xnr16", [PD, D], BF16)
    xtile = alloc("xtile", [PD, D], F32)

    tile_rows = [HALO] + [PD] * NCH
    tile_r0 = [0] + [HALO + i * PD for i in range(NCH)]

    for ti in range(9):
        r0, nr = tile_r0[ti], tile_rows[ti]
        if ti > 0:
            wait("y", "c_s")
            wait("s", "c_t")
        dma("y", xtile[0:nr, :], xs[r0:r0 + nr, :], "sLD")
        wait("s", "sLD")
        st = stats[:, ti * 8:(ti + 1) * 8]
        def s1(E, nr=nr):
            E.activation(xnr16[0:nr, :], xtile[0:nr, :], AF.Copy, accum_out=sacc[0:nr, 0:1])
            E.drain()
            return E.activation(xnr16[0:nr, :], xtile[0:nr, :], AF.Square, accum_out=sacc[0:nr, 1:2])
        Sc(s1)
        wait("v", "c_s")
        def v1(E, nr=nr, st=st):
            v = E
            v.tensor_scalar(st[0:nr, 0:1], sacc[0:nr, 0:1], 1.0 / D, None, ALU.mult)
            v.tensor_scalar(sacc[0:nr, 2:3], sacc[0:nr, 1:2], 1.0 / D, None, ALU.mult)
            v.drain()
            v.tensor_tensor(sacc[0:nr, 3:4], st[0:nr, 0:1], st[0:nr, 0:1], ALU.mult)
            v.drain()
            v.tensor_tensor(sacc[0:nr, 2:3], sacc[0:nr, 2:3], sacc[0:nr, 3:4], ALU.subtract)
            v.drain()
            v.tensor_scalar(sacc[0:nr, 3:4], sacc[0:nr, 2:3], 1e-5, None, ALU.add)
            v.drain()
            return v.reciprocal(st[0:nr, 4:5], sacc[0:nr, 3:4])
        Vc(v1)
        wait("s", "c_v")
        Sc(lambda E, nr=nr, st=st: E.activation(st[0:nr, 1:2], st[0:nr, 4:5], AF.Sqrt))
        wait("v", "c_s")
        def v2(E, nr=nr, st=st):
            v = E
            v.tensor_tensor(sacc[0:nr, 0:1], st[0:nr, 1:2], st[0:nr, 1:2], ALU.mult)
            v.drain()
            v.tensor_tensor(sacc[0:nr, 0:1], sacc[0:nr, 0:1], sacc[0:nr, 3:4], ALU.mult)
            v.drain()
            v.tensor_scalar(sacc[0:nr, 0:1], sacc[0:nr, 0:1], -0.5, 1.5, ALU.mult, ALU.add)
            v.drain()
            v.tensor_tensor(st[0:nr, 1:2], st[0:nr, 1:2], sacc[0:nr, 0:1], ALU.mult)
            v.drain()
            v.tensor_tensor(sacc[0:nr, 0:1], st[0:nr, 1:2], st[0:nr, 1:2], ALU.mult)
            v.drain()
            v.tensor_tensor(sacc[0:nr, 0:1], sacc[0:nr, 0:1], sacc[0:nr, 2:3], ALU.mult)
            v.drain()
            v.tensor_scalar(sacc[0:nr, 0:1], sacc[0:nr, 0:1], 1e-6, None, ALU.add)
            v.drain()
            return v.reciprocal(sacc[0:nr, 1:2], sacc[0:nr, 0:1])
        Vc(v2)
        wait("s", "c_v")
        Sc(lambda E, nr=nr, st=st: E.activation(st[0:nr, 2:3], sacc[0:nr, 1:2], AF.Sqrt))
        wait("v", "c_s")
        def v3(E, nr=nr, st=st):
            v = E
            v.tensor_tensor(sacc[0:nr, 2:3], st[0:nr, 2:3], st[0:nr, 2:3], ALU.mult)
            v.drain()
            v.tensor_tensor(sacc[0:nr, 2:3], sacc[0:nr, 2:3], sacc[0:nr, 0:1], ALU.mult)
            v.drain()
            v.tensor_scalar(sacc[0:nr, 2:3], sacc[0:nr, 2:3], -0.5, 1.5, ALU.mult, ALU.add)
            v.drain()
            v.tensor_tensor(st[0:nr, 2:3], st[0:nr, 2:3], sacc[0:nr, 2:3], ALU.mult)
            v.drain()
            v.tensor_tensor(st[0:nr, 3:4], st[0:nr, 1:2], st[0:nr, 2:3], ALU.mult)
            v.drain()
            v.tensor_tensor(st[0:nr, 5:6], st[0:nr, 0:1], st[0:nr, 3:4], ALU.mult)
            v.drain()
            v.tensor_scalar(st[0:nr, 5:6], st[0:nr, 5:6], -1.0, None, ALU.mult)
            v.drain()
            v.tensor_tensor(st[0:nr, 6:7], st[0:nr, 0:1], st[0:nr, 1:2], ALU.mult)
            v.drain()
            return v.tensor_scalar(st[0:nr, 6:7], st[0:nr, 6:7], -1.0, None, ALU.mult)
        Vc(v3)
        wait("s", "c_v")
        Sc(lambda E, nr=nr, st=st: E.activation(xnr16[0:nr, :], xtile[0:nr, :], AF.Identity,
                                                bias=st[0:nr, 5:6], scale=st[0:nr, 3:4]))
        wait("t", "c_s")
        for half in range(2):
            def tt(E, nr=nr, half=half):
                last = None
                for k in range(8):
                    kc = half * 8 + k
                    last = E.transpose(pt16[0:PD, k * PD:k * PD + nr],
                                       xnr16[0:nr, kc * PD:(kc + 1) * PD],
                                       ident16[0:nr, 0:nr])
                return last
            Tc(tt)
            wait("v", "c_t")
            def vh(E, r0=r0, nr=nr, half=half):
                last = None
                for k in range(8):
                    kc = half * 8 + k
                    last = E.tensor_scalar(h16[:, kc * TH + r0: kc * TH + r0 + nr],
                                           pt16[0:PD, k * PD:k * PD + nr],
                                           ns_col[:, kc:kc + 1], None, ALU.mult)
                return last
            Vc(vh)
            wait("t", "c_v")
        if ti >= 1:
            wait("y", "c_v")
            dma("y", statrow[0:1, (ti - 1) * PD:(ti - 1) * PD + nr], st[0:nr, 6:7], "sLD")

    if DEBUG:
        for e in ("v", "s", "t"):
            wait("y", "c_" + e)
    dbg_dump("h16", h16[:, :], [PD, KC * TH], BF16)

    w1buf = alloc("w1buf", [PD, 2 * KC * 256], BF16)
    psA = palloc("psA", [PD, 512], F32)
    psB = palloc("psB", [PD, 512], F32)
    psC = palloc("psC", [PD, 512], F32)
    raw16 = alloc("raw16", [PD, TH + 1], BF16)
    ssm16 = alloc("ssm16", [PD, TC], BF16)
    convt = alloc("convt", [PD, TC], F32)
    convt2 = alloc("convt2", [PD, TC], F32)
    sgst = alloc("sgst", [PD, 2 * TC], BF16)
    x_tm = alloc("x_tm", [PD, NCH * DI], BF16, side="right")

    chunk_order = [64, 65, 66] + list(range(32, 64)) + list(range(0, 32))
    GSZ = 2
    groups = [chunk_order[i:i + GSZ] for i in range(0, len(chunk_order), GSZ)]
    w1_marks = []

    def w1_load(gi):
        slot = gi % 2
        for j, c in enumerate(groups[gi]):
            c0 = c * PD
            w = min(PD, PROJ - c0)
            dst = w1buf[:, slot * KC * 256:(slot + 1) * KC * 256] \
                .rearrange("p (kc c) -> p kc c", c=256)[:, :, j * PD:j * PD + w]
            src = w1[:, c0:c0 + w].rearrange("(kc p) c -> p kc c", p=PD)
            dma("g", dst, src, "sW1")
        w1_marks.append(cnt["sW1"])

    w1_load(0)
    w1_load(1)

    h16v = h16[:, :].rearrange("p (kc t) -> p kc t", t=TH)

    def in_proj_mms(E, gi, gslot, c, tiles):
        slot = gi % 2
        w = min(PD, PROJ - c * PD)
        wt = w1buf[:, slot * KC * 256:(slot + 1) * KC * 256] \
            .rearrange("p (kc c) -> p kc c", c=256)
        last = None
        for kc in range(KC):
            for (ps, t0, tw) in tiles:
                last = E.matmul(ps[0:w, 0:tw], wt[:, kc, gslot * PD:gslot * PD + w],
                                h16v[:, kc, t0:t0 + tw],
                                start=(kc == 0), stop=(kc == KC - 1))
        return last

    def chunk_sched(idx, gi):
        gslot = idx % GSZ
        if gslot == 0:
            wait("t", "sW1", w1_marks[gi])
        return gslot

    ssm_tiles = [(psA, 0, 512), (psB, 512, 512), (psC, 1024, TH - 1024)]
    oth_tiles = [(psA, HALO, 512), (psB, HALO + 512, 512)]

    for idx, c in enumerate(chunk_order):
        gi = idx // GSZ
        gslot = chunk_sched(idx, gi)
        is_ssm = 32 <= c < 64
        tiles = ssm_tiles if is_ssm else oth_tiles
        Tc(lambda E, gi=gi, gslot=gslot, c=c, tiles=tiles: in_proj_mms(E, gi, gslot, c, tiles))
        if gslot == GSZ - 1:
            if gi + 2 < len(groups):
                wait("g", "c_t")
                w1_load(gi + 2)
        if c in (64, 65, 66):
            wait("v", "c_t")
            def vev(E, c=c):
                v = E
                last = None
                for (ps, t0, tw) in [(psA, 0, 512), (psB, 512, 512)]:
                    if c == 64:
                        v.tensor_copy(dtraw[0:NH, t0:t0 + tw], ps[0:NH, 0:tw])
                        for q in range(3):
                            last = v.tensor_copy(b_fm[q * 32:(q + 1) * 32, t0:t0 + tw],
                                                 ps[32 + q * 32:64 + q * 32, 0:tw])
                    elif c == 65:
                        v.tensor_copy(b_fm[96:PD, t0:t0 + tw], ps[0:32, 0:tw])
                        for q in range(3):
                            last = v.tensor_copy(c_fm[q * 32:(q + 1) * 32, t0:t0 + tw],
                                                 ps[32 + q * 32:64 + q * 32, 0:tw])
                    else:
                        last = v.tensor_copy(c_fm[96:PD, t0:t0 + tw], ps[0:32, 0:tw])
                return last
            Vc(vev)
            wait("t", "c_v")
        elif is_ssm:
            hc = c - 32
            wait("v", "c_t")
            def vr(E):
                v = E
                v.tensor_copy(raw16[:, 0:512], psA[:, 0:512])
                v.tensor_copy(raw16[:, 512:1024], psB[:, 0:512])
                return v.tensor_copy(raw16[:, 1024:TH], psC[:, 0:TH - 1024])
            Vc(vr)
            wait("t", "c_v")
            def vc(E, hc=hc):
                v = E
                v.tensor_scalar(convt[:, :], raw16[:, 0:TC], cw_col[:, hc * 4:hc * 4 + 1],
                                None, ALU.mult)
                v.drain()
                v.scalar_tensor_tensor(convt2[:, :], raw16[:, 1:1 + TC],
                                       cw_col[:, hc * 4 + 1:hc * 4 + 2], convt[:, :],
                                       ALU.mult, ALU.add)
                v.drain()
                v.scalar_tensor_tensor(convt[:, :], raw16[:, 2:2 + TC],
                                       cw_col[:, hc * 4 + 2:hc * 4 + 3], convt2[:, :],
                                       ALU.mult, ALU.add)
                v.drain()
                return v.scalar_tensor_tensor(convt2[:, :], raw16[:, 3:3 + TC],
                                              cw_col[:, hc * 4 + 3:hc * 4 + 4], convt[:, :],
                                              ALU.mult, ALU.add)
            Vc(vc)
            wait("s", "c_v")
            Sc(lambda E, hc=hc: E.activation(ssm16[:, :], convt2[:, :], AF.Silu,
                                             bias=cb_col[:, hc:hc + 1]))
            wait("t", "c_s")
            for half in range(2):
                def tx(E, half=half):
                    last = None
                    for k in range(4):
                        tc = half * 4 + k
                        last = E.transpose(pt16[0:PD, k * PD:(k + 1) * PD],
                                           ssm16[:, tc * PD:(tc + 1) * PD], ident16[:, :])
                    return last
                Tc(tx)
                wait("v", "c_t")
                def vx(E, hc=hc, half=half):
                    last = None
                    for k in range(4):
                        tc = half * 4 + k
                        last = E.tensor_copy(
                            x_tm[:, tc * DI + hc * PD:tc * DI + (hc + 1) * PD],
                            pt16[0:PD, k * PD:(k + 1) * PD])
                    return last
                Vc(vx)
                wait("t", "c_v")
        else:
            gc = c
            gslot2 = gc % 2
            wait("s", "c_t")
            if gc >= 2:
                wait("s", "sSP", 16 * (gc - 1))
            def sg_(E, gslot2=gslot2):
                E.activation(sgst[:, gslot2 * TC:gslot2 * TC + 512], psA[:, 0:512], AF.Silu)
                E.drain()
                return E.activation(sgst[:, gslot2 * TC + 512:(gslot2 + 1) * TC],
                                    psB[:, 0:512], AF.Silu)
            Sc(sg_)
            wait("t", "c_s")
            wait("y", "c_s")
            dma("y", sgd[gc * PD:(gc + 1) * PD, :],
                sgst[:, gslot2 * TC:(gslot2 + 1) * TC], "sSP")

        if idx == 2:
            emit_dt_path = True
            wait("s", "c_v")
            def sdt(E):
                s = E
                s.activation(tmp_s[0:NH, :], dtraw[0:NH, :], AF.Abs, bias=dtb_col[0:NH, 0:1])
                s.drain()
                s.activation(tmp_s[0:NH, :], tmp_s[0:NH, :], AF.Exp, scale=-1.0)
                s.drain()
                s.activation(tmp_s[0:NH, :], tmp_s[0:NH, :], AF.Ln, bias=1.0)
                s.drain()
                return s.activation(tmp_s2[0:NH, :], dtraw[0:NH, :], AF.Relu,
                                    bias=dtb_col[0:NH, 0:1])
            Sc(sdt)
            wait("v", "c_s")
            def vdt(E):
                v = E
                v.tensor_tensor(dt_fm[0:NH, :], tmp_s[0:NH, :], tmp_s2[0:NH, :], ALU.add)
                v.drain()
                v.tensor_scalar(tmp_s[0:NH, :], dt_fm[0:NH, :], acol[0:NH, 0:1], None, ALU.mult)
                v.drain()
                for ci in range(NCH):
                    sl = slice(ci * CH, (ci + 1) * CH)
                    v.tensor_tensor_scan(cum_fm[0:NH, sl], tmp_s[0:NH, sl], tmp_s[0:NH, sl],
                                         0.0, ALU.add, ALU.bypass)
                v.drain()
                for ci in range(NCH):
                    v.tensor_copy(clog[0:NH, ci:ci + 1],
                                  cum_fm[0:NH, (ci + 1) * CH - 1:(ci + 1) * CH])
                v.drain()
                v.tensor_reduce(ld_seg[0:NH, 0:1], clog[0:NH, :], AX.X, ALU.add)
                v.drain()
                v.tensor_tensor_scan(tmp_s[0:NH, 0:NCH], clog[0:NH, :], clog[0:NH, :],
                                     0.0, ALU.add, ALU.bypass)
                v.drain()
                return v.scalar_tensor_tensor(tmp_s2[0:NH, 0:NCH], tmp_s[0:NH, 0:NCH],
                                              -1.0, ld_seg[0:NH, 0:1].broadcast_to([NH, NCH]),
                                              ALU.mult, ALU.add)
            Vc(vdt)
            wait("s", "c_v")
            def sdt2(E):
                s = E
                s.activation(tmp_s[0:NH, 0:NCH], tmp_s2[0:NH, 0:NCH], AF.Exp)
                s.drain()
                s.activation(tmp_s2[0:NH, 0:NCH], clog[0:NH, :], AF.Exp)
                s.drain()
                s.activation(df_fm[0:NH, :], cum_fm[0:NH, :], AF.Exp)
                s.drain()
                last = None
                for ci in range(NCH):
                    sl = slice(ci * CH, (ci + 1) * CH)
                    last = s.activation(dtraw[0:NH, sl], cum_fm[0:NH, sl], AF.Exp,
                                        scale=-1.0, bias=clog[0:NH, ci:ci + 1])
                return last
            Sc(sdt2)
            for ci in range(NCH):
                wait("t", "c_s")
                wait("t", "c_v")
                def tdt(E, ci=ci):
                    sl = slice(ci * CH, (ci + 1) * CH)
                    E.transpose(pt32[0:CH, 0:NH], dt_fm[0:NH, sl], identf[0:NH, 0:NH])
                    E.transpose(pt32[0:CH, NH:2 * NH], dtraw[0:NH, sl],
                                identf[0:NH, 0:NH])
                    return E.transpose(pt32[0:CH, 2 * NH:3 * NH], cum_fm[0:NH, sl],
                                       identf[0:NH, 0:NH])
                Tc(tdt)
                wait("v", "c_t")
                def vdt2(E, ci=ci):
                    v = E
                    v.tensor_copy(dt_tm[:, ci * NH:(ci + 1) * NH], pt32[0:CH, 0:NH])
                    v.tensor_copy(dte_tm[:, ci * NH:(ci + 1) * NH], pt32[0:CH, NH:2 * NH])
                    return v.tensor_copy(cum_tm[:, ci * NH:(ci + 1) * NH],
                                         pt32[0:CH, 2 * NH:3 * NH])
                Vc(vdt2)
            wait("y", "c_s")
            dma("y", srow[0:1, 0:NH * NCH], tmp_s[0:NH, 0:NCH], "sLD")
            dma("y", srow[0:1, NH * NCH:2 * NH * NCH], tmp_s2[0:NH, 0:NCH], "sLD")
            wait("t", "sLD")
            wait("t", "c_v")
            def twc2(E):
                E.matmul(pt32[:, 0:256], ones_colf[:, :], srow[0:1, 0:256],
                         start=True, stop=False)
                return E.matmul(pt32[:, 256:512], ones_colf[:, :], srow[0:1, 256:512],
                                start=False, stop=True)
            Tc(twc2)
            wait("v", "c_t")
            def vwc2(E):
                v = E
                v.tensor_copy(wcd_tm[:, :], pt32[:, 0:256])
                v.tensor_copy(dc_cols[:, :], pt32[:, 256:512])
                v.drain()
                return v.tensor_tensor(
                    wcd_tm[:, :].rearrange("p (h c) -> p h c", c=NCH),
                    wcd_tm[:, :].rearrange("p (h c) -> p h c", c=NCH),
                    dte_tm[:, :].rearrange("p (c h) -> p c h", h=NH).transpose([0, 2, 1]),
                    ALU.mult)
            Vc(vwc2)
            wait("t", "c_v")
            def vbc(E):
                E.tensor_copy(b16[:, :], b_fm[:, :])
                return E.tensor_copy(c16[:, :], c_fm[:, :])
            Vc(vbc)
            wait("t", "c_v")
            for half in range(2):
                def tb(E, half=half):
                    last = None
                    for k in range(4):
                        ci = half * 4 + k
                        last = E.transpose(pt16[0:PD, k * PD:(k + 1) * PD],
                                           b16[:, ci * CH:(ci + 1) * CH], ident16[:, :])
                    return last
                Tc(tb)
                wait("v", "c_t")
                def vb(E, half=half):
                    last = None
                    for k in range(4):
                        ci = half * 4 + k
                        last = E.tensor_copy(btm16[:, ci * PD:(ci + 1) * PD],
                                             pt16[0:PD, k * PD:(k + 1) * PD])
                    return last
                Vc(vb)
                wait("t", "c_v")

    if DEBUG:
        for e in ("v", "s", "t"):
            wait("y", "c_" + e)
    dbg_dump("x_tm", x_tm[:, :], [PD, NCH * DI], BF16)
    dbg_dump("b_fm", b_fm[:, :], [PD, TC], F32)
    dbg_dump("c_fm", c_fm[:, :], [PD, TC], F32)
    dbg_dump("dt_fm", dt_fm[0:NH, :], [NH, TC], F32)
    dbg_dump("cum_fm", cum_fm[0:NH, :], [NH, TC], F32)
    dbg_dump("dte_fm", dtraw[0:NH, :], [NH, TC], F32)
    dbg_dump("df_fm", df_fm[0:NH, :], [NH, TC], F32)
    dbg_dump("wcd", wcd_tm[:, :], [PD, NCH * NH], F32)
    dbg_dump("dcc", dc_cols[:, :], [PD, NCH * NH], F32)
    dbg_dump("dt_tm", dt_tm[:, :], [PD, NCH * NH], F32)
    dbg_dump("cum_tm", cum_tm[:, :], [PD, NCH * NH], F32)

    free(sgst)
    free(convt2)
    free(convt)
    free(ssm16)
    free(raw16)
    free(psC)
    free(psB)
    free(psA)
    free(w1buf)
    free(xtile)
    free(xnr16)
    free(h16)
    free(pt16)
    free(pt32)
    barrier()

    x2p16 = alloc("x2p16", [PD, DI], BF16)
    sseg = palloc("sseg", [PD, DI], F32)
    for ci in range(NCH):
        wait("v", "c_t")
        def vx3(E, ci=ci):
            v = E
            dslc = dt_tm[:, ci * NH:(ci + 1) * NH]
            wslc = wcd_tm[:, :].rearrange("p (h c) -> p h c", c=NCH)[:, :, ci]
            v.tensor_tensor(x2p16[:, :].rearrange("p (h q) -> p h q", q=PD),
                            x_tm[:, ci * DI:(ci + 1) * DI].rearrange("p (h q) -> p h q", q=PD),
                            dslc.unsqueeze(2).broadcast_to([PD, NH, PD]), ALU.mult)
            v.drain()
            return v.tensor_tensor(x2p16[:, :].rearrange("p (h q) -> p h q", q=PD),
                                   x2p16[:, :].rearrange("p (h q) -> p h q", q=PD),
                                   wslc.unsqueeze(2).broadcast_to([PD, NH, PD]), ALU.mult)
        Vc(vx3)
        wait("t", "c_v")
        def tz(E, ci=ci):
            last = None
            for h in range(NH):
                last = E.matmul(sseg[:, h * PD:(h + 1) * PD],
                                btm16[:, ci * PD:(ci + 1) * PD],
                                x2p16[:, h * PD:(h + 1) * PD],
                                start=(ci == 0 and h % 4 == 0),
                                stop=(ci == NCH - 1 and h % 4 == 3))
            return last
        Tc(tz)
    wait("v", "c_t")
    ssg_sb = alloc("ssg_sb", [PD, DI], F32)
    Vc(lambda E: E.tensor_copy(ssg_sb[:, :], sseg[:, :]))
    free(sseg)
    def vldz(E):
        E.memset(wam[:, 3:4], 0.0)
        E.drain()
        return E.tensor_copy(wam[0:NH, 3:4], ld_seg[0:NH, 0:1])
    Vc(vldz)
    wait("y", "c_v")
    dma("y", cc_st_in[:, 0:DI], ssg_sb[:, :], "sST")
    dma("y", cc_st_in[:, DI:DI + 1], wam[:, 3:4], "sST")
    wait("g", "sST")
    Gc(lambda E: E.collective_compute("AllGather", ALU.bypass, RG,
                                      [cc_st_in[:, :]], [cc_st_out[:, :, :]]))
    dbg_dump("sseg", ssg_sb[:, :], [PD, DI], F32)
    free(ssg_sb)
    free(x2p16)
    barrier()

    p32b = palloc("pt32b", [PD, 512], F32)
    p16b = palloc("pt16b", [PD, 1024], BF16)
    alloc_quant()
    wait("t", "sLD")
    def ttsw(E):
        E.matmul(p32b[:, 0:1], ones_colf[:, :], wam_r[0:1, 0:1], start=True, stop=False)
        return E.matmul(p32b[:, 1:2], ones_colf[:, :], tsx_col[0:1, 0:1],
                        start=False, stop=True)
    Tc(ttsw)
    wait("v", "c_t")
    def vtsw(E):
        E.tensor_scalar(tsw_col[:, 0:1], p32b[:, 0:1], 1.0 / (448.0 * 6.0), None, ALU.mult)
        E.drain()
        return E.tensor_copy(tsx_col[:, 0:1], p32b[:, 1:2])
    Vc(vtsw)
    wait("t", "c_v")

    def emit_quant(src_ap, ts_ap, nr=PD, nb=NB):
        qs1, qs2, qs3, qsi = QB["qs1"], QB["qs2"], QB["qs3"], QB["qsi"]
        qb1, qb2, qb3, qbi = QB["qb1"], QB["qb2"], QB["qb3"], QB["qbi"]
        qdq16 = QB["qdq16"]
        def vq1(E):
            v = E
            v.tensor_reduce(qb1[0:nr, :], exp16(src_ap), AX.X, ALU.max,
                            apply_absolute_value=True)
            v.reciprocal(sacc[0:nr, 0:1], ts_ap[0:nr, 0:1])
            v.drain()
            v.tensor_scalar(qb1[0:nr, :], qb1[0:nr, :], sacc[0:nr, 0:1], 1.0 / 6.0,
                            ALU.mult, ALU.mult)
            v.drain()
            return v.tensor_scalar(qb1[0:nr, :], qb1[0:nr, :], 448.0, 2.0 ** -20,
                                   ALU.min, ALU.max)
        Vc(vq1)
        wait("s", "c_v")
        Sc(lambda E: E.activation(qb2[0:nr, :], qb1[0:nr, :], AF.Ln))
        wait("v", "c_s")
        def vq2(E):
            v = E
            v.tensor_scalar(qb2[0:nr, :], qb2[0:nr, :], 1.0 / LN2, 16.0, ALU.mult, ALU.add)
            v.drain()
            v.tensor_scalar(qbi[0:nr, :], qb2[0:nr, :], 0.0, None, ALU.add)
            v.drain()
            v.scalar_tensor_tensor(qb3[0:nr, :], qb2[0:nr, :], 0.0, qbi[0:nr, :],
                                   ALU.add, ALU.is_lt)
            v.drain()
            v.tensor_tensor(qb2[0:nr, :], qbi[0:nr, :], qb3[0:nr, :], ALU.subtract)
            v.drain()
            return v.tensor_scalar(qb2[0:nr, :], qb2[0:nr, :], 10.0, None, ALU.max)
        Vc(vq2)
        wait("s", "c_v")
        def sq2(E):
            E.activation(qb3[0:nr, :], qb2[0:nr, :], AF.Exp, scale=-LN2, bias=cconst[0:nr, 0:1])
            E.drain()
            return E.activation(qb2[0:nr, :], qb2[0:nr, :], AF.Exp, scale=LN2,
                                bias=cconst[0:nr, 1:2])
        Sc(sq2)
        wait("v", "c_s")
        def vq3(E):
            v = E
            v.tensor_tensor(qb1[0:nr, :], qb1[0:nr, :], qb3[0:nr, :], ALU.mult)
            v.drain()
            rup(v, qb3[0:nr, :], qb1[0:nr, :], qbi[0:nr, :], qb1[0:nr, :])
            v.drain()
            v.tensor_tensor(qb3[0:nr, :], qb3[0:nr, :], qb2[0:nr, :], ALU.mult)
            v.drain()
            v.tensor_scalar(qb3[0:nr, :], qb3[0:nr, :], 512.0, 0.25, ALU.mult, ALU.add)
            v.drain()
            rup(v, qb2[0:nr, :], qb3[0:nr, :], qbi[0:nr, :], qb1[0:nr, :])
            v.drain()
            v.tensor_scalar(qb1[0:nr, :], qb2[0:nr, :], 1.0 / 512.0, None, ALU.mult)
            v.drain()
            v.tensor_scalar(qb2[0:nr, :], qb1[0:nr, :], 0.0, None, ALU.is_equal)
            v.drain()
            v.tensor_tensor(qb2[0:nr, :], qb2[0:nr, :], qb1[0:nr, :], ALU.add)
            v.drain()
            v.tensor_scalar(qb2[0:nr, :], qb2[0:nr, :], ts_ap[0:nr, 0:1], None, ALU.mult)
            v.drain()
            v.reciprocal(qb2[0:nr, :], qb2[0:nr, :])
            v.drain()
            return v.tensor_tensor(exp16(qs1[0:nr, :]), exp16(src_ap),
                                   bc16(qb2[0:nr, :], nb), ALU.mult)
        Vc(vq3)
        wait("s", "c_v")
        def sq3(E):
            E.activation(qs2[0:nr, :], qs1[0:nr, :], AF.Abs)
            E.drain()
            return E.activation(qs3[0:nr, :], qs1[0:nr, :], AF.Sign)
        Sc(sq3)
        wait("v", "c_s")
        def vq4(E):
            v = E
            v.tensor_scalar(qs1[0:nr, :], qs2[0:nr, :], 2.0, None, ALU.mult)
            v.drain()
            v.scalar_tensor_tensor(qs1[0:nr, :], qs2[0:nr, :], 2.0, qs1[0:nr, :],
                                   ALU.add, ALU.min)
            v.drain()
            v.tensor_scalar(qs2[0:nr, :], qs2[0:nr, :], 0.5, 4.0, ALU.mult, ALU.add)
            v.drain()
            v.tensor_tensor(qs1[0:nr, :], qs1[0:nr, :], qs2[0:nr, :], ALU.min)
            v.drain()
            v.tensor_scalar(qs1[0:nr, :], qs1[0:nr, :], 7.0, None, ALU.min)
            v.drain()
            rup(v, qs2[0:nr, :], qs1[0:nr, :], qsi[0:nr, :], qs1[0:nr, :])
            v.drain()
            v.tensor_scalar(qs1[0:nr, :], qs2[0:nr, :], 0.5, None, ALU.mult)
            v.drain()
            v.scalar_tensor_tensor(qs1[0:nr, :], qs2[0:nr, :], -2.0, qs1[0:nr, :],
                                   ALU.add, ALU.max)
            v.drain()
            v.tensor_scalar(qs2[0:nr, :], qs2[0:nr, :], 2.0, -8.0, ALU.mult, ALU.add)
            v.drain()
            v.tensor_tensor(qs1[0:nr, :], qs1[0:nr, :], qs2[0:nr, :], ALU.max)
            v.drain()
            v.tensor_tensor(qs1[0:nr, :], qs1[0:nr, :], qs3[0:nr, :], ALU.mult)
            v.drain()
            return v.tensor_tensor(exp16(qdq16[0:nr, :]), exp16(qs1[0:nr, :]),
                                   bc16(qb1[0:nr, :], nb), ALU.mult)
        Vc(vq4)

    wqt_sb = alloc("wqt_sb", [PD, KC * 256], BF16)
    for wi in range(2):
        wait("y", "c_v")
        dma("y", wfbuf[:, :], wffs[wi * PD:(wi + 1) * PD, :], "sLD")
        wait("v", "sLD")
        emit_quant(wfbuf[:, :], tsw_col)
        wait("t", "c_v")
        for half in range(2):
            def twq(E, half=half, q16=QB["qdq16"]):
                last = None
                for k in range(8):
                    kc = half * 8 + k
                    last = E.transpose(p16b[0:PD, k * PD:(k + 1) * PD],
                                       q16[:, kc * PD:(kc + 1) * PD], ident16[:, :])
                return last
            Tc(twq)
            wait("v", "c_t")
            def vwq(E, wi=wi, half=half):
                last = None
                for k in range(8):
                    kc = half * 8 + k
                    last = E.tensor_copy(
                        wqt_sb[:, kc * 256 + wi * PD:kc * 256 + (wi + 1) * PD],
                        p16b[0:PD, k * PD:(k + 1) * PD])
                return last
            Vc(vwq)
            wait("t", "c_v")
    wait("y", "c_v")
    dma("y", cc_wq_in[:, :], wqt_sb[:, :], "sST")
    wait("g", "sST")
    Gc(lambda E: E.collective_compute("AllGather", ALU.bypass, RG,
                                      [cc_wq_in[:, :]], [cc_wq_out[:, :, :]]))
    free(wqt_sb)
    free_quant()
    barrier()

    wait("y", "c_g")
    for u in range(NC):
        dma("y", ldg[0:NH, u:u + 1], cc_st_out[u, 0:NH, DI:DI + 1], "sLD")
    dma("y", pmk[0:1, :], pmask[:, :], "sLD")
    wait("t", "sLD")
    def tpm(E):
        return E.matmul(p32b[0:NH, 0:NC], ones_colf[0:1, 0:NH], pmk[0:1, :],
                        start=True, stop=True)
    Tc(tpm)
    wait("v", "c_t")
    Vc(lambda E: E.tensor_copy(pmk[0:NH, :], p32b[0:NH, 0:NC]))
    wait("v", "sLD")
    def vwm2(E):
        v = E
        v.tensor_tensor(ldg[0:NH, 0:NC], ldg[0:NH, 0:NC], pmk[0:NH, :], ALU.mult)
        v.drain()
        v.tensor_tensor_scan(tmp_s[0:NH, 0:NC], ldg[0:NH, 0:NC], ldg[0:NH, 0:NC],
                             0.0, ALU.add, ALU.bypass)
        v.drain()
        v.tensor_reduce(ldg[0:NH, NC:NC + 1], ldg[0:NH, 0:NC], AX.X, ALU.add)
        v.drain()
        return v.scalar_tensor_tensor(tmp_s[0:NH, 0:NC], tmp_s[0:NH, 0:NC], -1.0,
                                      ldg[0:NH, NC:NC + 1].broadcast_to([NH, NC]),
                                      ALU.mult, ALU.add)
    Vc(vwm2)
    wait("s", "c_v")
    Sc(lambda E: E.activation(tmp_s[0:NH, 0:NC], tmp_s[0:NH, 0:NC], AF.Exp))
    wait("v", "c_s")
    Vc(lambda E: E.tensor_tensor(tmp_s[0:NH, 0:NC], tmp_s[0:NH, 0:NC], pmk[0:NH, :],
                                 ALU.mult))
    wait("y", "c_v")
    dma("y", srow[0:1, 0:NH * NC], tmp_s[0:NH, 0:NC], "sLD")
    wait("t", "sLD")
    wait("t", "c_v")
    def twm2(E):
        return E.matmul(p32b[:, 0:256], ones_colf[:, :], srow[0:1, 0:256],
                        start=True, stop=True)
    Tc(twm2)
    wait("v", "c_t")
    Vc(lambda E: E.tensor_copy(wm_cols[:, 0:256], p32b[:, 0:256]))
    wait("t", "c_v")

    s_run = alloc("s_run", [PD, DI], F32)
    s16 = alloc("s16", [PD, DI], BF16)
    sgu = alloc("sgu", [PD, 2 * DI], F32)
    sgu_sems = ["sLD", "sDT", "sXT", "sFF"]
    vdone = []
    for u in range(NC):
        slot = u % 2
        if u >= 2:
            wait("y", "c_v", vdone[u - 2])
        for j in range(8):
            dma("y", sgu[:, slot * DI + j * 512:slot * DI + (j + 1) * 512],
                cc_st_out[u, :, j * 512:(j + 1) * 512], sgu_sems[j % 4])
        for s_ in sgu_sems:
            wait("v", s_)
        for h in range(NH):
            if u == 0:
                def vch0(E, h=h, slot=slot):
                    return E.tensor_scalar(s_run[:, h * PD:(h + 1) * PD],
                                           sgu[:, slot * DI + h * PD:slot * DI + (h + 1) * PD],
                                           wm_cols[:, h * NC:h * NC + 1], None, ALU.mult)
                Vc(vch0)
            else:
                def vchu(E, h=h, u=u, slot=slot):
                    return E.scalar_tensor_tensor(s_run[:, h * PD:(h + 1) * PD],
                                                  sgu[:, slot * DI + h * PD:slot * DI + (h + 1) * PD],
                                                  wm_cols[:, h * NC + u:h * NC + u + 1],
                                                  s_run[:, h * PD:(h + 1) * PD],
                                                  ALU.mult, ALU.add)
                Vc(vchu)
        vdone.append(cnt["c_v"])
    Vc(lambda E: E.tensor_copy(s16[:, :], s_run[:, :]))
    if DEBUG:
        wait("y", "c_v")
    dbg_dump("sinit", s_run[:, :], [PD, DI], F32)
    dbg_dump("wm", wm_cols[:, :], [PD, NC * NH], F32)
    free(sgu)
    free(p16b)
    free(p32b)
    barrier()

    crow = alloc("crow", [1, 1024], F32)
    dfrow = alloc("dfrow", [1, 1024], F32)
    sgbuf = alloc("sgbuf", [PD, 2 * NH * PD], BF16)
    x3c = alloc("x3c", [PD, 1024], BF16)
    x2c = alloc("x2c", [PD, 1024], BF16)
    xdc = alloc("xdc", [PD, 1024], BF16)
    wpre = alloc("wpre", [PD, NH * PD], BF16)
    cbm = alloc("cbm", [PD, PD], BF16)
    cdd16 = alloc("cdd16", [PD, 8 * PD], BF16)
    y3st = alloc("y3st", [PD, NH * PD], BF16)
    psY = palloc("psY", [PD, 1024], F32)
    psZ = palloc("psZ", [PD, 1024], F32)
    psR = palloc("psR", [PD, 1024], F32)
    psCB = palloc("psCB", [PD, PD], F32)

    def sg_load(ci, slot):
        dst = sgbuf[:, slot * NH * PD:(slot + 1) * NH * PD] \
            .rearrange("p (h t) -> p h t", t=PD)
        src = sgd[:, ci * PD:(ci + 1) * PD].rearrange("(h p) t -> p h t", p=PD)
        dma("y", dst, src, "sSG")

    sg_load(0, 0)

    vmarks = []
    for ci in range(NCH):
        slot = ci % 2
        if ci + 1 < NCH:
            if ci >= 1:
                wait("y", "c_v", vmarks[ci - 1])
            sg_load(ci + 1, (ci + 1) % 2)
        wait("v", "c_t")
        wait("v", "sY3")
        def tcb(E, ci=ci):
            return E.matmul(psCB[:, :], b16[:, ci * CH:(ci + 1) * CH],
                            c16[:, ci * CH:(ci + 1) * CH], start=True, stop=True)
        Tc(tcb)
        wait("v", "c_t")
        Vc(lambda E: E.tensor_tensor(cbm[:, :], psCB[:, :], tri01[:, :], ALU.mult))
        for hg in range(4):
            wait("y", "c_t")
            dma("y", crow[0:1, :], cum_fm[hg * 8:(hg + 1) * 8, ci * CH:(ci + 1) * CH], "sDT")
            wait("t", "c_v")
            wait("t", "c_s")
            wait("t", "sDT")
            def tcr(E, ci=ci, hg=hg):
                last = None
                for j in range(8):
                    last = E.matmul(psR[:, j * PD:(j + 1) * PD], ones_colf[:, :],
                                    crow[0:1, j * CH:(j + 1) * CH],
                                    start=(j in (0, 4)), stop=(j in (3, 7)))
                return last
            Tc(tcr)
            wait("s", "c_t")
            def swp(E, ci=ci, hg=hg):
                s = E
                for j in range(8):
                    h = hg * 8 + j
                    s.activation(wpre[:, h * PD:(h + 1) * PD], psR[:, j * PD:(j + 1) * PD],
                                 AF.Relu, scale=-1.0,
                                 bias=cum_tm[:, ci * NH + h:ci * NH + h + 1])
                s.drain()
                last = None
                for j in range(8):
                    h = hg * 8 + j
                    last = s.activation(wpre[:, h * PD:(h + 1) * PD],
                                        wpre[:, h * PD:(h + 1) * PD], AF.Exp, scale=-1.0)
                return last
            Sc(swp)
        wait("v", "c_s")
        def vwp(E):
            return E.tensor_tensor(wpre[:, :].rearrange("p (h q) -> p h q", q=PD),
                                   wpre[:, :].rearrange("p (h q) -> p h q", q=PD),
                                   cbm[:, :].unsqueeze(1).broadcast_to([PD, NH, PD]),
                                   ALU.mult)
        Vc(vwp)
        wait("v", "sSG", 16 * (ci + 1))
        for hg in range(4):
            wait("y", "c_t")
            dma("y", dfrow[0:1, :], df_fm[hg * 8:(hg + 1) * 8, ci * CH:(ci + 1) * CH], "sDT")
            wait("t", "c_v")
            wait("t", "sDT")
            def tdf(E, ci=ci, hg=hg):
                last = None
                for j in range(8):
                    last = E.matmul(psR[:, j * PD:(j + 1) * PD], ones_colf[:, :],
                                    dfrow[0:1, j * CH:(j + 1) * CH],
                                    start=(j in (0, 4)), stop=(j in (3, 7)))
                return last
            Tc(tdf)
            wait("v", "c_t")
            def vcd(E, ci=ci):
                return E.tensor_tensor(
                    cdd16[:, :].rearrange("p (j q) -> p j q", q=PD),
                    c16[:, ci * CH:(ci + 1) * CH].unsqueeze(1).broadcast_to([PD, 8, PD]),
                    psR[:, 0:1024].rearrange("p (j q) -> p j q", q=PD), ALU.mult)
            Vc(vcd)
            def vxh(E, ci=ci, hg=hg):
                v = E
                xc = x_tm[:, ci * DI + hg * 1024:ci * DI + (hg + 1) * 1024] \
                    .rearrange("p (h q) -> p h q", q=PD)
                dcol = dt_tm[:, ci * NH + hg * 8:ci * NH + (hg + 1) * 8]
                ecol = dte_tm[:, ci * NH + hg * 8:ci * NH + (hg + 1) * 8]
                v.tensor_tensor(x3c[:, :].rearrange("p (h q) -> p h q", q=PD), xc,
                                dcol.unsqueeze(2).broadcast_to([PD, 8, PD]), ALU.mult)
                v.drain()
                v.tensor_tensor(x2c[:, :].rearrange("p (h q) -> p h q", q=PD),
                                x3c[:, :].rearrange("p (h q) -> p h q", q=PD),
                                ecol.unsqueeze(2).broadcast_to([PD, 8, PD]), ALU.mult)
                v.drain()
                return v.tensor_tensor(xdc[:, :].rearrange("p (h q) -> p h q", q=PD), xc,
                                       d16[:, hg * 8:(hg + 1) * 8].unsqueeze(2)
                                       .broadcast_to([PD, 8, PD]), ALU.mult)
            Vc(vxh)
            wait("t", "c_v")
            def ty(E, hg=hg):
                last = None
                for j in range(8):
                    h = hg * 8 + j
                    E.matmul(psY[:, j * PD:(j + 1) * PD], x3c[:, j * PD:(j + 1) * PD],
                             wpre[:, h * PD:(h + 1) * PD], start=(j in (0, 4)), stop=False)
                    E.matmul(psY[:, j * PD:(j + 1) * PD], xdc[:, j * PD:(j + 1) * PD],
                             ident16[:, :], start=False, stop=False)
                    last = E.matmul(psY[:, j * PD:(j + 1) * PD], s16[:, h * PD:(h + 1) * PD],
                                    cdd16[:, j * PD:(j + 1) * PD], start=False,
                                    stop=(j in (3, 7)))
                return last
            Tc(ty)
            def tz2(E, ci=ci, hg=hg):
                last = None
                for j in range(8):
                    last = E.matmul(psZ[:, j * PD:(j + 1) * PD],
                                    btm16[:, ci * PD:(ci + 1) * PD],
                                    x2c[:, j * PD:(j + 1) * PD],
                                    start=(j in (0, 4)), stop=(j in (3, 7)))
                return last
            Tc(tz2)
            wait("v", "c_t")
            def vy(E, ci=ci, hg=hg, slot=slot):
                v = E
                sgv = sgbuf[:, slot * NH * PD + hg * 1024:slot * NH * PD + (hg + 1) * 1024]
                v.tensor_tensor(y3st[:, hg * 1024:(hg + 1) * 1024], psY[:, 0:1024], sgv,
                                ALU.mult)
                v.drain()
                for j in range(8):
                    h = hg * 8 + j
                    v.scalar_tensor_tensor(s_run[:, h * PD:(h + 1) * PD],
                                           s_run[:, h * PD:(h + 1) * PD],
                                           dc_cols[:, h * NCH + ci:h * NCH + ci + 1],
                                           psZ[:, j * PD:(j + 1) * PD], ALU.mult, ALU.add)
                v.drain()
                return v.tensor_copy(s16[:, hg * 1024:(hg + 1) * 1024],
                                     s_run[:, hg * 1024:(hg + 1) * 1024])
            Vc(vy)
        vmarks.append(cnt["c_v"])
        wait("y", "c_v", cnt["c_v"])
        def y3w(E, ci=ci):
            dst = y3d[:, ci * PD:(ci + 1) * PD].rearrange("(h p) t -> p h t", p=PD)
            return E.dma_start(dst, y3st[:, :].rearrange("p (h t) -> p h t", t=PD))
        h_ = sems["sY3"]
        cnt["sY3"] += 16
        emit("y", lambda E, h_=h_, y3w=y3w, ci=ci: y3w(E, ci).then_inc(h_, 16))

    free(psCB)
    free(psR)
    free(psZ)
    free(psY)
    free(y3st)
    free(cdd16)
    free(cbm)
    free(wpre)
    free(xdc)
    free(x2c)
    free(x3c)
    free(sgbuf)
    free(dfrow)
    free(crow)
    free(s16)
    free(s_run)
    free(x_tm)
    barrier()

    free(tmp_s2)
    free(tmp_s)
    free(dtraw)
    free(c_fm)
    free(b_fm)
    free(btm16)
    free(c16)
    free(b16)
    free(df_fm)
    free(cum_fm)
    free(dt_fm)
    barrier()
    xt2 = alloc("xtile2", [PD, D], F32)
    s_full = alloc("s_full", [PD, NCH * D], F32)
    y3sb = alloc("y3sb", [PD, NH * TC], BF16)
    w2b = alloc("w2b", [PD, 2 * 512], BF16)
    psO = palloc("psO", [PD, 4096], F32)

    wait("y", "sY3")
    dma("y", y3sb[:, :].rearrange("p (h t) -> p h t", t=TC),
        y3d[:, :].rearrange("(h p) t -> p h t", p=PD), "sLD")
    w2_marks = []

    def w2_load(di, dq, slot):
        dst = w2b[:, slot * 512:(slot + 1) * 512]
        src = w2[di * PD:(di + 1) * PD, dq * 512:(dq + 1) * 512]
        dma("g", dst, src, "sW2")
        w2_marks.append(cnt["sW2"])

    pairs = [(dq, di) for dq in range(4) for di in range(NH)]
    w2_load(pairs[0][1], pairs[0][0], 0)
    w2_load(pairs[1][1], pairs[1][0], 1)
    wait("t", "sLD")
    for j in range(len(pairs)):
        dq, di = pairs[j]
        if True:
            wait("t", "sW2", w2_marks[j])
            slot = j % 2
            def top(E, dq=dq, di=di, slot=slot):
                last = None
                for tc in range(NCH):
                    last = E.matmul(psO[:, tc * 512:(tc + 1) * 512],
                                    y3sb[:, di * TC + tc * PD:di * TC + (tc + 1) * PD],
                                    w2b[:, slot * 512:(slot + 1) * 512],
                                    start=(di == 0), stop=False)
                return last
            Tc(top)
            if j + 2 < len(pairs):
                wait("g", "c_t")
                nd, ni = pairs[j + 2]
                w2_load(ni, nd, j % 2)
        if di == NH - 1:
            emit_tail = True
        else:
            continue
        def tmr(E, dq=dq):
            last = None
            for tc in range(NCH):
                last = E.matmul(psO[:, tc * 512:(tc + 1) * 512],
                                statrow[0:1, tc * PD:(tc + 1) * PD],
                                ones_row[0:1, 0:512], start=False, stop=True)
            return last
        Tc(tmr)
        for tc in range(NCH):
            r0 = HALO + tc * PD
            wait("y", "c_v")
            dma("y", xt2[:, 0:512], xs[r0:r0 + PD, dq * 512:(dq + 1) * 512], "sXT")
            wait("v", "sXT")
            wait("v", "c_t")
            st = stats[:, (tc + 1) * 8:(tc + 2) * 8]
            def vs5(E, tc=tc, dq=dq, st=st):
                return E.scalar_tensor_tensor(
                    s_full[:, tc * D + dq * 512:tc * D + (dq + 1) * 512],
                    xt2[:, 0:512], st[:, 1:2], psO[:, tc * 512:(tc + 1) * 512],
                    ALU.mult, ALU.add)
            Vc(vs5)
        wait("t", "c_v")
    free(psO)
    free(w2b)
    free(y3sb)
    barrier()

    if DEBUG:
        for e in ("v", "s", "t"):
            wait("y", "c_" + e)
    dbg_dump("sfull", s_full[:, :], [PD, NCH * D], F32)
    dbg_dump("y3", y3d[:, :], [DI, TC], BF16)
    dbg_dump("sg", sgd[:, :], [DI, TC], BF16)

    ffb2 = alloc("ffb2", [1, D], F32)
    os_col = alloc("os_col", [PD, 2], F32)
    ffb_row = alloc("ffb_row", [1, D], F32)
    dma("y", ffb_row[0:1, :], ffb.ap().unsqueeze(0), "sLD")
    wait("v", "sLD")
    def vos(E):
        v = E
        v.tensor_tensor(os_col[:, 0:1], tsx_col[:, 0:1], tsw_col[:, 0:1], ALU.mult)
        v.drain()
        v.reciprocal(os_col[:, 1:2], os_col[:, 0:1])
        v.drain()
        return v.tensor_scalar(ffb2[0:1, :], ffb_row[0:1, :], os_col[0:1, 1:2], None,
                               ALU.mult)
    Vc(vos)
    wait("y", "c_v")
    free(ffb_row)
    wdqa = alloc("wdqa", [PD, KC * D], BF16)
    wait("y", "c_g")
    for ng in range(4):
        for r in range(2):
            dst = wdqa[:, :].rearrange("p (kc n) -> p kc n", n=D) \
                [:, :, ng * 512 + r * 256:ng * 512 + (r + 1) * 256]
            src = cc_wq_out[2 * ng + r, :, :].rearrange("p (kc n) -> p kc n", n=256)
            dma("y", dst, src, ["sFF", "sW2"][r])
    xtq = alloc("xtq", [PD, D], F32)
    xres = alloc("xres", [PD, 512], F32)
    xdqc = alloc("xdqc", [PD, 2 * KC * PD], BF16)
    QB["qs1"] = xt2[:, 0:1024]
    QB["qs3"] = xt2[:, 1024:2048]
    QB["qs2"] = alloc("q6_qs2", [PD, 1024], F32)
    QB["qsi"] = alloc("q6_qsi", [PD, 1024], I32)
    QB["qb1"] = alloc("q6_qb1", [PD, 64], F32)
    QB["qb2"] = alloc("q6_qb2", [PD, 64], F32)
    QB["qb3"] = alloc("q6_qb3", [PD, 64], F32)
    QB["qbi"] = alloc("q6_qbi", [PD, 64], I32)
    q6_dq = alloc("q6_dq", [PD, 2048], BF16)
    p16c = palloc("pt16c", [PD, 1024], BF16)
    psG = palloc("psG", [PD, 512], F32)
    gtile = alloc("gtile", [PD, 512], F32)
    ftile = alloc("ftile", [PD, 512], F32)
    xnp_marks = [0] * NCH

    def emit_xq_a(tc):
        r0 = HALO + tc * PD
        if tc >= 1:
            wait("y", "c_s", xnp_marks[tc - 1])
        dma("y", xtq[:, :], xs[r0:r0 + PD, :], "sXT")
        wait("v", "c_t")
        st = stats[:, (tc + 1) * 8:(tc + 2) * 8]
        for qh in range(2):
            wait("s", "sXT")
            wait("s", "c_v")
            Sc(lambda E, st=st, qh=qh, q2=QB["qs2"]: E.activation(
                q2[:, :], xtq[:, qh * 1024:(qh + 1) * 1024], AF.Identity,
                bias=st[:, 6:7], scale=st[:, 1:2]))
            wait("v", "c_s")
            QB["qdq16"] = q6_dq[:, qh * 1024:(qh + 1) * 1024]
            emit_quant(QB["qs2"][:, :], tsx_col, nb=64)
        xnp_marks[tc] = cnt["c_s"]

    def emit_xq_b(tc):
        slot = tc % 2
        wait("t", "c_v")
        for qh in range(2):
            def txq(E, qh=qh):
                last = None
                for k in range(8):
                    last = E.transpose(p16c[0:PD, k * PD:(k + 1) * PD],
                                       q6_dq[:, qh * 1024 + k * PD:qh * 1024 + (k + 1) * PD],
                                       ident16[:, :])
                return last
            Tc(txq)
            wait("v", "c_t")
            def vxq(E, qh=qh, slot=slot):
                last = None
                for k in range(8):
                    kc = qh * 8 + k
                    last = E.tensor_copy(
                        xdqc[:, slot * KC * PD + kc * PD:slot * KC * PD + (kc + 1) * PD],
                        p16c[0:PD, k * PD:(k + 1) * PD])
                return last
            Vc(vxq)
            wait("t", "c_v")

    emit_xq_a(0)
    for tc in range(NCH):
        slot = tc % 2
        r0 = HALO + tc * PD
        emit_xq_b(tc)
        if tc + 1 < NCH:
            emit_xq_a(tc + 1)
        if tc == 0:
            wait("t", "sFF")
            wait("t", "sW2")
        for ng in range(4):
            def tff(E, tc=tc, ng=ng, slot=slot):
                for kc in range(KC):
                    E.matmul(psG[:, 0:512],
                             xdqc[:, slot * KC * PD + kc * PD:slot * KC * PD + (kc + 1) * PD],
                             wdqa[:, :].rearrange("p (kc n) -> p kc n", n=D)
                             [:, kc, ng * 512:(ng + 1) * 512],
                             start=(kc == 0), stop=False)
                return E.matmul(psG[:, 0:512], ones_colf[0:1, 0:PD],
                                ffb2[0:1, ng * 512:(ng + 1) * 512], start=False, stop=True)
            Tc(tff)
            wait("s", "c_t")
            Sc(lambda E: E.activation(gtile[:, :], psG[:, 0:512], AF.Silu,
                                      scale=os_col[:, 0:1]))
            wait("y", "c_v")
            dma("y", xres[:, :], xs[r0:r0 + PD, ng * 512:(ng + 1) * 512], "sXT")
            wait("v", "c_s")
            wait("v", "sXT")
            def vf(E, tc=tc, ng=ng):
                v = E
                v.tensor_tensor(ftile[:, :], s_full[:, tc * D + ng * 512:tc * D + (ng + 1) * 512],
                                gtile[:, :], ALU.mult)
                v.drain()
                return v.tensor_tensor(ftile[:, :], ftile[:, :], xres[:, :], ALU.add)
            Vc(vf)
            wait("t", "c_v")
            wait("y", "c_v")
            dma("y", out[tc * PD:(tc + 1) * PD, ng * 512:(ng + 1) * 512], ftile[:, :], "sOUT")
            wait("v", "sOUT")

    if DEBUG:
        for e in ("v", "s", "t"):
            wait("y", "c_" + e)
    dbg_dump("wq", cc_wq_out[:, :, :].rearrange("a p c -> (a p) c"), [NC * PD, KC * 256], BF16)
    wait("y", "sOUT")

    with nc.Block() as block:
        block.tensor(lambda E: [f(E) for f in ops["t"]])
        block.vector(lambda E: [f(E) for f in ops["v"]])
        block.scalar(lambda E: [f(E) for f in ops["s"]])
        block.gpsimd(lambda E: [f(E) for f in ops["g"]])
        block.sync(lambda E: [f(E) for f in ops["y"]])
    es.close()
    return nc


_NC_CACHE = None


def kernel(**inputs):
    global _NC_CACHE
    x = np.asarray(inputs["L0_attn_x"], np.float32).reshape(8192, D)
    ns = np.ascontiguousarray(np.asarray(inputs["L0_attn_norm_scale"], np.float32))
    w1 = np.ascontiguousarray(np.asarray(inputs["L0_attn_in_proj_w"], np.float32))
    convw = np.ascontiguousarray(np.asarray(inputs["L0_attn_conv_weight"], np.float32))
    convb = np.ascontiguousarray(np.asarray(inputs["L0_attn_conv_bias"], np.float32))
    alog = np.ascontiguousarray(np.asarray(inputs["L0_attn_A_log"], np.float32))
    dtb = np.ascontiguousarray(np.asarray(inputs["L0_attn_dt_bias"], np.float32))
    dvec = np.ascontiguousarray(np.asarray(inputs["L0_attn_D"], np.float32))
    w2 = np.ascontiguousarray(np.asarray(inputs["L0_attn_out_proj_w"], np.float32))
    wff = np.ascontiguousarray(np.asarray(inputs["L0_ffn_weight"], np.float32))
    ffb = np.ascontiguousarray(np.asarray(inputs["L0_ffn_bias"], np.float32))
    xts = np.asarray(inputs["L0_ffn_x_tensor_scale"], np.float32).reshape(1, 1)

    if _NC_CACHE is None:
        _NC_CACHE = build_nc()
    nc = _NC_CACHE

    in_maps = []
    for k in range(NC):
        lo = k * TC
        if k == 0:
            xsk = np.concatenate([np.zeros((HALO, D), np.float32), x[0:TC]], axis=0)
        else:
            xsk = x[lo - HALO:lo + TC]
        pm = np.zeros((1, NC), np.float32)
        pm[0, :k] = 1.0
        in_maps.append(dict(
            xs=np.ascontiguousarray(xsk), ns_in=ns, w1=w1, convw=convw, convb=convb,
            alog=alog, dtb=dtb, dvec=dvec, w2=w2,
            wffs=np.ascontiguousarray(wff[k * 256:(k + 1) * 256, :]),
            ffb=ffb, xts=xts, pmask=pm,
        ))
    res = run_bass_kernel_spmd(nc, in_maps, core_ids=list(range(NC)))
    outs = [np.asarray(res.results[k]["out"]) for k in range(NC)]
    return np.concatenate(outs, axis=0).reshape(1, 8192, D).astype(np.float32)


# version 09 (final)
# speedup vs baseline: 1.0060x; 1.0060x over previous
import numpy as np
import contextlib
import concourse.bass as bass
import concourse.mybir as mybir
from concourse.bass_utils import run_bass_kernel_spmd
from concourse.replica_groups import maybe_share_collective_output_space

F32 = mybir.dt.float32
BF16 = mybir.dt.bfloat16
I32 = mybir.dt.int32
ALU = mybir.AluOpType
AF = mybir.ActivationFunctionType
AX = mybir.AxisListType

NC = 8
TC = 1024
HALO = 3
TH = TC + HALO
D = 2048
DI = 4096
NH = 32
PD = 128
KC = D // PD
NCH = TC // PD
CH = PD
PROJ = 8480
NB = D // 16
LN2 = float(np.log(2.0))
RG = [list(range(NC))]
ENGINES = ("t", "v", "s", "g", "y")
DEBUG = False


def build_nc():
    nc = bass.Bass("TRN2", target_bir_lowering=False)

    xs = nc.dram_tensor("xs", [TH, D], F32, kind="ExternalInput")
    ns_in = nc.dram_tensor("ns_in", [D], F32, kind="ExternalInput")
    w1 = nc.dram_tensor("w1", [D, PROJ], F32, kind="ExternalInput")
    convw = nc.dram_tensor("convw", [4, DI], F32, kind="ExternalInput")
    convb = nc.dram_tensor("convb", [DI], F32, kind="ExternalInput")
    alog = nc.dram_tensor("alog", [NH], F32, kind="ExternalInput")
    dtb = nc.dram_tensor("dtb", [NH], F32, kind="ExternalInput")
    dvec = nc.dram_tensor("dvec", [DI], F32, kind="ExternalInput")
    w2 = nc.dram_tensor("w2", [DI, D], F32, kind="ExternalInput")
    wffs = nc.dram_tensor("wffs", [2 * PD, D], F32, kind="ExternalInput")
    ffb = nc.dram_tensor("ffb", [D], F32, kind="ExternalInput")
    xts = nc.dram_tensor("xts", [1, 1], F32, kind="ExternalInput")
    pmask = nc.dram_tensor("pmask", [1, NC], F32, kind="ExternalInput")
    out = nc.dram_tensor("out", [TC, D], F32, kind="ExternalOutput")
    dbg_t = {}

    def dbg_out(name, shape, dt):
        if DEBUG and name not in dbg_t:
            dbg_t[name] = nc.dram_tensor("dbg_" + name, shape, dt, kind="ExternalOutput")
        return dbg_t.get(name)

    sgd = nc.dram_tensor("sgd", [DI, TC], BF16, kind="Internal")
    y3d = nc.dram_tensor("y3d", [DI, TC], BF16, kind="Internal")
    cc_st_in = nc.dram_tensor("cc_st_in", [PD, DI + 1], F32, kind="Internal")
    cc_st_out = nc.dram_tensor(
        "cc_st_out", [NC, PD, DI + 1], F32, kind="Internal",
        addr_space=maybe_share_collective_output_space("AllGather", RG))
    cc_wq_in = nc.dram_tensor("cc_wq_in", [PD, KC * 256], BF16, kind="Internal")
    cc_wq_out = nc.dram_tensor(
        "cc_wq_out", [NC, PD, KC * 256], BF16, kind="Internal",
        addr_space=maybe_share_collective_output_space("AllGather", RG))
    cc_am_in = nc.dram_tensor("cc_am_in", [1, 8], F32, kind="Internal")
    cc_am_out = nc.dram_tensor(
        "cc_am_out", [1, 8], F32, kind="Internal",
        addr_space=maybe_share_collective_output_space("AllReduce", RG))

    ops = {e: [] for e in ENGINES}
    sems = {}
    cnt = {}
    es = contextlib.ExitStack()

    def mksem(name):
        sems[name] = es.enter_context(nc.semaphore(name))
        cnt[name] = 0
    for e in ENGINES:
        mksem("c_" + e)
    for s_ in ("sLD", "sST", "sW1", "sW2", "sSP", "sSG", "sY3", "sW1f", "sOUT",
               "sDT", "sXT", "sFF"):
        mksem(s_)

    def emit(e, fn):
        ops[e].append(fn)

    def wait(e, name, v=None):
        h = sems[name]
        tgt = cnt[name] if v is None else v
        emit(e, lambda E, h=h, tgt=tgt: E.wait_ge(h, tgt))

    def _mk(e, fn, drain):
        name = "c_" + e
        h = sems[name]
        cnt[name] += 1
        def run(E, h=h, fn=fn, drain=drain):
            if drain:
                E.drain()
            r = fn(E)
            return r.then_inc(h, 1)
        emit(e, run)

    def Tc(fn):
        _mk("t", fn, False)

    def Vc(fn):
        _mk("v", fn, True)

    def Sc(fn):
        _mk("s", fn, True)

    def Gc(fn):
        _mk("g", fn, True)

    def dbg_dump(name, src_ap, shape, dt):
        if not DEBUG:
            return
        t = dbg_out(name, shape, dt)
        dma("y", t.ap(), src_ap, "sST")

    def dma(e, dst, src, sem):
        h = sems[sem]
        pre = cnt[sem]
        cnt[sem] += 16
        def run(E, h=h, dst=dst, src=src, pre=pre):
            E.wait_ge(h, pre)
            return E.dma_start(dst, src,
                               allow_slow_non_contiguous=True).then_inc(h, 16)
        emit(e, run)

    def tnop_inc(sem):
        h = sems[sem]
        cnt[sem] += 1
        emit("t", lambda E, h=h: E.nop().then_inc(h, 1))

    _guards = {}

    def alloc(name, shape, dt, side=None):
        g = nc.sbuf_tensor(name, shape, dt, side=side) if side else nc.sbuf_tensor(name, shape, dt)
        h = g.__enter__()
        _guards[id(h)] = g
        return h

    def palloc(name, shape, dt):
        g = nc.psum_tensor(name, shape, dt)
        h = g.__enter__()
        _guards[id(h)] = g
        return h

    def free(h):
        _guards.pop(id(h)).__exit__(None, None, None)

    def barrier():
        names = ["c_t", "c_v", "c_s", "c_g", "sLD", "sST", "sW1", "sW2",
                 "sSP", "sSG", "sY3", "sOUT", "sDT", "sXT", "sFF"]
        snap = [(n, cnt[n]) for n in names]
        for e in ENGINES:
            for (n, v) in snap:
                if v > 0:
                    h2 = sems[n]
                    emit(e, lambda E, h2=h2, v=v: E.wait_ge(h2, v))

    ones_col = alloc("ones_col", [1, PD], BF16)
    ones_colf = alloc("ones_colf", [1, PD], F32)
    ident16 = alloc("ident16", [PD, PD], BF16)
    identf = alloc("identf", [PD, PD], F32)
    tri01 = alloc("tri01", [PD, PD], F32)
    ns_col = alloc("ns_col", [PD, KC], F32)
    cw_col = alloc("cw_col", [PD, NH * 4], F32)
    cb_col = alloc("cb_col", [PD, NH], F32)
    d16 = alloc("d16", [PD, NH], BF16)
    dtb_col = alloc("dtb_col", [NH, 1], F32)
    alog_col = alloc("alog_col", [NH, 1], F32)
    acol = alloc("acol", [NH, 1], F32)
    tsx_col = alloc("tsx_col", [PD, 1], F32)
    tsw_col = alloc("tsw_col", [PD, 1], F32)
    cconst = alloc("cconst", [PD, 8], F32)
    pm_row = alloc("pm_row", [1, NC], F32)
    stats = alloc("stats", [PD, 9 * 8], F32)
    statrow = alloc("statrow", [1, NCH * PD], F32)
    ones_row = alloc("ones_row", [1, 512], F32)
    dt_fm = alloc("dt_fm", [NH, TC], F32, side="right")
    cum_fm = alloc("cum_fm", [NH, TC], F32, side="right")
    df_fm = alloc("df_fm", [NH, TC], F32, side="right")
    clog = alloc("clog", [NH, NCH], F32)
    ld_seg = alloc("ld_seg", [NH, 1], F32)
    dt_tm = alloc("dt_tm", [PD, NCH * NH], F32)
    dte_tm = alloc("dte_tm", [PD, NCH * NH], F32)
    wcd_tm = alloc("wcd_tm", [PD, NCH * NH], F32)
    cum_tm = alloc("cum_tm", [PD, NCH * NH], F32)
    dc_cols = alloc("dc_cols", [PD, NCH * NH], F32)
    wm_cols = alloc("wm_cols", [PD, NC * NH], F32)
    srow = alloc("srow", [1, 512], F32)
    b16 = alloc("b16", [PD, TC], BF16, side="right")
    c16 = alloc("c16", [PD, TC], BF16, side="right")
    btm16 = alloc("btm16", [PD, NCH * PD], BF16, side="right")
    b_fm = alloc("b_fm", [PD, TC], F32, side="right")
    c_fm = alloc("c_fm", [PD, TC], F32, side="right")
    dtraw = alloc("dtraw", [NH, TC], F32, side="right")
    wam = alloc("wam", [PD, 4], F32)
    wam_r = alloc("wam_r", [1, PD], F32)
    ldg = alloc("ldg", [NH, NC + 1], F32)
    pmk = alloc("pmk", [NH, NC], F32)
    sacc = alloc("sacc", [PD, 4], F32)

    QB = {}

    _qgen = [0]

    def alloc_quant():
        g = _qgen[0]
        _qgen[0] += 1
        QB["qs1"] = alloc("qs1_%d" % g, [PD, D], F32)
        QB["qs2"] = alloc("qs2_%d" % g, [PD, D], F32)
        QB["qs3"] = alloc("qs3_%d" % g, [PD, D], F32)
        QB["qsi"] = alloc("qsi_%d" % g, [PD, D], I32)
        QB["qb1"] = alloc("qb1_%d" % g, [PD, NB], F32)
        QB["qb2"] = alloc("qb2_%d" % g, [PD, NB], F32)
        QB["qb3"] = alloc("qb3_%d" % g, [PD, NB], F32)
        QB["qbi"] = alloc("qbi_%d" % g, [PD, NB], I32)
        QB["qdq16"] = alloc("qdq16_%d" % g, [PD, D], BF16)

    def free_quant():
        for k in ("qdq16", "qbi", "qb3", "qb2", "qb1", "qsi", "qs3", "qs2", "qs1"):
            free(QB[k])

    pt32 = palloc("pt32", [PD, 512], F32)
    pt16 = palloc("pt16", [PD, 1024], BF16)

    def exp16(a):
        return a.rearrange("p (b k) -> p b k", k=16)

    def bc16(a):
        return a.unsqueeze(2).broadcast_to([PD, NB, 16])

    def rup(v, out_ap, in_ap, it_ap, mk_ap):
        v.tensor_scalar(it_ap, in_ap, 0.0, None, ALU.add)
        v.drain()
        v.scalar_tensor_tensor(mk_ap, in_ap, -0.5, it_ap, ALU.add, ALU.is_ge)
        v.drain()
        return v.tensor_tensor(out_ap, it_ap, mk_ap, ALU.add)

    def g0(E):
        E.memset(ones_col[:, :], 1.0)
        E.memset(ones_colf[:, :], 1.0)
        E.memset(identf[:, :], 1.0)
        E.drain()
        E.affine_select(identf[:, :], identf[:, :], [[1, PD]], ALU.is_equal, 0.0,
                        base=0, channel_multiplier=-1)
        E.drain()
        E.tensor_copy(ident16[:, :], identf[:, :])
        E.memset(tri01[:, :], 1.0)
        E.drain()
        E.affine_select(tri01[:, :], tri01[:, :], [[1, PD]], ALU.is_ge, 0.0,
                        base=0, channel_multiplier=-1)
        E.memset(cconst[:, 0:1], 19.0 * LN2)
        E.memset(cconst[:, 1:2], -19.0 * LN2)
        return E.memset(ones_row[:, :], 1.0)
    Gc(g0)

    for dst, src in [
        (ns_col[:, :], ns_in.ap().rearrange("(kc p) -> p kc", p=PD)),
        (cw_col[:, 0:NH * 4:4], bass.AP(convw, 0 * DI, [[1, PD], [PD, NH]])),
        (cw_col[:, 1:NH * 4:4], bass.AP(convw, 1 * DI, [[1, PD], [PD, NH]])),
        (cw_col[:, 2:NH * 4:4], bass.AP(convw, 2 * DI, [[1, PD], [PD, NH]])),
        (cw_col[:, 3:NH * 4:4], bass.AP(convw, 3 * DI, [[1, PD], [PD, NH]])),
        (cb_col[:, :], convb.ap().rearrange("(hc p) -> p hc", p=PD)),
        (dtb_col[:, :], dtb.ap().unsqueeze(1)),
        (alog_col[:, :], alog.ap().unsqueeze(1)),
        (pm_row[:, :], pmask[:, :]),
        (tsx_col[0:1, 0:1], xts[:, :]),
    ]:
        dma("y", dst, src, "sLD")
    dcolf = alloc("dcolf", [PD, NH], F32)
    wfbuf = alloc("wfbuf", [PD, D], F32)
    tmp_s = alloc("tmp_s", [NH, TC], F32)
    tmp_s2 = alloc("tmp_s2", [NH, TC], F32)
    dma("y", dcolf[:, :], dvec.ap().rearrange("(hc p) -> p hc", p=PD), "sLD")

    wait("s", "sLD")
    Sc(lambda E: E.activation(acol[:, :], alog_col[:, :], AF.Exp))
    wait("v", "c_s")
    wait("v", "sLD")
    def v0(E):
        E.tensor_scalar(acol[:, :], acol[:, :], -1.0, None, ALU.mult)
        E.drain()
        return E.tensor_copy(d16[:, :], dcolf[:, :])
    Vc(v0)

    dma("y", wfbuf[:, :], wffs[0:PD, :], "sLD")
    wait("v", "sLD")
    Vc(lambda E: E.tensor_reduce(wam[:, 0:1], wfbuf[:, :], AX.X, ALU.max,
                                 apply_absolute_value=True))
    wait("y", "c_v")
    dma("y", wfbuf[:, :], wffs[PD:2 * PD, :], "sLD")
    wait("v", "sLD")
    def vwa2(E):
        E.tensor_reduce(wam[:, 1:2], wfbuf[:, :], AX.X, ALU.max,
                        apply_absolute_value=True)
        E.drain()
        return E.tensor_reduce(wam[:, 2:3], wam[:, 0:2], AX.X, ALU.max)
    Vc(vwa2)
    wait("t", "c_v")
    Tc(lambda E: E.transpose(pt32[0:1, 0:PD], wam[:, 2:3], identf[:, :]))
    wait("v", "c_t")
    def vwa3(E):
        E.tensor_copy(wam_r[0:1, 0:PD], pt32[0:1, 0:PD])
        E.drain()
        return E.tensor_reduce(wam_r[0:1, 0:1], wam_r[0:1, 0:PD], AX.X, ALU.max)
    Vc(vwa3)
    wait("y", "c_v")
    dma("y", cc_am_in[0:1, 0:8], wam_r[0:1, 0:8], "sST")
    wait("g", "sST")
    Gc(lambda E: E.collective_compute("AllReduce", ALU.max, RG,
                                      [cc_am_in[:, :]], [cc_am_out[:, :]]))
    wait("y", "c_g")
    dma("y", wam_r[0:1, 0:1], cc_am_out[0:1, 0:1], "sLD")

    h16 = alloc("h16", [PD, KC * TH], BF16)
    xnr16 = alloc("xnr16", [PD, D], BF16)
    xtile = alloc("xtile", [PD, D], F32)

    tile_rows = [HALO] + [PD] * NCH
    tile_r0 = [0] + [HALO + i * PD for i in range(NCH)]

    for ti in range(9):
        r0, nr = tile_r0[ti], tile_rows[ti]
        if ti > 0:
            wait("y", "c_s")
            wait("s", "c_t")
        dma("y", xtile[0:nr, :], xs[r0:r0 + nr, :], "sLD")
        wait("s", "sLD")
        st = stats[:, ti * 8:(ti + 1) * 8]
        def s1(E, nr=nr):
            E.activation(xnr16[0:nr, :], xtile[0:nr, :], AF.Copy, accum_out=sacc[0:nr, 0:1])
            E.drain()
            return E.activation(xnr16[0:nr, :], xtile[0:nr, :], AF.Square, accum_out=sacc[0:nr, 1:2])
        Sc(s1)
        wait("v", "c_s")
        def v1(E, nr=nr, st=st):
            v = E
            v.tensor_scalar(st[0:nr, 0:1], sacc[0:nr, 0:1], 1.0 / D, None, ALU.mult)
            v.tensor_scalar(sacc[0:nr, 2:3], sacc[0:nr, 1:2], 1.0 / D, None, ALU.mult)
            v.drain()
            v.tensor_tensor(sacc[0:nr, 3:4], st[0:nr, 0:1], st[0:nr, 0:1], ALU.mult)
            v.drain()
            v.tensor_tensor(sacc[0:nr, 2:3], sacc[0:nr, 2:3], sacc[0:nr, 3:4], ALU.subtract)
            v.drain()
            v.tensor_scalar(sacc[0:nr, 3:4], sacc[0:nr, 2:3], 1e-5, None, ALU.add)
            v.drain()
            return v.reciprocal(st[0:nr, 4:5], sacc[0:nr, 3:4])
        Vc(v1)
        wait("s", "c_v")
        Sc(lambda E, nr=nr, st=st: E.activation(st[0:nr, 1:2], st[0:nr, 4:5], AF.Sqrt))
        wait("v", "c_s")
        def v2(E, nr=nr, st=st):
            v = E
            v.tensor_tensor(sacc[0:nr, 0:1], st[0:nr, 1:2], st[0:nr, 1:2], ALU.mult)
            v.drain()
            v.tensor_tensor(sacc[0:nr, 0:1], sacc[0:nr, 0:1], sacc[0:nr, 3:4], ALU.mult)
            v.drain()
            v.tensor_scalar(sacc[0:nr, 0:1], sacc[0:nr, 0:1], -0.5, 1.5, ALU.mult, ALU.add)
            v.drain()
            v.tensor_tensor(st[0:nr, 1:2], st[0:nr, 1:2], sacc[0:nr, 0:1], ALU.mult)
            v.drain()
            v.tensor_tensor(sacc[0:nr, 0:1], st[0:nr, 1:2], st[0:nr, 1:2], ALU.mult)
            v.drain()
            v.tensor_tensor(sacc[0:nr, 0:1], sacc[0:nr, 0:1], sacc[0:nr, 2:3], ALU.mult)
            v.drain()
            v.tensor_scalar(sacc[0:nr, 0:1], sacc[0:nr, 0:1], 1e-6, None, ALU.add)
            v.drain()
            return v.reciprocal(sacc[0:nr, 1:2], sacc[0:nr, 0:1])
        Vc(v2)
        wait("s", "c_v")
        Sc(lambda E, nr=nr, st=st: E.activation(st[0:nr, 2:3], sacc[0:nr, 1:2], AF.Sqrt))
        wait("v", "c_s")
        def v3(E, nr=nr, st=st):
            v = E
            v.tensor_tensor(sacc[0:nr, 2:3], st[0:nr, 2:3], st[0:nr, 2:3], ALU.mult)
            v.drain()
            v.tensor_tensor(sacc[0:nr, 2:3], sacc[0:nr, 2:3], sacc[0:nr, 0:1], ALU.mult)
            v.drain()
            v.tensor_scalar(sacc[0:nr, 2:3], sacc[0:nr, 2:3], -0.5, 1.5, ALU.mult, ALU.add)
            v.drain()
            v.tensor_tensor(st[0:nr, 2:3], st[0:nr, 2:3], sacc[0:nr, 2:3], ALU.mult)
            v.drain()
            v.tensor_tensor(st[0:nr, 3:4], st[0:nr, 1:2], st[0:nr, 2:3], ALU.mult)
            v.drain()
            v.tensor_tensor(st[0:nr, 5:6], st[0:nr, 0:1], st[0:nr, 3:4], ALU.mult)
            v.drain()
            v.tensor_scalar(st[0:nr, 5:6], st[0:nr, 5:6], -1.0, None, ALU.mult)
            v.drain()
            v.tensor_tensor(st[0:nr, 6:7], st[0:nr, 0:1], st[0:nr, 1:2], ALU.mult)
            v.drain()
            return v.tensor_scalar(st[0:nr, 6:7], st[0:nr, 6:7], -1.0, None, ALU.mult)
        Vc(v3)
        wait("s", "c_v")
        Sc(lambda E, nr=nr, st=st: E.activation(xnr16[0:nr, :], xtile[0:nr, :], AF.Identity,
                                                bias=st[0:nr, 5:6], scale=st[0:nr, 3:4]))
        wait("t", "c_s")
        for half in range(2):
            def tt(E, nr=nr, half=half):
                last = None
                for k in range(8):
                    kc = half * 8 + k
                    last = E.transpose(pt16[0:PD, k * PD:k * PD + nr],
                                       xnr16[0:nr, kc * PD:(kc + 1) * PD],
                                       ident16[0:nr, 0:nr])
                return last
            Tc(tt)
            wait("v", "c_t")
            def vh(E, r0=r0, nr=nr, half=half):
                last = None
                for k in range(8):
                    kc = half * 8 + k
                    last = E.tensor_scalar(h16[:, kc * TH + r0: kc * TH + r0 + nr],
                                           pt16[0:PD, k * PD:k * PD + nr],
                                           ns_col[:, kc:kc + 1], None, ALU.mult)
                return last
            Vc(vh)
            wait("t", "c_v")
        if ti >= 1:
            wait("y", "c_v")
            dma("y", statrow[0:1, (ti - 1) * PD:(ti - 1) * PD + nr], st[0:nr, 6:7], "sLD")

    if DEBUG:
        for e in ("v", "s", "t"):
            wait("y", "c_" + e)
    dbg_dump("h16", h16[:, :], [PD, KC * TH], BF16)

    w1buf = alloc("w1buf", [PD, 2 * KC * 256], BF16)
    psA = palloc("psA", [PD, 512], F32)
    psB = palloc("psB", [PD, 512], F32)
    psC = palloc("psC", [PD, 512], F32)
    raw16 = alloc("raw16", [PD, TH + 1], BF16)
    ssm16 = alloc("ssm16", [PD, TC], BF16)
    convt = alloc("convt", [PD, TC], F32)
    convt2 = alloc("convt2", [PD, TC], F32)
    sgst = alloc("sgst", [PD, 2 * TC], BF16)
    x_tm = alloc("x_tm", [PD, NCH * DI], BF16, side="right")

    chunk_order = [64, 65, 66] + list(range(32, 64)) + list(range(0, 32))
    GSZ = 2
    groups = [chunk_order[i:i + GSZ] for i in range(0, len(chunk_order), GSZ)]
    w1_marks = []

    def w1_load(gi):
        slot = gi % 2
        for j, c in enumerate(groups[gi]):
            c0 = c * PD
            w = min(PD, PROJ - c0)
            dst = w1buf[:, slot * KC * 256:(slot + 1) * KC * 256] \
                .rearrange("p (kc c) -> p kc c", c=256)[:, :, j * PD:j * PD + w]
            src = w1[:, c0:c0 + w].rearrange("(kc p) c -> p kc c", p=PD)
            dma("g", dst, src, "sW1")
        w1_marks.append(cnt["sW1"])

    w1_load(0)
    w1_load(1)

    h16v = h16[:, :].rearrange("p (kc t) -> p kc t", t=TH)

    def in_proj_mms(E, gi, gslot, c, tiles):
        slot = gi % 2
        w = min(PD, PROJ - c * PD)
        wt = w1buf[:, slot * KC * 256:(slot + 1) * KC * 256] \
            .rearrange("p (kc c) -> p kc c", c=256)
        last = None
        for kc in range(KC):
            for (ps, t0, tw) in tiles:
                last = E.matmul(ps[0:w, 0:tw], wt[:, kc, gslot * PD:gslot * PD + w],
                                h16v[:, kc, t0:t0 + tw],
                                start=(kc == 0), stop=(kc == KC - 1))
        return last

    def chunk_sched(idx, gi):
        gslot = idx % GSZ
        if gslot == 0:
            wait("t", "sW1", w1_marks[gi])
        return gslot

    ssm_tiles = [(psA, 0, 512), (psB, 512, 512), (psC, 1024, TH - 1024)]
    oth_tiles = [(psA, HALO, 512), (psB, HALO + 512, 512)]

    for idx, c in enumerate(chunk_order):
        gi = idx // GSZ
        gslot = chunk_sched(idx, gi)
        is_ssm = 32 <= c < 64
        tiles = ssm_tiles if is_ssm else oth_tiles
        Tc(lambda E, gi=gi, gslot=gslot, c=c, tiles=tiles: in_proj_mms(E, gi, gslot, c, tiles))
        if gslot == GSZ - 1:
            if gi + 2 < len(groups):
                wait("g", "c_t")
                w1_load(gi + 2)
        if c in (64, 65, 66):
            wait("v", "c_t")
            def vev(E, c=c):
                v = E
                last = None
                for (ps, t0, tw) in [(psA, 0, 512), (psB, 512, 512)]:
                    if c == 64:
                        v.tensor_copy(dtraw[0:NH, t0:t0 + tw], ps[0:NH, 0:tw])
                        for q in range(3):
                            last = v.tensor_copy(b_fm[q * 32:(q + 1) * 32, t0:t0 + tw],
                                                 ps[32 + q * 32:64 + q * 32, 0:tw])
                    elif c == 65:
                        v.tensor_copy(b_fm[96:PD, t0:t0 + tw], ps[0:32, 0:tw])
                        for q in range(3):
                            last = v.tensor_copy(c_fm[q * 32:(q + 1) * 32, t0:t0 + tw],
                                                 ps[32 + q * 32:64 + q * 32, 0:tw])
                    else:
                        last = v.tensor_copy(c_fm[96:PD, t0:t0 + tw], ps[0:32, 0:tw])
                return last
            Vc(vev)
            wait("t", "c_v")
        elif is_ssm:
            hc = c - 32
            wait("v", "c_t")
            def vr(E):
                v = E
                v.tensor_copy(raw16[:, 0:512], psA[:, 0:512])
                v.tensor_copy(raw16[:, 512:1024], psB[:, 0:512])
                return v.tensor_copy(raw16[:, 1024:TH], psC[:, 0:TH - 1024])
            Vc(vr)
            wait("t", "c_v")
            def vc(E, hc=hc):
                v = E
                v.tensor_scalar(convt[:, :], raw16[:, 0:TC], cw_col[:, hc * 4:hc * 4 + 1],
                                None, ALU.mult)
                v.drain()
                v.scalar_tensor_tensor(convt2[:, :], raw16[:, 1:1 + TC],
                                       cw_col[:, hc * 4 + 1:hc * 4 + 2], convt[:, :],
                                       ALU.mult, ALU.add)
                v.drain()
                v.scalar_tensor_tensor(convt[:, :], raw16[:, 2:2 + TC],
                                       cw_col[:, hc * 4 + 2:hc * 4 + 3], convt2[:, :],
                                       ALU.mult, ALU.add)
                v.drain()
                return v.scalar_tensor_tensor(convt2[:, :], raw16[:, 3:3 + TC],
                                              cw_col[:, hc * 4 + 3:hc * 4 + 4], convt[:, :],
                                              ALU.mult, ALU.add)
            Vc(vc)
            wait("s", "c_v")
            Sc(lambda E, hc=hc: E.activation(ssm16[:, :], convt2[:, :], AF.Silu,
                                             bias=cb_col[:, hc:hc + 1]))
            wait("t", "c_s")
            for half in range(2):
                def tx(E, half=half):
                    last = None
                    for k in range(4):
                        tc = half * 4 + k
                        last = E.transpose(pt16[0:PD, k * PD:(k + 1) * PD],
                                           ssm16[:, tc * PD:(tc + 1) * PD], ident16[:, :])
                    return last
                Tc(tx)
                wait("v", "c_t")
                def vx(E, hc=hc, half=half):
                    last = None
                    for k in range(4):
                        tc = half * 4 + k
                        last = E.tensor_copy(
                            x_tm[:, tc * DI + hc * PD:tc * DI + (hc + 1) * PD],
                            pt16[0:PD, k * PD:(k + 1) * PD])
                    return last
                Vc(vx)
                wait("t", "c_v")
        else:
            gc = c
            gslot2 = gc % 2
            wait("s", "c_t")
            if gc >= 2:
                wait("s", "sSP", 16 * (gc - 1))
            def sg_(E, gslot2=gslot2):
                E.activation(sgst[:, gslot2 * TC:gslot2 * TC + 512], psA[:, 0:512], AF.Silu)
                E.drain()
                return E.activation(sgst[:, gslot2 * TC + 512:(gslot2 + 1) * TC],
                                    psB[:, 0:512], AF.Silu)
            Sc(sg_)
            wait("t", "c_s")
            wait("y", "c_s")
            dma("y", sgd[gc * PD:(gc + 1) * PD, :],
                sgst[:, gslot2 * TC:(gslot2 + 1) * TC], "sSP")

        if idx == 2:
            emit_dt_path = True
            wait("s", "c_v")
            def sdt(E):
                s = E
                s.activation(tmp_s[0:NH, :], dtraw[0:NH, :], AF.Abs, bias=dtb_col[0:NH, 0:1])
                s.drain()
                s.activation(tmp_s[0:NH, :], tmp_s[0:NH, :], AF.Exp, scale=-1.0)
                s.drain()
                s.activation(tmp_s[0:NH, :], tmp_s[0:NH, :], AF.Ln, bias=1.0)
                s.drain()
                return s.activation(tmp_s2[0:NH, :], dtraw[0:NH, :], AF.Relu,
                                    bias=dtb_col[0:NH, 0:1])
            Sc(sdt)
            wait("v", "c_s")
            def vdt(E):
                v = E
                v.tensor_tensor(dt_fm[0:NH, :], tmp_s[0:NH, :], tmp_s2[0:NH, :], ALU.add)
                v.drain()
                v.tensor_scalar(tmp_s[0:NH, :], dt_fm[0:NH, :], acol[0:NH, 0:1], None, ALU.mult)
                v.drain()
                for ci in range(NCH):
                    sl = slice(ci * CH, (ci + 1) * CH)
                    v.tensor_tensor_scan(cum_fm[0:NH, sl], tmp_s[0:NH, sl], tmp_s[0:NH, sl],
                                         0.0, ALU.add, ALU.bypass)
                v.drain()
                for ci in range(NCH):
                    v.tensor_copy(clog[0:NH, ci:ci + 1],
                                  cum_fm[0:NH, (ci + 1) * CH - 1:(ci + 1) * CH])
                v.drain()
                v.tensor_reduce(ld_seg[0:NH, 0:1], clog[0:NH, :], AX.X, ALU.add)
                v.drain()
                v.tensor_tensor_scan(tmp_s[0:NH, 0:NCH], clog[0:NH, :], clog[0:NH, :],
                                     0.0, ALU.add, ALU.bypass)
                v.drain()
                return v.scalar_tensor_tensor(tmp_s2[0:NH, 0:NCH], tmp_s[0:NH, 0:NCH],
                                              -1.0, ld_seg[0:NH, 0:1].broadcast_to([NH, NCH]),
                                              ALU.mult, ALU.add)
            Vc(vdt)
            wait("s", "c_v")
            def sdt2(E):
                s = E
                s.activation(tmp_s[0:NH, 0:NCH], tmp_s2[0:NH, 0:NCH], AF.Exp)
                s.drain()
                s.activation(tmp_s2[0:NH, 0:NCH], clog[0:NH, :], AF.Exp)
                s.drain()
                s.activation(df_fm[0:NH, :], cum_fm[0:NH, :], AF.Exp)
                s.drain()
                last = None
                for ci in range(NCH):
                    sl = slice(ci * CH, (ci + 1) * CH)
                    last = s.activation(dtraw[0:NH, sl], cum_fm[0:NH, sl], AF.Exp,
                                        scale=-1.0, bias=clog[0:NH, ci:ci + 1])
                return last
            Sc(sdt2)
            for ci in range(NCH):
                wait("t", "c_s")
                wait("t", "c_v")
                def tdt(E, ci=ci):
                    sl = slice(ci * CH, (ci + 1) * CH)
                    E.transpose(pt32[0:CH, 0:NH], dt_fm[0:NH, sl], identf[0:NH, 0:NH])
                    E.transpose(pt32[0:CH, NH:2 * NH], dtraw[0:NH, sl],
                                identf[0:NH, 0:NH])
                    return E.transpose(pt32[0:CH, 2 * NH:3 * NH], cum_fm[0:NH, sl],
                                       identf[0:NH, 0:NH])
                Tc(tdt)
                wait("v", "c_t")
                def vdt2(E, ci=ci):
                    v = E
                    v.tensor_copy(dt_tm[:, ci * NH:(ci + 1) * NH], pt32[0:CH, 0:NH])
                    v.tensor_copy(dte_tm[:, ci * NH:(ci + 1) * NH], pt32[0:CH, NH:2 * NH])
                    return v.tensor_copy(cum_tm[:, ci * NH:(ci + 1) * NH],
                                         pt32[0:CH, 2 * NH:3 * NH])
                Vc(vdt2)
            wait("y", "c_s")
            dma("y", srow[0:1, 0:NH * NCH], tmp_s[0:NH, 0:NCH], "sLD")
            dma("y", srow[0:1, NH * NCH:2 * NH * NCH], tmp_s2[0:NH, 0:NCH], "sLD")
            wait("t", "sLD")
            wait("t", "c_v")
            def twc2(E):
                E.matmul(pt32[:, 0:256], ones_colf[:, :], srow[0:1, 0:256],
                         start=True, stop=False)
                return E.matmul(pt32[:, 256:512], ones_colf[:, :], srow[0:1, 256:512],
                                start=False, stop=True)
            Tc(twc2)
            wait("v", "c_t")
            def vwc2(E):
                v = E
                v.tensor_copy(wcd_tm[:, :], pt32[:, 0:256])
                v.tensor_copy(dc_cols[:, :], pt32[:, 256:512])
                v.drain()
                return v.tensor_tensor(
                    wcd_tm[:, :].rearrange("p (h c) -> p h c", c=NCH),
                    wcd_tm[:, :].rearrange("p (h c) -> p h c", c=NCH),
                    dte_tm[:, :].rearrange("p (c h) -> p c h", h=NH).transpose([0, 2, 1]),
                    ALU.mult)
            Vc(vwc2)
            wait("t", "c_v")
            def vbc(E):
                E.tensor_copy(b16[:, :], b_fm[:, :])
                return E.tensor_copy(c16[:, :], c_fm[:, :])
            Vc(vbc)
            wait("t", "c_v")
            for half in range(2):
                def tb(E, half=half):
                    last = None
                    for k in range(4):
                        ci = half * 4 + k
                        last = E.transpose(pt16[0:PD, k * PD:(k + 1) * PD],
                                           b16[:, ci * CH:(ci + 1) * CH], ident16[:, :])
                    return last
                Tc(tb)
                wait("v", "c_t")
                def vb(E, half=half):
                    last = None
                    for k in range(4):
                        ci = half * 4 + k
                        last = E.tensor_copy(btm16[:, ci * PD:(ci + 1) * PD],
                                             pt16[0:PD, k * PD:(k + 1) * PD])
                    return last
                Vc(vb)
                wait("t", "c_v")

    if DEBUG:
        for e in ("v", "s", "t"):
            wait("y", "c_" + e)
    dbg_dump("x_tm", x_tm[:, :], [PD, NCH * DI], BF16)
    dbg_dump("b_fm", b_fm[:, :], [PD, TC], F32)
    dbg_dump("c_fm", c_fm[:, :], [PD, TC], F32)
    dbg_dump("dt_fm", dt_fm[0:NH, :], [NH, TC], F32)
    dbg_dump("cum_fm", cum_fm[0:NH, :], [NH, TC], F32)
    dbg_dump("dte_fm", dtraw[0:NH, :], [NH, TC], F32)
    dbg_dump("df_fm", df_fm[0:NH, :], [NH, TC], F32)
    dbg_dump("wcd", wcd_tm[:, :], [PD, NCH * NH], F32)
    dbg_dump("dcc", dc_cols[:, :], [PD, NCH * NH], F32)
    dbg_dump("dt_tm", dt_tm[:, :], [PD, NCH * NH], F32)
    dbg_dump("cum_tm", cum_tm[:, :], [PD, NCH * NH], F32)

    free(sgst)
    free(convt2)
    free(convt)
    free(ssm16)
    free(raw16)
    free(psC)
    free(psB)
    free(psA)
    free(w1buf)
    free(xtile)
    free(xnr16)
    free(h16)
    free(pt16)
    free(pt32)
    barrier()

    x2p16 = alloc("x2p16", [PD, DI], BF16)
    sseg = palloc("sseg", [PD, DI], F32)
    for ci in range(NCH):
        wait("v", "c_t")
        def vx3(E, ci=ci):
            v = E
            dslc = dt_tm[:, ci * NH:(ci + 1) * NH]
            wslc = wcd_tm[:, :].rearrange("p (h c) -> p h c", c=NCH)[:, :, ci]
            v.tensor_tensor(x2p16[:, :].rearrange("p (h q) -> p h q", q=PD),
                            x_tm[:, ci * DI:(ci + 1) * DI].rearrange("p (h q) -> p h q", q=PD),
                            dslc.unsqueeze(2).broadcast_to([PD, NH, PD]), ALU.mult)
            v.drain()
            return v.tensor_tensor(x2p16[:, :].rearrange("p (h q) -> p h q", q=PD),
                                   x2p16[:, :].rearrange("p (h q) -> p h q", q=PD),
                                   wslc.unsqueeze(2).broadcast_to([PD, NH, PD]), ALU.mult)
        Vc(vx3)
        wait("t", "c_v")
        def tz(E, ci=ci):
            last = None
            for h in range(NH):
                last = E.matmul(sseg[:, h * PD:(h + 1) * PD],
                                btm16[:, ci * PD:(ci + 1) * PD],
                                x2p16[:, h * PD:(h + 1) * PD],
                                start=(ci == 0 and h % 4 == 0),
                                stop=(ci == NCH - 1 and h % 4 == 3))
            return last
        Tc(tz)
    wait("v", "c_t")
    ssg_sb = alloc("ssg_sb", [PD, DI], F32)
    Vc(lambda E: E.tensor_copy(ssg_sb[:, :], sseg[:, :]))
    free(sseg)
    def vldz(E):
        E.memset(wam[:, 3:4], 0.0)
        E.drain()
        return E.tensor_copy(wam[0:NH, 3:4], ld_seg[0:NH, 0:1])
    Vc(vldz)
    wait("y", "c_v")
    dma("y", cc_st_in[:, 0:DI], ssg_sb[:, :], "sST")
    dma("y", cc_st_in[:, DI:DI + 1], wam[:, 3:4], "sST")
    wait("g", "sST")
    Gc(lambda E: E.collective_compute("AllGather", ALU.bypass, RG,
                                      [cc_st_in[:, :]], [cc_st_out[:, :, :]]))
    dbg_dump("sseg", ssg_sb[:, :], [PD, DI], F32)
    free(ssg_sb)
    free(x2p16)
    barrier()

    p32b = palloc("pt32b", [PD, 512], F32)
    p16b = palloc("pt16b", [PD, 1024], BF16)
    alloc_quant()
    wait("t", "sLD")
    def ttsw(E):
        E.matmul(p32b[:, 0:1], ones_colf[:, :], wam_r[0:1, 0:1], start=True, stop=False)
        return E.matmul(p32b[:, 1:2], ones_colf[:, :], tsx_col[0:1, 0:1],
                        start=False, stop=True)
    Tc(ttsw)
    wait("v", "c_t")
    def vtsw(E):
        E.tensor_scalar(tsw_col[:, 0:1], p32b[:, 0:1], 1.0 / (448.0 * 6.0), None, ALU.mult)
        E.drain()
        return E.tensor_copy(tsx_col[:, 0:1], p32b[:, 1:2])
    Vc(vtsw)
    wait("t", "c_v")

    def emit_quant(src_ap, ts_ap, nr=PD):
        qs1, qs2, qs3, qsi = QB["qs1"], QB["qs2"], QB["qs3"], QB["qsi"]
        qb1, qb2, qb3, qbi = QB["qb1"], QB["qb2"], QB["qb3"], QB["qbi"]
        qdq16 = QB["qdq16"]
        def vq1(E):
            v = E
            v.tensor_reduce(qb1[0:nr, :], exp16(src_ap), AX.X, ALU.max,
                            apply_absolute_value=True)
            v.reciprocal(sacc[0:nr, 0:1], ts_ap[0:nr, 0:1])
            v.drain()
            v.tensor_scalar(qb1[0:nr, :], qb1[0:nr, :], sacc[0:nr, 0:1], 1.0 / 6.0,
                            ALU.mult, ALU.mult)
            v.drain()
            return v.tensor_scalar(qb1[0:nr, :], qb1[0:nr, :], 448.0, 2.0 ** -20,
                                   ALU.min, ALU.max)
        Vc(vq1)
        wait("s", "c_v")
        Sc(lambda E: E.activation(qb2[0:nr, :], qb1[0:nr, :], AF.Ln))
        wait("v", "c_s")
        def vq2(E):
            v = E
            v.tensor_scalar(qb2[0:nr, :], qb2[0:nr, :], 1.0 / LN2, 16.0, ALU.mult, ALU.add)
            v.drain()
            v.tensor_scalar(qbi[0:nr, :], qb2[0:nr, :], 0.0, None, ALU.add)
            v.drain()
            v.scalar_tensor_tensor(qb3[0:nr, :], qb2[0:nr, :], 0.0, qbi[0:nr, :],
                                   ALU.add, ALU.is_lt)
            v.drain()
            v.tensor_tensor(qb2[0:nr, :], qbi[0:nr, :], qb3[0:nr, :], ALU.subtract)
            v.drain()
            return v.tensor_scalar(qb2[0:nr, :], qb2[0:nr, :], 10.0, None, ALU.max)
        Vc(vq2)
        wait("s", "c_v")
        def sq2(E):
            E.activation(qb3[0:nr, :], qb2[0:nr, :], AF.Exp, scale=-LN2, bias=cconst[0:nr, 0:1])
            E.drain()
            return E.activation(qb2[0:nr, :], qb2[0:nr, :], AF.Exp, scale=LN2,
                                bias=cconst[0:nr, 1:2])
        Sc(sq2)
        wait("v", "c_s")
        def vq3(E):
            v = E
            v.tensor_tensor(qb1[0:nr, :], qb1[0:nr, :], qb3[0:nr, :], ALU.mult)
            v.drain()
            rup(v, qb3[0:nr, :], qb1[0:nr, :], qbi[0:nr, :], qb1[0:nr, :])
            v.drain()
            v.tensor_tensor(qb3[0:nr, :], qb3[0:nr, :], qb2[0:nr, :], ALU.mult)
            v.drain()
            v.tensor_scalar(qb3[0:nr, :], qb3[0:nr, :], 512.0, 0.25, ALU.mult, ALU.add)
            v.drain()
            rup(v, qb2[0:nr, :], qb3[0:nr, :], qbi[0:nr, :], qb1[0:nr, :])
            v.drain()
            v.tensor_scalar(qb1[0:nr, :], qb2[0:nr, :], 1.0 / 512.0, None, ALU.mult)
            v.drain()
            v.tensor_scalar(qb2[0:nr, :], qb1[0:nr, :], 0.0, None, ALU.is_equal)
            v.drain()
            v.tensor_tensor(qb2[0:nr, :], qb2[0:nr, :], qb1[0:nr, :], ALU.add)
            v.drain()
            v.tensor_scalar(qb2[0:nr, :], qb2[0:nr, :], ts_ap[0:nr, 0:1], None, ALU.mult)
            v.drain()
            v.reciprocal(qb2[0:nr, :], qb2[0:nr, :])
            v.drain()
            return v.tensor_tensor(exp16(qs1[0:nr, :]), exp16(src_ap), bc16(qb2[0:nr, :]),
                                   ALU.mult)
        Vc(vq3)
        wait("s", "c_v")
        def sq3(E):
            E.activation(qs2[0:nr, :], qs1[0:nr, :], AF.Abs)
            E.drain()
            return E.activation(qs3[0:nr, :], qs1[0:nr, :], AF.Sign)
        Sc(sq3)
        wait("v", "c_s")
        def vq4(E):
            v = E
            v.tensor_scalar(qs1[0:nr, :], qs2[0:nr, :], 2.0, None, ALU.mult)
            v.drain()
            v.scalar_tensor_tensor(qs1[0:nr, :], qs2[0:nr, :], 2.0, qs1[0:nr, :],
                                   ALU.add, ALU.min)
            v.drain()
            v.tensor_scalar(qs2[0:nr, :], qs2[0:nr, :], 0.5, 4.0, ALU.mult, ALU.add)
            v.drain()
            v.tensor_tensor(qs1[0:nr, :], qs1[0:nr, :], qs2[0:nr, :], ALU.min)
            v.drain()
            v.tensor_scalar(qs1[0:nr, :], qs1[0:nr, :], 7.0, None, ALU.min)
            v.drain()
            rup(v, qs2[0:nr, :], qs1[0:nr, :], qsi[0:nr, :], qs1[0:nr, :])
            v.drain()
            v.tensor_scalar(qs1[0:nr, :], qs2[0:nr, :], 0.5, None, ALU.mult)
            v.drain()
            v.scalar_tensor_tensor(qs1[0:nr, :], qs2[0:nr, :], -2.0, qs1[0:nr, :],
                                   ALU.add, ALU.max)
            v.drain()
            v.tensor_scalar(qs2[0:nr, :], qs2[0:nr, :], 2.0, -8.0, ALU.mult, ALU.add)
            v.drain()
            v.tensor_tensor(qs1[0:nr, :], qs1[0:nr, :], qs2[0:nr, :], ALU.max)
            v.drain()
            v.tensor_tensor(qs1[0:nr, :], qs1[0:nr, :], qs3[0:nr, :], ALU.mult)
            v.drain()
            return v.tensor_tensor(exp16(qdq16[0:nr, :]), exp16(qs1[0:nr, :]),
                                   bc16(qb1[0:nr, :]), ALU.mult)
        Vc(vq4)

    wqt_sb = alloc("wqt_sb", [PD, KC * 256], BF16)
    for wi in range(2):
        wait("y", "c_v")
        dma("y", wfbuf[:, :], wffs[wi * PD:(wi + 1) * PD, :], "sLD")
        wait("v", "sLD")
        emit_quant(wfbuf[:, :], tsw_col)
        wait("t", "c_v")
        for half in range(2):
            def twq(E, half=half, q16=QB["qdq16"]):
                last = None
                for k in range(8):
                    kc = half * 8 + k
                    last = E.transpose(p16b[0:PD, k * PD:(k + 1) * PD],
                                       q16[:, kc * PD:(kc + 1) * PD], ident16[:, :])
                return last
            Tc(twq)
            wait("v", "c_t")
            def vwq(E, wi=wi, half=half):
                last = None
                for k in range(8):
                    kc = half * 8 + k
                    last = E.tensor_copy(
                        wqt_sb[:, kc * 256 + wi * PD:kc * 256 + (wi + 1) * PD],
                        p16b[0:PD, k * PD:(k + 1) * PD])
                return last
            Vc(vwq)
            wait("t", "c_v")
    wait("y", "c_v")
    dma("y", cc_wq_in[:, :], wqt_sb[:, :], "sST")
    wait("g", "sST")
    Gc(lambda E: E.collective_compute("AllGather", ALU.bypass, RG,
                                      [cc_wq_in[:, :]], [cc_wq_out[:, :, :]]))
    free(wqt_sb)
    free_quant()
    barrier()

    wait("y", "c_g")
    for u in range(NC):
        dma("y", ldg[0:NH, u:u + 1], cc_st_out[u, 0:NH, DI:DI + 1], "sLD")
    dma("y", pmk[0:1, :], pmask[:, :], "sLD")
    wait("t", "sLD")
    def tpm(E):
        return E.matmul(p32b[0:NH, 0:NC], ones_colf[0:1, 0:NH], pmk[0:1, :],
                        start=True, stop=True)
    Tc(tpm)
    wait("v", "c_t")
    Vc(lambda E: E.tensor_copy(pmk[0:NH, :], p32b[0:NH, 0:NC]))
    wait("v", "sLD")
    def vwm2(E):
        v = E
        v.tensor_tensor(ldg[0:NH, 0:NC], ldg[0:NH, 0:NC], pmk[0:NH, :], ALU.mult)
        v.drain()
        v.tensor_tensor_scan(tmp_s[0:NH, 0:NC], ldg[0:NH, 0:NC], ldg[0:NH, 0:NC],
                             0.0, ALU.add, ALU.bypass)
        v.drain()
        v.tensor_reduce(ldg[0:NH, NC:NC + 1], ldg[0:NH, 0:NC], AX.X, ALU.add)
        v.drain()
        return v.scalar_tensor_tensor(tmp_s[0:NH, 0:NC], tmp_s[0:NH, 0:NC], -1.0,
                                      ldg[0:NH, NC:NC + 1].broadcast_to([NH, NC]),
                                      ALU.mult, ALU.add)
    Vc(vwm2)
    wait("s", "c_v")
    Sc(lambda E: E.activation(tmp_s[0:NH, 0:NC], tmp_s[0:NH, 0:NC], AF.Exp))
    wait("v", "c_s")
    Vc(lambda E: E.tensor_tensor(tmp_s[0:NH, 0:NC], tmp_s[0:NH, 0:NC], pmk[0:NH, :],
                                 ALU.mult))
    wait("y", "c_v")
    dma("y", srow[0:1, 0:NH * NC], tmp_s[0:NH, 0:NC], "sLD")
    wait("t", "sLD")
    wait("t", "c_v")
    def twm2(E):
        return E.matmul(p32b[:, 0:256], ones_colf[:, :], srow[0:1, 0:256],
                        start=True, stop=True)
    Tc(twm2)
    wait("v", "c_t")
    Vc(lambda E: E.tensor_copy(wm_cols[:, 0:256], p32b[:, 0:256]))
    wait("t", "c_v")

    s_run = alloc("s_run", [PD, DI], F32)
    s16 = alloc("s16", [PD, DI], BF16)
    sgu = alloc("sgu", [PD, 2 * DI], F32)
    sgu_sems = ["sLD", "sDT", "sXT", "sFF"]
    vdone = []
    for u in range(NC):
        slot = u % 2
        if u >= 2:
            wait("y", "c_v", vdone[u - 2])
        for j in range(8):
            dma("y", sgu[:, slot * DI + j * 512:slot * DI + (j + 1) * 512],
                cc_st_out[u, :, j * 512:(j + 1) * 512], sgu_sems[j % 4])
        for s_ in sgu_sems:
            wait("v", s_)
        for h in range(NH):
            if u == 0:
                def vch0(E, h=h, slot=slot):
                    return E.tensor_scalar(s_run[:, h * PD:(h + 1) * PD],
                                           sgu[:, slot * DI + h * PD:slot * DI + (h + 1) * PD],
                                           wm_cols[:, h * NC:h * NC + 1], None, ALU.mult)
                Vc(vch0)
            else:
                def vchu(E, h=h, u=u, slot=slot):
                    return E.scalar_tensor_tensor(s_run[:, h * PD:(h + 1) * PD],
                                                  sgu[:, slot * DI + h * PD:slot * DI + (h + 1) * PD],
                                                  wm_cols[:, h * NC + u:h * NC + u + 1],
                                                  s_run[:, h * PD:(h + 1) * PD],
                                                  ALU.mult, ALU.add)
                Vc(vchu)
        vdone.append(cnt["c_v"])
    Vc(lambda E: E.tensor_copy(s16[:, :], s_run[:, :]))
    if DEBUG:
        wait("y", "c_v")
    dbg_dump("sinit", s_run[:, :], [PD, DI], F32)
    dbg_dump("wm", wm_cols[:, :], [PD, NC * NH], F32)
    free(sgu)
    free(p16b)
    free(p32b)
    barrier()

    crow = alloc("crow", [1, 1024], F32)
    dfrow = alloc("dfrow", [1, 1024], F32)
    sgbuf = alloc("sgbuf", [PD, 2 * NH * PD], BF16)
    x3c = alloc("x3c", [PD, 1024], BF16)
    x2c = alloc("x2c", [PD, 1024], BF16)
    xdc = alloc("xdc", [PD, 1024], BF16)
    wpre = alloc("wpre", [PD, NH * PD], BF16)
    cbm = alloc("cbm", [PD, PD], BF16)
    cdd16 = alloc("cdd16", [PD, 8 * PD], BF16)
    y3st = alloc("y3st", [PD, NH * PD], BF16)
    psY = palloc("psY", [PD, 1024], F32)
    psZ = palloc("psZ", [PD, 1024], F32)
    psR = palloc("psR", [PD, 1024], F32)
    psCB = palloc("psCB", [PD, PD], F32)

    def sg_load(ci, slot):
        dst = sgbuf[:, slot * NH * PD:(slot + 1) * NH * PD] \
            .rearrange("p (h t) -> p h t", t=PD)
        src = sgd[:, ci * PD:(ci + 1) * PD].rearrange("(h p) t -> p h t", p=PD)
        dma("y", dst, src, "sSG")

    sg_load(0, 0)

    vmarks = []
    for ci in range(NCH):
        slot = ci % 2
        if ci + 1 < NCH:
            if ci >= 1:
                wait("y", "c_v", vmarks[ci - 1])
            sg_load(ci + 1, (ci + 1) % 2)
        wait("v", "c_t")
        wait("v", "sY3")
        def tcb(E, ci=ci):
            return E.matmul(psCB[:, :], b16[:, ci * CH:(ci + 1) * CH],
                            c16[:, ci * CH:(ci + 1) * CH], start=True, stop=True)
        Tc(tcb)
        wait("v", "c_t")
        Vc(lambda E: E.tensor_tensor(cbm[:, :], psCB[:, :], tri01[:, :], ALU.mult))
        for hg in range(4):
            wait("y", "c_t")
            dma("y", crow[0:1, :], cum_fm[hg * 8:(hg + 1) * 8, ci * CH:(ci + 1) * CH], "sDT")
            wait("t", "c_v")
            wait("t", "c_s")
            wait("t", "sDT")
            def tcr(E, ci=ci, hg=hg):
                last = None
                for j in (0, 4):
                    last = E.matmul(psR[:, j * PD:(j + 4) * PD], ones_colf[:, :],
                                    crow[0:1, j * CH:(j + 4) * CH],
                                    start=True, stop=True)
                return last
            Tc(tcr)
            wait("s", "c_t")
            def swp(E, ci=ci, hg=hg):
                s = E
                for j in range(8):
                    h = hg * 8 + j
                    s.activation(wpre[:, h * PD:(h + 1) * PD], psR[:, j * PD:(j + 1) * PD],
                                 AF.Relu, scale=-1.0,
                                 bias=cum_tm[:, ci * NH + h:ci * NH + h + 1])
                s.drain()
                last = None
                for j in range(8):
                    h = hg * 8 + j
                    last = s.activation(wpre[:, h * PD:(h + 1) * PD],
                                        wpre[:, h * PD:(h + 1) * PD], AF.Exp, scale=-1.0)
                return last
            Sc(swp)
        wait("v", "c_s")
        def vwp(E):
            return E.tensor_tensor(wpre[:, :].rearrange("p (h q) -> p h q", q=PD),
                                   wpre[:, :].rearrange("p (h q) -> p h q", q=PD),
                                   cbm[:, :].unsqueeze(1).broadcast_to([PD, NH, PD]),
                                   ALU.mult)
        Vc(vwp)
        wait("v", "sSG", 16 * (ci + 1))
        for hg in range(4):
            wait("y", "c_t")
            dma("y", dfrow[0:1, :], df_fm[hg * 8:(hg + 1) * 8, ci * CH:(ci + 1) * CH], "sDT")
            wait("t", "c_v")
            wait("t", "sDT")
            def tdf(E, ci=ci, hg=hg):
                last = None
                for j in (0, 4):
                    last = E.matmul(psR[:, j * PD:(j + 4) * PD], ones_colf[:, :],
                                    dfrow[0:1, j * CH:(j + 4) * CH],
                                    start=True, stop=True)
                return last
            Tc(tdf)
            wait("v", "c_t")
            def vcd(E, ci=ci):
                return E.tensor_tensor(
                    cdd16[:, :].rearrange("p (j q) -> p j q", q=PD),
                    c16[:, ci * CH:(ci + 1) * CH].unsqueeze(1).broadcast_to([PD, 8, PD]),
                    psR[:, 0:1024].rearrange("p (j q) -> p j q", q=PD), ALU.mult)
            Vc(vcd)
            def vxh(E, ci=ci, hg=hg):
                v = E
                xc = x_tm[:, ci * DI + hg * 1024:ci * DI + (hg + 1) * 1024] \
                    .rearrange("p (h q) -> p h q", q=PD)
                dcol = dt_tm[:, ci * NH + hg * 8:ci * NH + (hg + 1) * 8]
                ecol = dte_tm[:, ci * NH + hg * 8:ci * NH + (hg + 1) * 8]
                v.tensor_tensor(x3c[:, :].rearrange("p (h q) -> p h q", q=PD), xc,
                                dcol.unsqueeze(2).broadcast_to([PD, 8, PD]), ALU.mult)
                v.drain()
                v.tensor_tensor(x2c[:, :].rearrange("p (h q) -> p h q", q=PD),
                                x3c[:, :].rearrange("p (h q) -> p h q", q=PD),
                                ecol.unsqueeze(2).broadcast_to([PD, 8, PD]), ALU.mult)
                v.drain()
                return v.tensor_tensor(xdc[:, :].rearrange("p (h q) -> p h q", q=PD), xc,
                                       d16[:, hg * 8:(hg + 1) * 8].unsqueeze(2)
                                       .broadcast_to([PD, 8, PD]), ALU.mult)
            Vc(vxh)
            wait("t", "c_v")
            def ty(E, hg=hg):
                last = None
                for j in range(8):
                    h = hg * 8 + j
                    E.matmul(psY[:, j * PD:(j + 1) * PD], x3c[:, j * PD:(j + 1) * PD],
                             wpre[:, h * PD:(h + 1) * PD], start=(j in (0, 4)), stop=False)
                    E.matmul(psY[:, j * PD:(j + 1) * PD], xdc[:, j * PD:(j + 1) * PD],
                             ident16[:, :], start=False, stop=False)
                    last = E.matmul(psY[:, j * PD:(j + 1) * PD], s16[:, h * PD:(h + 1) * PD],
                                    cdd16[:, j * PD:(j + 1) * PD], start=False,
                                    stop=(j in (3, 7)))
                return last
            Tc(ty)
            def tz2(E, ci=ci, hg=hg):
                last = None
                for j in range(8):
                    last = E.matmul(psZ[:, j * PD:(j + 1) * PD],
                                    btm16[:, ci * PD:(ci + 1) * PD],
                                    x2c[:, j * PD:(j + 1) * PD],
                                    start=(j in (0, 4)), stop=(j in (3, 7)))
                return last
            Tc(tz2)
            wait("v", "c_t")
            def vy(E, ci=ci, hg=hg, slot=slot):
                v = E
                sgv = sgbuf[:, slot * NH * PD + hg * 1024:slot * NH * PD + (hg + 1) * 1024]
                v.tensor_tensor(y3st[:, hg * 1024:(hg + 1) * 1024], psY[:, 0:1024], sgv,
                                ALU.mult)
                v.drain()
                for j in range(8):
                    h = hg * 8 + j
                    v.scalar_tensor_tensor(s_run[:, h * PD:(h + 1) * PD],
                                           s_run[:, h * PD:(h + 1) * PD],
                                           dc_cols[:, h * NCH + ci:h * NCH + ci + 1],
                                           psZ[:, j * PD:(j + 1) * PD], ALU.mult, ALU.add)
                v.drain()
                return v.tensor_copy(s16[:, hg * 1024:(hg + 1) * 1024],
                                     s_run[:, hg * 1024:(hg + 1) * 1024])
            Vc(vy)
        vmarks.append(cnt["c_v"])
        wait("y", "c_v", cnt["c_v"])
        def y3w(E, ci=ci):
            dst = y3d[:, ci * PD:(ci + 1) * PD].rearrange("(h p) t -> p h t", p=PD)
            return E.dma_start(dst, y3st[:, :].rearrange("p (h t) -> p h t", t=PD))
        h_ = sems["sY3"]
        cnt["sY3"] += 16
        emit("y", lambda E, h_=h_, y3w=y3w, ci=ci: y3w(E, ci).then_inc(h_, 16))

    free(psCB)
    free(psR)
    free(psZ)
    free(psY)
    free(y3st)
    free(cdd16)
    free(cbm)
    free(wpre)
    free(xdc)
    free(x2c)
    free(x3c)
    free(sgbuf)
    free(dfrow)
    free(crow)
    free(s16)
    free(s_run)
    free(x_tm)
    barrier()

    free(tmp_s2)
    free(tmp_s)
    free(dtraw)
    free(c_fm)
    free(b_fm)
    free(btm16)
    free(c16)
    free(b16)
    free(df_fm)
    free(cum_fm)
    free(dt_fm)
    barrier()
    xt2 = alloc("xtile2", [PD, D], F32)
    s_full = alloc("s_full", [PD, NCH * D], F32)
    y3sb = alloc("y3sb", [PD, NH * TC], BF16)
    w2b = alloc("w2b", [PD, 2 * 512], BF16)
    psO = palloc("psO", [PD, 4096], F32)

    wait("y", "sY3")
    dma("y", y3sb[:, :].rearrange("p (h t) -> p h t", t=TC),
        y3d[:, :].rearrange("(h p) t -> p h t", p=PD), "sLD")
    w2_marks = []

    def w2_load(di, dq, slot):
        dst = w2b[:, slot * 512:(slot + 1) * 512]
        src = w2[di * PD:(di + 1) * PD, dq * 512:(dq + 1) * 512]
        dma("g", dst, src, "sW2")
        w2_marks.append(cnt["sW2"])

    pairs = [(dq, di) for dq in range(4) for di in range(NH)]
    w2_load(pairs[0][1], pairs[0][0], 0)
    w2_load(pairs[1][1], pairs[1][0], 1)
    wait("t", "sLD")
    for j in range(len(pairs)):
        dq, di = pairs[j]
        if True:
            wait("t", "sW2", w2_marks[j])
            slot = j % 2
            def top(E, dq=dq, di=di, slot=slot):
                last = None
                for tc in range(NCH):
                    last = E.matmul(psO[:, tc * 512:(tc + 1) * 512],
                                    y3sb[:, di * TC + tc * PD:di * TC + (tc + 1) * PD],
                                    w2b[:, slot * 512:(slot + 1) * 512],
                                    start=(di == 0), stop=False)
                return last
            Tc(top)
            if j + 2 < len(pairs):
                wait("g", "c_t")
                nd, ni = pairs[j + 2]
                w2_load(ni, nd, j % 2)
        if di == NH - 1:
            emit_tail = True
        else:
            continue
        def tmr(E, dq=dq):
            last = None
            for tc in range(NCH):
                last = E.matmul(psO[:, tc * 512:(tc + 1) * 512],
                                statrow[0:1, tc * PD:(tc + 1) * PD],
                                ones_row[0:1, 0:512], start=False, stop=True)
            return last
        Tc(tmr)
        for tc in range(NCH):
            r0 = HALO + tc * PD
            wait("y", "c_v")
            dma("y", xt2[:, 0:512], xs[r0:r0 + PD, dq * 512:(dq + 1) * 512], "sXT")
            wait("v", "sXT")
            wait("v", "c_t")
            st = stats[:, (tc + 1) * 8:(tc + 2) * 8]
            def vs5(E, tc=tc, dq=dq, st=st):
                return E.scalar_tensor_tensor(
                    s_full[:, tc * D + dq * 512:tc * D + (dq + 1) * 512],
                    xt2[:, 0:512], st[:, 1:2], psO[:, tc * 512:(tc + 1) * 512],
                    ALU.mult, ALU.add)
            Vc(vs5)
        wait("t", "c_v")
    free(psO)
    free(w2b)
    free(y3sb)
    barrier()

    if DEBUG:
        for e in ("v", "s", "t"):
            wait("y", "c_" + e)
    dbg_dump("sfull", s_full[:, :], [PD, NCH * D], F32)
    dbg_dump("y3", y3d[:, :], [DI, TC], BF16)
    dbg_dump("sg", sgd[:, :], [DI, TC], BF16)

    ffb_row = alloc("ffb_row", [1, D], F32)
    ffb2 = alloc("ffb2", [1, D], F32)
    os_col = alloc("os_col", [PD, 2], F32)
    dma("y", ffb_row[0:1, :], ffb.ap().unsqueeze(0), "sLD")
    wait("v", "sLD")
    def vos(E):
        v = E
        v.tensor_tensor(os_col[:, 0:1], tsx_col[:, 0:1], tsw_col[:, 0:1], ALU.mult)
        v.drain()
        v.reciprocal(os_col[:, 1:2], os_col[:, 0:1])
        v.drain()
        return v.tensor_scalar(ffb2[0:1, :], ffb_row[0:1, :], os_col[0:1, 1:2], None,
                               ALU.mult)
    Vc(vos)
    xdq = alloc("xdq", [PD, KC * TC], BF16)
    alloc_quant()
    p16c = palloc("pt16c", [PD, 1024], BF16)
    for tc in range(NCH):
        r0 = HALO + tc * PD
        wait("y", "c_s")
        dma("y", xt2[:, :], xs[r0:r0 + PD, :], "sXT")
        wait("s", "sXT")
        wait("s", "c_v")
        st = stats[:, (tc + 1) * 8:(tc + 2) * 8]
        Sc(lambda E, st=st, q1=QB["qs1"]: E.activation(q1[:, :], xt2[:, :], AF.Identity,
                                                       bias=st[:, 6:7], scale=st[:, 1:2]))
        wait("v", "c_s")
        Vc(lambda E, q1=QB["qs1"], q2=QB["qs2"]: E.tensor_copy(q2[:, :], q1[:, :]))
        emit_quant(QB["qs2"][:, :], tsx_col)
        wait("t", "c_v")
        for half in range(2):
            def txq(E, half=half, q16=QB["qdq16"]):
                last = None
                for k in range(8):
                    kc = half * 8 + k
                    last = E.transpose(p16c[0:PD, k * PD:(k + 1) * PD],
                                       q16[:, kc * PD:(kc + 1) * PD], ident16[:, :])
                return last
            Tc(txq)
            wait("v", "c_t")
            def vxq(E, tc=tc, half=half):
                last = None
                for k in range(8):
                    kc = half * 8 + k
                    last = E.tensor_copy(xdq[:, kc * TC + tc * PD:kc * TC + (tc + 1) * PD],
                                         p16c[0:PD, k * PD:(k + 1) * PD])
                return last
            Vc(vxq)
            wait("t", "c_v")

    wdqb = alloc("wdqb", [PD, KC * 512], BF16)
    psG = palloc("psG", [PD, 512], F32)
    gtile = alloc("gtile", [PD, 512], F32)
    ftile = alloc("ftile", [PD, 512], F32)
    for ng in range(4):
        for r in range(2):
            dst = wdqb[:, :].rearrange("p (kc n) -> p kc n", n=512)[:, :, r * 256:(r + 1) * 256]
            src = cc_wq_out[2 * ng + r, :, :].rearrange("p (kc n) -> p kc n", n=256)
            dma("y", dst, src, "sFF")
        wait("t", "sFF")
        for tc in range(NCH):
            def tff(E, tc=tc, ng=ng):
                for kc in range(KC):
                    E.matmul(psG[:, 0:512], xdq[:, kc * TC + tc * PD:kc * TC + (tc + 1) * PD],
                             wdqb[:, :].rearrange("p (kc n) -> p kc n", n=512)[:, kc, :],
                             start=(kc == 0), stop=False)
                return E.matmul(psG[:, 0:512], ones_colf[0:1, 0:PD],
                                ffb2[0:1, ng * 512:(ng + 1) * 512], start=False, stop=True)
            Tc(tff)
            wait("s", "c_t")
            Sc(lambda E: E.activation(gtile[:, :], psG[:, 0:512], AF.Silu,
                                      scale=os_col[:, 0:1]))
            r0 = HALO + tc * PD
            wait("y", "c_v")
            dma("y", xt2[:, 0:512], xs[r0:r0 + PD, ng * 512:(ng + 1) * 512], "sXT")
            wait("v", "c_s")
            wait("v", "sXT")
            def vf(E, tc=tc, ng=ng):
                v = E
                v.tensor_tensor(ftile[:, :], s_full[:, tc * D + ng * 512:tc * D + (ng + 1) * 512],
                                gtile[:, :], ALU.mult)
                v.drain()
                return v.tensor_tensor(ftile[:, :], ftile[:, :], xt2[:, 0:512], ALU.add)
            Vc(vf)
            wait("t", "c_v")
            wait("y", "c_v")
            dma("y", out[tc * PD:(tc + 1) * PD, ng * 512:(ng + 1) * 512], ftile[:, :], "sOUT")
            wait("v", "sOUT")

    if DEBUG:
        for e in ("v", "s", "t"):
            wait("y", "c_" + e)
    dbg_dump("xdq", xdq[:, :], [PD, KC * TC], BF16)
    dbg_dump("wq", cc_wq_out[:, :, :].rearrange("a p c -> (a p) c"), [NC * PD, KC * 256], BF16)
    wait("y", "sOUT")

    with nc.Block() as block:
        block.tensor(lambda E: [f(E) for f in ops["t"]])
        block.vector(lambda E: [f(E) for f in ops["v"]])
        block.scalar(lambda E: [f(E) for f in ops["s"]])
        block.gpsimd(lambda E: [f(E) for f in ops["g"]])
        block.sync(lambda E: [f(E) for f in ops["y"]])
    es.close()
    return nc


_NC_CACHE = None


def kernel(**inputs):
    global _NC_CACHE
    x = np.asarray(inputs["L0_attn_x"], np.float32).reshape(8192, D)
    ns = np.ascontiguousarray(np.asarray(inputs["L0_attn_norm_scale"], np.float32))
    w1 = np.ascontiguousarray(np.asarray(inputs["L0_attn_in_proj_w"], np.float32))
    convw = np.ascontiguousarray(np.asarray(inputs["L0_attn_conv_weight"], np.float32))
    convb = np.ascontiguousarray(np.asarray(inputs["L0_attn_conv_bias"], np.float32))
    alog = np.ascontiguousarray(np.asarray(inputs["L0_attn_A_log"], np.float32))
    dtb = np.ascontiguousarray(np.asarray(inputs["L0_attn_dt_bias"], np.float32))
    dvec = np.ascontiguousarray(np.asarray(inputs["L0_attn_D"], np.float32))
    w2 = np.ascontiguousarray(np.asarray(inputs["L0_attn_out_proj_w"], np.float32))
    wff = np.ascontiguousarray(np.asarray(inputs["L0_ffn_weight"], np.float32))
    ffb = np.ascontiguousarray(np.asarray(inputs["L0_ffn_bias"], np.float32))
    xts = np.asarray(inputs["L0_ffn_x_tensor_scale"], np.float32).reshape(1, 1)

    if _NC_CACHE is None:
        _NC_CACHE = build_nc()
    nc = _NC_CACHE

    in_maps = []
    for k in range(NC):
        lo = k * TC
        if k == 0:
            xsk = np.concatenate([np.zeros((HALO, D), np.float32), x[0:TC]], axis=0)
        else:
            xsk = x[lo - HALO:lo + TC]
        pm = np.zeros((1, NC), np.float32)
        pm[0, :k] = 1.0
        in_maps.append(dict(
            xs=np.ascontiguousarray(xsk), ns_in=ns, w1=w1, convw=convw, convb=convb,
            alog=alog, dtb=dtb, dvec=dvec, w2=w2,
            wffs=np.ascontiguousarray(wff[k * 256:(k + 1) * 256, :]),
            ffb=ffb, xts=xts, pmask=pm,
        ))
    res = run_bass_kernel_spmd(nc, in_maps, core_ids=list(range(NC)))
    outs = [np.asarray(res.results[k]["out"]) for k in range(NC)]
    return np.concatenate(outs, axis=0).reshape(1, 8192, D).astype(np.float32)


# version 12
# speedup vs baseline: 1.0167x; 1.0107x over previous
import numpy as np
import contextlib
import concourse.bass as bass
import concourse.mybir as mybir
from concourse.bass_utils import run_bass_kernel_spmd
from concourse.replica_groups import maybe_share_collective_output_space

F32 = mybir.dt.float32
BF16 = mybir.dt.bfloat16
I32 = mybir.dt.int32
ALU = mybir.AluOpType
AF = mybir.ActivationFunctionType
AX = mybir.AxisListType

NC = 8
TC = 1024
HALO = 3
TH = TC + HALO
D = 2048
DI = 4096
NH = 32
PD = 128
KC = D // PD
NCH = TC // PD
CH = PD
PROJ = 8480
NB = D // 16
LN2 = float(np.log(2.0))
RG = [list(range(NC))]
ENGINES = ("t", "v", "s", "g", "y")
DEBUG = False


def build_nc():
    nc = bass.Bass("TRN2", target_bir_lowering=False)

    xs = nc.dram_tensor("xs", [TH, D], F32, kind="ExternalInput")
    ns_in = nc.dram_tensor("ns_in", [D], F32, kind="ExternalInput")
    w1 = nc.dram_tensor("w1", [D, PROJ], F32, kind="ExternalInput")
    convw = nc.dram_tensor("convw", [4, DI], F32, kind="ExternalInput")
    convb = nc.dram_tensor("convb", [DI], F32, kind="ExternalInput")
    alog = nc.dram_tensor("alog", [NH], F32, kind="ExternalInput")
    dtb = nc.dram_tensor("dtb", [NH], F32, kind="ExternalInput")
    dvec = nc.dram_tensor("dvec", [DI], F32, kind="ExternalInput")
    w2 = nc.dram_tensor("w2", [DI, D], F32, kind="ExternalInput")
    wffs = nc.dram_tensor("wffs", [2 * PD, D], F32, kind="ExternalInput")
    ffb = nc.dram_tensor("ffb", [D], F32, kind="ExternalInput")
    xts = nc.dram_tensor("xts", [1, 1], F32, kind="ExternalInput")
    pmask = nc.dram_tensor("pmask", [1, NC], F32, kind="ExternalInput")
    out = nc.dram_tensor("out", [TC, D], F32, kind="ExternalOutput")
    dbg_t = {}

    def dbg_out(name, shape, dt):
        if DEBUG and name not in dbg_t:
            dbg_t[name] = nc.dram_tensor("dbg_" + name, shape, dt, kind="ExternalOutput")
        return dbg_t.get(name)

    sgd = nc.dram_tensor("sgd", [DI, TC], BF16, kind="Internal")
    y3d = nc.dram_tensor("y3d", [DI, TC], BF16, kind="Internal")
    cc_st_in = nc.dram_tensor("cc_st_in", [PD, DI + 1], F32, kind="Internal")
    cc_st_out = nc.dram_tensor(
        "cc_st_out", [NC, PD, DI + 1], F32, kind="Internal",
        addr_space=maybe_share_collective_output_space("AllGather", RG))
    cc_wq_in = nc.dram_tensor("cc_wq_in", [PD, KC * 256], BF16, kind="Internal")
    cc_wq_out = nc.dram_tensor(
        "cc_wq_out", [NC, PD, KC * 256], BF16, kind="Internal",
        addr_space=maybe_share_collective_output_space("AllGather", RG))
    cc_am_in = nc.dram_tensor("cc_am_in", [1, 8], F32, kind="Internal")
    cc_am_out = nc.dram_tensor(
        "cc_am_out", [1, 8], F32, kind="Internal",
        addr_space=maybe_share_collective_output_space("AllReduce", RG))

    ops = {e: [] for e in ENGINES}
    sems = {}
    cnt = {}
    es = contextlib.ExitStack()

    def mksem(name):
        sems[name] = es.enter_context(nc.semaphore(name))
        cnt[name] = 0
    for e in ENGINES:
        mksem("c_" + e)
    for s_ in ("sLD", "sST", "sW1", "sW2", "sSP", "sSG", "sY3", "sW1f", "sOUT",
               "sDT", "sXT", "sFF"):
        mksem(s_)

    def emit(e, fn):
        ops[e].append(fn)

    def wait(e, name, v=None):
        h = sems[name]
        tgt = cnt[name] if v is None else v
        emit(e, lambda E, h=h, tgt=tgt: E.wait_ge(h, tgt))

    def _mk(e, fn, drain):
        name = "c_" + e
        h = sems[name]
        cnt[name] += 1
        def run(E, h=h, fn=fn, drain=drain):
            if drain:
                E.drain()
            r = fn(E)
            return r.then_inc(h, 1)
        emit(e, run)

    def Tc(fn):
        _mk("t", fn, False)

    def Vc(fn):
        _mk("v", fn, True)

    def Sc(fn):
        _mk("s", fn, True)

    def Gc(fn):
        _mk("g", fn, True)

    def dbg_dump(name, src_ap, shape, dt):
        if not DEBUG:
            return
        t = dbg_out(name, shape, dt)
        dma("y", t.ap(), src_ap, "sST")

    def dma(e, dst, src, sem):
        h = sems[sem]
        pre = cnt[sem]
        cnt[sem] += 16
        def run(E, h=h, dst=dst, src=src, pre=pre):
            E.wait_ge(h, pre)
            return E.dma_start(dst, src,
                               allow_slow_non_contiguous=True).then_inc(h, 16)
        emit(e, run)

    def tnop_inc(sem):
        h = sems[sem]
        cnt[sem] += 1
        emit("t", lambda E, h=h: E.nop().then_inc(h, 1))

    _guards = {}

    def alloc(name, shape, dt, side=None):
        g = nc.sbuf_tensor(name, shape, dt, side=side) if side else nc.sbuf_tensor(name, shape, dt)
        h = g.__enter__()
        _guards[id(h)] = g
        return h

    def palloc(name, shape, dt):
        g = nc.psum_tensor(name, shape, dt)
        h = g.__enter__()
        _guards[id(h)] = g
        return h

    def free(h):
        _guards.pop(id(h)).__exit__(None, None, None)

    def barrier():
        names = ["c_t", "c_v", "c_s", "c_g", "sLD", "sST", "sW1", "sW2",
                 "sSP", "sSG", "sY3", "sOUT", "sDT", "sXT", "sFF"]
        snap = [(n, cnt[n]) for n in names]
        for e in ENGINES:
            for (n, v) in snap:
                if v > 0:
                    h2 = sems[n]
                    emit(e, lambda E, h2=h2, v=v: E.wait_ge(h2, v))

    ones_col = alloc("ones_col", [1, PD], BF16)
    ones_colf = alloc("ones_colf", [1, PD], F32)
    ident16 = alloc("ident16", [PD, PD], BF16)
    identf = alloc("identf", [PD, PD], F32)
    tri01 = alloc("tri01", [PD, PD], F32)
    ns_col = alloc("ns_col", [PD, KC], F32)
    cw_col = alloc("cw_col", [PD, NH * 4], F32)
    cb_col = alloc("cb_col", [PD, NH], F32)
    d16 = alloc("d16", [PD, NH], BF16)
    dtb_col = alloc("dtb_col", [NH, 1], F32)
    alog_col = alloc("alog_col", [NH, 1], F32)
    acol = alloc("acol", [NH, 1], F32)
    tsx_col = alloc("tsx_col", [PD, 1], F32)
    tsw_col = alloc("tsw_col", [PD, 1], F32)
    cconst = alloc("cconst", [PD, 8], F32)
    pm_row = alloc("pm_row", [1, NC], F32)
    stats = alloc("stats", [PD, 9 * 8], F32)
    statrow = alloc("statrow", [1, NCH * PD], F32)
    ones_row = alloc("ones_row", [1, 512], F32)
    dt_fm = alloc("dt_fm", [NH, TC], F32, side="right")
    cum_fm = alloc("cum_fm", [NH, TC], F32, side="right")
    df_fm = alloc("df_fm", [NH, TC], F32, side="right")
    clog = alloc("clog", [NH, NCH], F32)
    ld_seg = alloc("ld_seg", [NH, 1], F32)
    dt_tm = alloc("dt_tm", [PD, NCH * NH], F32)
    dte_tm = alloc("dte_tm", [PD, NCH * NH], F32)
    wcd_tm = alloc("wcd_tm", [PD, NCH * NH], F32)
    cum_tm = alloc("cum_tm", [PD, NCH * NH], F32)
    dc_cols = alloc("dc_cols", [PD, NCH * NH], F32)
    wm_cols = alloc("wm_cols", [PD, NC * NH], F32)
    srow = alloc("srow", [1, 512], F32)
    b16 = alloc("b16", [PD, TC], BF16, side="right")
    c16 = alloc("c16", [PD, TC], BF16, side="right")
    btm16 = alloc("btm16", [PD, NCH * PD], BF16, side="right")
    b_fm = alloc("b_fm", [PD, TC], F32, side="right")
    c_fm = alloc("c_fm", [PD, TC], F32, side="right")
    dtraw = alloc("dtraw", [NH, TC], F32, side="right")
    wam = alloc("wam", [PD, 4], F32)
    wam_r = alloc("wam_r", [1, PD], F32)
    ldg = alloc("ldg", [NH, NC + 1], F32)
    pmk = alloc("pmk", [NH, NC], F32)
    sacc = alloc("sacc", [PD, 4], F32)

    QB = {}

    _qgen = [0]

    def alloc_quant():
        g = _qgen[0]
        _qgen[0] += 1
        QB["qs1"] = alloc("qs1_%d" % g, [PD, D], F32)
        QB["qs2"] = alloc("qs2_%d" % g, [PD, D], F32)
        QB["qs3"] = alloc("qs3_%d" % g, [PD, D], F32)
        QB["qsi"] = alloc("qsi_%d" % g, [PD, D], I32)
        QB["qb1"] = alloc("qb1_%d" % g, [PD, NB], F32)
        QB["qb2"] = alloc("qb2_%d" % g, [PD, NB], F32)
        QB["qb3"] = alloc("qb3_%d" % g, [PD, NB], F32)
        QB["qbi"] = alloc("qbi_%d" % g, [PD, NB], I32)
        QB["qdq16"] = alloc("qdq16_%d" % g, [PD, D], BF16)

    def free_quant():
        for k in ("qdq16", "qbi", "qb3", "qb2", "qb1", "qsi", "qs3", "qs2", "qs1"):
            free(QB[k])

    pt32 = palloc("pt32", [PD, 512], F32)
    pt16 = palloc("pt16", [PD, 1024], BF16)

    def exp16(a):
        return a.rearrange("p (b k) -> p b k", k=16)

    def bc16(a):
        return a.unsqueeze(2).broadcast_to([PD, NB, 16])

    def rup(v, out_ap, in_ap, it_ap, mk_ap):
        v.tensor_scalar(it_ap, in_ap, 0.0, None, ALU.add)
        v.drain()
        v.scalar_tensor_tensor(mk_ap, in_ap, -0.5, it_ap, ALU.add, ALU.is_ge)
        v.drain()
        return v.tensor_tensor(out_ap, it_ap, mk_ap, ALU.add)

    def g0(E):
        E.memset(ones_col[:, :], 1.0)
        E.memset(ones_colf[:, :], 1.0)
        E.memset(identf[:, :], 1.0)
        E.drain()
        E.affine_select(identf[:, :], identf[:, :], [[1, PD]], ALU.is_equal, 0.0,
                        base=0, channel_multiplier=-1)
        E.drain()
        E.tensor_copy(ident16[:, :], identf[:, :])
        E.memset(tri01[:, :], 1.0)
        E.drain()
        E.affine_select(tri01[:, :], tri01[:, :], [[1, PD]], ALU.is_ge, 0.0,
                        base=0, channel_multiplier=-1)
        E.memset(cconst[:, 0:1], 19.0 * LN2)
        E.memset(cconst[:, 1:2], -19.0 * LN2)
        return E.memset(ones_row[:, :], 1.0)
    Gc(g0)

    for dst, src in [
        (ns_col[:, :], ns_in.ap().rearrange("(kc p) -> p kc", p=PD)),
        (cw_col[:, 0:NH * 4:4], bass.AP(convw, 0 * DI, [[1, PD], [PD, NH]])),
        (cw_col[:, 1:NH * 4:4], bass.AP(convw, 1 * DI, [[1, PD], [PD, NH]])),
        (cw_col[:, 2:NH * 4:4], bass.AP(convw, 2 * DI, [[1, PD], [PD, NH]])),
        (cw_col[:, 3:NH * 4:4], bass.AP(convw, 3 * DI, [[1, PD], [PD, NH]])),
        (cb_col[:, :], convb.ap().rearrange("(hc p) -> p hc", p=PD)),
        (dtb_col[:, :], dtb.ap().unsqueeze(1)),
        (alog_col[:, :], alog.ap().unsqueeze(1)),
        (pm_row[:, :], pmask[:, :]),
        (tsx_col[0:1, 0:1], xts[:, :]),
    ]:
        dma("y", dst, src, "sLD")
    dcolf = alloc("dcolf", [PD, NH], F32)
    wfbuf = alloc("wfbuf", [PD, D], F32)
    tmp_s = alloc("tmp_s", [NH, TC], F32)
    tmp_s2 = alloc("tmp_s2", [NH, TC], F32)
    dma("y", dcolf[:, :], dvec.ap().rearrange("(hc p) -> p hc", p=PD), "sLD")

    wait("s", "sLD")
    Sc(lambda E: E.activation(acol[:, :], alog_col[:, :], AF.Exp))
    wait("v", "c_s")
    wait("v", "sLD")
    def v0(E):
        E.tensor_scalar(acol[:, :], acol[:, :], -1.0, None, ALU.mult)
        E.drain()
        return E.tensor_copy(d16[:, :], dcolf[:, :])
    Vc(v0)

    dma("y", wfbuf[:, :], wffs[0:PD, :], "sLD")
    wait("v", "sLD")
    Vc(lambda E: E.tensor_reduce(wam[:, 0:1], wfbuf[:, :], AX.X, ALU.max,
                                 apply_absolute_value=True))
    wait("y", "c_v")
    dma("y", wfbuf[:, :], wffs[PD:2 * PD, :], "sLD")
    wait("v", "sLD")
    def vwa2(E):
        E.tensor_reduce(wam[:, 1:2], wfbuf[:, :], AX.X, ALU.max,
                        apply_absolute_value=True)
        E.drain()
        return E.tensor_reduce(wam[:, 2:3], wam[:, 0:2], AX.X, ALU.max)
    Vc(vwa2)
    wait("t", "c_v")
    Tc(lambda E: E.transpose(pt32[0:1, 0:PD], wam[:, 2:3], identf[:, :]))
    wait("v", "c_t")
    def vwa3(E):
        E.tensor_copy(wam_r[0:1, 0:PD], pt32[0:1, 0:PD])
        E.drain()
        return E.tensor_reduce(wam_r[0:1, 0:1], wam_r[0:1, 0:PD], AX.X, ALU.max)
    Vc(vwa3)
    wait("y", "c_v")
    dma("y", cc_am_in[0:1, 0:8], wam_r[0:1, 0:8], "sST")
    wait("g", "sST")
    Gc(lambda E: E.collective_compute("AllReduce", ALU.max, RG,
                                      [cc_am_in[:, :]], [cc_am_out[:, :]]))
    wait("y", "c_g")
    dma("y", wam_r[0:1, 0:1], cc_am_out[0:1, 0:1], "sLD")

    h16 = alloc("h16", [PD, KC * TH], BF16)
    xnr16 = alloc("xnr16", [PD, D], BF16)
    xtile = alloc("xtile", [PD, D], F32)

    tile_rows = [HALO] + [PD] * NCH
    tile_r0 = [0] + [HALO + i * PD for i in range(NCH)]

    for ti in range(9):
        r0, nr = tile_r0[ti], tile_rows[ti]
        if ti > 0:
            wait("y", "c_s")
            wait("s", "c_t")
        dma("y", xtile[0:nr, :], xs[r0:r0 + nr, :], "sLD")
        wait("s", "sLD")
        st = stats[:, ti * 8:(ti + 1) * 8]
        def s1(E, nr=nr):
            E.activation(xnr16[0:nr, :], xtile[0:nr, :], AF.Copy, accum_out=sacc[0:nr, 0:1])
            E.drain()
            return E.activation(xnr16[0:nr, :], xtile[0:nr, :], AF.Square, accum_out=sacc[0:nr, 1:2])
        Sc(s1)
        wait("v", "c_s")
        def v1(E, nr=nr, st=st):
            v = E
            v.tensor_scalar(st[0:nr, 0:1], sacc[0:nr, 0:1], 1.0 / D, None, ALU.mult)
            v.tensor_scalar(sacc[0:nr, 2:3], sacc[0:nr, 1:2], 1.0 / D, None, ALU.mult)
            v.drain()
            v.tensor_tensor(sacc[0:nr, 3:4], st[0:nr, 0:1], st[0:nr, 0:1], ALU.mult)
            v.drain()
            v.tensor_tensor(sacc[0:nr, 2:3], sacc[0:nr, 2:3], sacc[0:nr, 3:4], ALU.subtract)
            v.drain()
            v.tensor_scalar(sacc[0:nr, 3:4], sacc[0:nr, 2:3], 1e-5, None, ALU.add)
            v.drain()
            return v.reciprocal(st[0:nr, 4:5], sacc[0:nr, 3:4])
        Vc(v1)
        wait("s", "c_v")
        Sc(lambda E, nr=nr, st=st: E.activation(st[0:nr, 1:2], st[0:nr, 4:5], AF.Sqrt))
        wait("v", "c_s")
        def v2(E, nr=nr, st=st):
            v = E
            v.tensor_tensor(sacc[0:nr, 0:1], st[0:nr, 1:2], st[0:nr, 1:2], ALU.mult)
            v.drain()
            v.tensor_tensor(sacc[0:nr, 0:1], sacc[0:nr, 0:1], sacc[0:nr, 3:4], ALU.mult)
            v.drain()
            v.tensor_scalar(sacc[0:nr, 0:1], sacc[0:nr, 0:1], -0.5, 1.5, ALU.mult, ALU.add)
            v.drain()
            v.tensor_tensor(st[0:nr, 1:2], st[0:nr, 1:2], sacc[0:nr, 0:1], ALU.mult)
            v.drain()
            v.tensor_tensor(sacc[0:nr, 0:1], st[0:nr, 1:2], st[0:nr, 1:2], ALU.mult)
            v.drain()
            v.tensor_tensor(sacc[0:nr, 0:1], sacc[0:nr, 0:1], sacc[0:nr, 2:3], ALU.mult)
            v.drain()
            v.tensor_scalar(sacc[0:nr, 0:1], sacc[0:nr, 0:1], 1e-6, None, ALU.add)
            v.drain()
            return v.reciprocal(sacc[0:nr, 1:2], sacc[0:nr, 0:1])
        Vc(v2)
        wait("s", "c_v")
        Sc(lambda E, nr=nr, st=st: E.activation(st[0:nr, 2:3], sacc[0:nr, 1:2], AF.Sqrt))
        wait("v", "c_s")
        def v3(E, nr=nr, st=st):
            v = E
            v.tensor_tensor(sacc[0:nr, 2:3], st[0:nr, 2:3], st[0:nr, 2:3], ALU.mult)
            v.drain()
            v.tensor_tensor(sacc[0:nr, 2:3], sacc[0:nr, 2:3], sacc[0:nr, 0:1], ALU.mult)
            v.drain()
            v.tensor_scalar(sacc[0:nr, 2:3], sacc[0:nr, 2:3], -0.5, 1.5, ALU.mult, ALU.add)
            v.drain()
            v.tensor_tensor(st[0:nr, 2:3], st[0:nr, 2:3], sacc[0:nr, 2:3], ALU.mult)
            v.drain()
            v.tensor_tensor(st[0:nr, 3:4], st[0:nr, 1:2], st[0:nr, 2:3], ALU.mult)
            v.drain()
            v.tensor_tensor(st[0:nr, 5:6], st[0:nr, 0:1], st[0:nr, 3:4], ALU.mult)
            v.drain()
            v.tensor_scalar(st[0:nr, 5:6], st[0:nr, 5:6], -1.0, None, ALU.mult)
            v.drain()
            v.tensor_tensor(st[0:nr, 6:7], st[0:nr, 0:1], st[0:nr, 1:2], ALU.mult)
            v.drain()
            return v.tensor_scalar(st[0:nr, 6:7], st[0:nr, 6:7], -1.0, None, ALU.mult)
        Vc(v3)
        wait("s", "c_v")
        Sc(lambda E, nr=nr, st=st: E.activation(xnr16[0:nr, :], xtile[0:nr, :], AF.Identity,
                                                bias=st[0:nr, 5:6], scale=st[0:nr, 3:4]))
        wait("t", "c_s")
        for half in range(2):
            def tt(E, nr=nr, half=half):
                last = None
                for k in range(8):
                    kc = half * 8 + k
                    last = E.transpose(pt16[0:PD, k * PD:k * PD + nr],
                                       xnr16[0:nr, kc * PD:(kc + 1) * PD],
                                       ident16[0:nr, 0:nr])
                return last
            Tc(tt)
            wait("v", "c_t")
            def vh(E, r0=r0, nr=nr, half=half):
                last = None
                for k in range(8):
                    kc = half * 8 + k
                    last = E.tensor_scalar(h16[:, kc * TH + r0: kc * TH + r0 + nr],
                                           pt16[0:PD, k * PD:k * PD + nr],
                                           ns_col[:, kc:kc + 1], None, ALU.mult)
                return last
            Vc(vh)
            wait("t", "c_v")
        if ti >= 1:
            wait("y", "c_v")
            dma("y", statrow[0:1, (ti - 1) * PD:(ti - 1) * PD + nr], st[0:nr, 6:7], "sLD")

    if DEBUG:
        for e in ("v", "s", "t"):
            wait("y", "c_" + e)
    dbg_dump("h16", h16[:, :], [PD, KC * TH], BF16)

    w1buf = alloc("w1buf", [PD, 2 * KC * 256], BF16)
    psA = palloc("psA", [PD, 512], F32)
    psB = palloc("psB", [PD, 512], F32)
    psC = palloc("psC", [PD, 512], F32)
    raw16 = alloc("raw16", [PD, TH + 1], BF16)
    ssm16 = alloc("ssm16", [PD, TC], BF16)
    convt = alloc("convt", [PD, TC], F32)
    convt2 = alloc("convt2", [PD, TC], F32)
    sgst = alloc("sgst", [PD, 2 * TC], BF16)
    x_tm = alloc("x_tm", [PD, NCH * DI], BF16, side="right")

    chunk_order = [64, 65, 66] + list(range(32, 64)) + list(range(0, 32))
    GSZ = 2
    groups = [chunk_order[i:i + GSZ] for i in range(0, len(chunk_order), GSZ)]
    w1_marks = []

    def w1_load(gi):
        slot = gi % 2
        for j, c in enumerate(groups[gi]):
            c0 = c * PD
            w = min(PD, PROJ - c0)
            dst = w1buf[:, slot * KC * 256:(slot + 1) * KC * 256] \
                .rearrange("p (kc c) -> p kc c", c=256)[:, :, j * PD:j * PD + w]
            src = w1[:, c0:c0 + w].rearrange("(kc p) c -> p kc c", p=PD)
            dma("g", dst, src, "sW1")
        w1_marks.append(cnt["sW1"])

    w1_load(0)
    w1_load(1)

    h16v = h16[:, :].rearrange("p (kc t) -> p kc t", t=TH)

    def in_proj_mms(E, gi, gslot, c, tiles):
        slot = gi % 2
        w = min(PD, PROJ - c * PD)
        wt = w1buf[:, slot * KC * 256:(slot + 1) * KC * 256] \
            .rearrange("p (kc c) -> p kc c", c=256)
        last = None
        for kc in range(KC):
            for (ps, t0, tw) in tiles:
                last = E.matmul(ps[0:w, 0:tw], wt[:, kc, gslot * PD:gslot * PD + w],
                                h16v[:, kc, t0:t0 + tw],
                                start=(kc == 0), stop=(kc == KC - 1))
        return last

    def chunk_sched(idx, gi):
        gslot = idx % GSZ
        if gslot == 0:
            wait("t", "sW1", w1_marks[gi])
        return gslot

    ssm_tiles = [(psA, 0, 512), (psB, 512, 512), (psC, 1024, TH - 1024)]
    oth_tiles = [(psA, HALO, 512), (psB, HALO + 512, 512)]

    for idx, c in enumerate(chunk_order):
        gi = idx // GSZ
        gslot = chunk_sched(idx, gi)
        is_ssm = 32 <= c < 64
        tiles = ssm_tiles if is_ssm else oth_tiles
        Tc(lambda E, gi=gi, gslot=gslot, c=c, tiles=tiles: in_proj_mms(E, gi, gslot, c, tiles))
        if gslot == GSZ - 1:
            if gi + 2 < len(groups):
                wait("g", "c_t")
                w1_load(gi + 2)
        if c in (64, 65, 66):
            wait("v", "c_t")
            def vev(E, c=c):
                v = E
                last = None
                for (ps, t0, tw) in [(psA, 0, 512), (psB, 512, 512)]:
                    if c == 64:
                        v.tensor_copy(dtraw[0:NH, t0:t0 + tw], ps[0:NH, 0:tw])
                        for q in range(3):
                            last = v.tensor_copy(b_fm[q * 32:(q + 1) * 32, t0:t0 + tw],
                                                 ps[32 + q * 32:64 + q * 32, 0:tw])
                    elif c == 65:
                        v.tensor_copy(b_fm[96:PD, t0:t0 + tw], ps[0:32, 0:tw])
                        for q in range(3):
                            last = v.tensor_copy(c_fm[q * 32:(q + 1) * 32, t0:t0 + tw],
                                                 ps[32 + q * 32:64 + q * 32, 0:tw])
                    else:
                        last = v.tensor_copy(c_fm[96:PD, t0:t0 + tw], ps[0:32, 0:tw])
                return last
            Vc(vev)
            wait("t", "c_v")
        elif is_ssm:
            hc = c - 32
            wait("v", "c_t")
            def vr(E):
                v = E
                v.tensor_copy(raw16[:, 0:512], psA[:, 0:512])
                v.tensor_copy(raw16[:, 512:1024], psB[:, 0:512])
                return v.tensor_copy(raw16[:, 1024:TH], psC[:, 0:TH - 1024])
            Vc(vr)
            wait("t", "c_v")
            def vc(E, hc=hc):
                v = E
                v.tensor_scalar(convt[:, :], raw16[:, 0:TC], cw_col[:, hc * 4:hc * 4 + 1],
                                None, ALU.mult)
                v.drain()
                v.scalar_tensor_tensor(convt2[:, :], raw16[:, 1:1 + TC],
                                       cw_col[:, hc * 4 + 1:hc * 4 + 2], convt[:, :],
                                       ALU.mult, ALU.add)
                v.drain()
                v.scalar_tensor_tensor(convt[:, :], raw16[:, 2:2 + TC],
                                       cw_col[:, hc * 4 + 2:hc * 4 + 3], convt2[:, :],
                                       ALU.mult, ALU.add)
                v.drain()
                return v.scalar_tensor_tensor(convt2[:, :], raw16[:, 3:3 + TC],
                                              cw_col[:, hc * 4 + 3:hc * 4 + 4], convt[:, :],
                                              ALU.mult, ALU.add)
            Vc(vc)
            wait("s", "c_v")
            Sc(lambda E, hc=hc: E.activation(ssm16[:, :], convt2[:, :], AF.Silu,
                                             bias=cb_col[:, hc:hc + 1]))
            wait("t", "c_s")
            for half in range(2):
                def tx(E, half=half):
                    last = None
                    for k in range(4):
                        tc = half * 4 + k
                        last = E.transpose(pt16[0:PD, k * PD:(k + 1) * PD],
                                           ssm16[:, tc * PD:(tc + 1) * PD], ident16[:, :])
                    return last
                Tc(tx)
                wait("v", "c_t")
                def vx(E, hc=hc, half=half):
                    last = None
                    for k in range(4):
                        tc = half * 4 + k
                        last = E.tensor_copy(
                            x_tm[:, tc * DI + hc * PD:tc * DI + (hc + 1) * PD],
                            pt16[0:PD, k * PD:(k + 1) * PD])
                    return last
                Vc(vx)
                wait("t", "c_v")
        else:
            gc = c
            gslot2 = gc % 2
            wait("s", "c_t")
            if gc >= 2:
                wait("s", "sSP", 16 * (gc - 1))
            def sg_(E, gslot2=gslot2):
                E.activation(sgst[:, gslot2 * TC:gslot2 * TC + 512], psA[:, 0:512], AF.Silu)
                E.drain()
                return E.activation(sgst[:, gslot2 * TC + 512:(gslot2 + 1) * TC],
                                    psB[:, 0:512], AF.Silu)
            Sc(sg_)
            wait("t", "c_s")
            wait("y", "c_s")
            dma("y", sgd[gc * PD:(gc + 1) * PD, :],
                sgst[:, gslot2 * TC:(gslot2 + 1) * TC], "sSP")

        if idx == 2:
            emit_dt_path = True
            wait("s", "c_v")
            def sdt(E):
                s = E
                s.activation(tmp_s[0:NH, :], dtraw[0:NH, :], AF.Abs, bias=dtb_col[0:NH, 0:1])
                s.drain()
                s.activation(tmp_s[0:NH, :], tmp_s[0:NH, :], AF.Exp, scale=-1.0)
                s.drain()
                s.activation(tmp_s[0:NH, :], tmp_s[0:NH, :], AF.Ln, bias=1.0)
                s.drain()
                return s.activation(tmp_s2[0:NH, :], dtraw[0:NH, :], AF.Relu,
                                    bias=dtb_col[0:NH, 0:1])
            Sc(sdt)
            wait("v", "c_s")
            def vdt(E):
                v = E
                v.tensor_tensor(dt_fm[0:NH, :], tmp_s[0:NH, :], tmp_s2[0:NH, :], ALU.add)
                v.drain()
                v.tensor_scalar(tmp_s[0:NH, :], dt_fm[0:NH, :], acol[0:NH, 0:1], None, ALU.mult)
                v.drain()
                for ci in range(NCH):
                    sl = slice(ci * CH, (ci + 1) * CH)
                    v.tensor_tensor_scan(cum_fm[0:NH, sl], tmp_s[0:NH, sl], tmp_s[0:NH, sl],
                                         0.0, ALU.add, ALU.bypass)
                v.drain()
                for ci in range(NCH):
                    v.tensor_copy(clog[0:NH, ci:ci + 1],
                                  cum_fm[0:NH, (ci + 1) * CH - 1:(ci + 1) * CH])
                v.drain()
                v.tensor_reduce(ld_seg[0:NH, 0:1], clog[0:NH, :], AX.X, ALU.add)
                v.drain()
                v.tensor_tensor_scan(tmp_s[0:NH, 0:NCH], clog[0:NH, :], clog[0:NH, :],
                                     0.0, ALU.add, ALU.bypass)
                v.drain()
                return v.scalar_tensor_tensor(tmp_s2[0:NH, 0:NCH], tmp_s[0:NH, 0:NCH],
                                              -1.0, ld_seg[0:NH, 0:1].broadcast_to([NH, NCH]),
                                              ALU.mult, ALU.add)
            Vc(vdt)
            wait("s", "c_v")
            def sdt2(E):
                s = E
                s.activation(tmp_s[0:NH, 0:NCH], tmp_s2[0:NH, 0:NCH], AF.Exp)
                s.drain()
                s.activation(tmp_s2[0:NH, 0:NCH], clog[0:NH, :], AF.Exp)
                s.drain()
                s.activation(df_fm[0:NH, :], cum_fm[0:NH, :], AF.Exp)
                s.drain()
                last = None
                for ci in range(NCH):
                    sl = slice(ci * CH, (ci + 1) * CH)
                    last = s.activation(dtraw[0:NH, sl], cum_fm[0:NH, sl], AF.Exp,
                                        scale=-1.0, bias=clog[0:NH, ci:ci + 1])
                return last
            Sc(sdt2)
            for ci in range(NCH):
                wait("t", "c_s")
                wait("t", "c_v")
                def tdt(E, ci=ci):
                    sl = slice(ci * CH, (ci + 1) * CH)
                    E.transpose(pt32[0:CH, 0:NH], dt_fm[0:NH, sl], identf[0:NH, 0:NH])
                    E.transpose(pt32[0:CH, NH:2 * NH], dtraw[0:NH, sl],
                                identf[0:NH, 0:NH])
                    return E.transpose(pt32[0:CH, 2 * NH:3 * NH], cum_fm[0:NH, sl],
                                       identf[0:NH, 0:NH])
                Tc(tdt)
                wait("v", "c_t")
                def vdt2(E, ci=ci):
                    v = E
                    v.tensor_copy(dt_tm[:, ci * NH:(ci + 1) * NH], pt32[0:CH, 0:NH])
                    v.tensor_copy(dte_tm[:, ci * NH:(ci + 1) * NH], pt32[0:CH, NH:2 * NH])
                    return v.tensor_copy(cum_tm[:, ci * NH:(ci + 1) * NH],
                                         pt32[0:CH, 2 * NH:3 * NH])
                Vc(vdt2)
            wait("y", "c_s")
            dma("y", srow[0:1, 0:NH * NCH], tmp_s[0:NH, 0:NCH], "sLD")
            dma("y", srow[0:1, NH * NCH:2 * NH * NCH], tmp_s2[0:NH, 0:NCH], "sLD")
            wait("t", "sLD")
            wait("t", "c_v")
            def twc2(E):
                E.matmul(pt32[:, 0:256], ones_colf[:, :], srow[0:1, 0:256],
                         start=True, stop=False)
                return E.matmul(pt32[:, 256:512], ones_colf[:, :], srow[0:1, 256:512],
                                start=False, stop=True)
            Tc(twc2)
            wait("v", "c_t")
            def vwc2(E):
                v = E
                v.tensor_copy(wcd_tm[:, :], pt32[:, 0:256])
                v.tensor_copy(dc_cols[:, :], pt32[:, 256:512])
                v.drain()
                return v.tensor_tensor(
                    wcd_tm[:, :].rearrange("p (h c) -> p h c", c=NCH),
                    wcd_tm[:, :].rearrange("p (h c) -> p h c", c=NCH),
                    dte_tm[:, :].rearrange("p (c h) -> p c h", h=NH).transpose([0, 2, 1]),
                    ALU.mult)
            Vc(vwc2)
            wait("t", "c_v")
            def vbc(E):
                E.tensor_copy(b16[:, :], b_fm[:, :])
                return E.tensor_copy(c16[:, :], c_fm[:, :])
            Vc(vbc)
            wait("t", "c_v")
            for half in range(2):
                def tb(E, half=half):
                    last = None
                    for k in range(4):
                        ci = half * 4 + k
                        last = E.transpose(pt16[0:PD, k * PD:(k + 1) * PD],
                                           b16[:, ci * CH:(ci + 1) * CH], ident16[:, :])
                    return last
                Tc(tb)
                wait("v", "c_t")
                def vb(E, half=half):
                    last = None
                    for k in range(4):
                        ci = half * 4 + k
                        last = E.tensor_copy(btm16[:, ci * PD:(ci + 1) * PD],
                                             pt16[0:PD, k * PD:(k + 1) * PD])
                    return last
                Vc(vb)
                wait("t", "c_v")

    if DEBUG:
        for e in ("v", "s", "t"):
            wait("y", "c_" + e)
    dbg_dump("x_tm", x_tm[:, :], [PD, NCH * DI], BF16)
    dbg_dump("b_fm", b_fm[:, :], [PD, TC], F32)
    dbg_dump("c_fm", c_fm[:, :], [PD, TC], F32)
    dbg_dump("dt_fm", dt_fm[0:NH, :], [NH, TC], F32)
    dbg_dump("cum_fm", cum_fm[0:NH, :], [NH, TC], F32)
    dbg_dump("dte_fm", dtraw[0:NH, :], [NH, TC], F32)
    dbg_dump("df_fm", df_fm[0:NH, :], [NH, TC], F32)
    dbg_dump("wcd", wcd_tm[:, :], [PD, NCH * NH], F32)
    dbg_dump("dcc", dc_cols[:, :], [PD, NCH * NH], F32)
    dbg_dump("dt_tm", dt_tm[:, :], [PD, NCH * NH], F32)
    dbg_dump("cum_tm", cum_tm[:, :], [PD, NCH * NH], F32)

    free(sgst)
    free(convt2)
    free(convt)
    free(ssm16)
    free(raw16)
    free(psC)
    free(psB)
    free(psA)
    free(w1buf)
    free(xtile)
    free(xnr16)
    free(h16)
    free(pt16)
    free(pt32)
    barrier()

    x2p16 = alloc("x2p16", [PD, DI], BF16)
    sseg = palloc("sseg", [PD, DI], F32)
    for ci in range(NCH):
        wait("v", "c_t")
        def vx3(E, ci=ci):
            v = E
            dslc = dt_tm[:, ci * NH:(ci + 1) * NH]
            wslc = wcd_tm[:, :].rearrange("p (h c) -> p h c", c=NCH)[:, :, ci]
            v.tensor_tensor(x2p16[:, :].rearrange("p (h q) -> p h q", q=PD),
                            x_tm[:, ci * DI:(ci + 1) * DI].rearrange("p (h q) -> p h q", q=PD),
                            dslc.unsqueeze(2).broadcast_to([PD, NH, PD]), ALU.mult)
            v.drain()
            return v.tensor_tensor(x2p16[:, :].rearrange("p (h q) -> p h q", q=PD),
                                   x2p16[:, :].rearrange("p (h q) -> p h q", q=PD),
                                   wslc.unsqueeze(2).broadcast_to([PD, NH, PD]), ALU.mult)
        Vc(vx3)
        wait("t", "c_v")
        def tz(E, ci=ci):
            last = None
            for h in range(NH):
                last = E.matmul(sseg[:, h * PD:(h + 1) * PD],
                                btm16[:, ci * PD:(ci + 1) * PD],
                                x2p16[:, h * PD:(h + 1) * PD],
                                start=(ci == 0 and h % 4 == 0),
                                stop=(ci == NCH - 1 and h % 4 == 3))
            return last
        Tc(tz)
    wait("v", "c_t")
    ssg_sb = alloc("ssg_sb", [PD, DI], F32)
    Vc(lambda E: E.tensor_copy(ssg_sb[:, :], sseg[:, :]))
    free(sseg)
    def vldz(E):
        E.memset(wam[:, 3:4], 0.0)
        E.drain()
        return E.tensor_copy(wam[0:NH, 3:4], ld_seg[0:NH, 0:1])
    Vc(vldz)
    wait("y", "c_v")
    dma("y", cc_st_in[:, 0:DI], ssg_sb[:, :], "sST")
    dma("y", cc_st_in[:, DI:DI + 1], wam[:, 3:4], "sST")
    wait("g", "sST")
    Gc(lambda E: E.collective_compute("AllGather", ALU.bypass, RG,
                                      [cc_st_in[:, :]], [cc_st_out[:, :, :]]))
    dbg_dump("sseg", ssg_sb[:, :], [PD, DI], F32)
    free(ssg_sb)
    free(x2p16)
    barrier()

    p32b = palloc("pt32b", [PD, 512], F32)
    p16b = palloc("pt16b", [PD, 1024], BF16)
    alloc_quant()
    wait("t", "sLD")
    def ttsw(E):
        E.matmul(p32b[:, 0:1], ones_colf[:, :], wam_r[0:1, 0:1], start=True, stop=False)
        return E.matmul(p32b[:, 1:2], ones_colf[:, :], tsx_col[0:1, 0:1],
                        start=False, stop=True)
    Tc(ttsw)
    wait("v", "c_t")
    def vtsw(E):
        E.tensor_scalar(tsw_col[:, 0:1], p32b[:, 0:1], 1.0 / (448.0 * 6.0), None, ALU.mult)
        E.drain()
        return E.tensor_copy(tsx_col[:, 0:1], p32b[:, 1:2])
    Vc(vtsw)
    wait("t", "c_v")

    def emit_quant(src_ap, ts_ap, nr=PD):
        qs1, qs2, qs3, qsi = QB["qs1"], QB["qs2"], QB["qs3"], QB["qsi"]
        qb1, qb2, qb3, qbi = QB["qb1"], QB["qb2"], QB["qb3"], QB["qbi"]
        qdq16 = QB["qdq16"]
        def vq1(E):
            v = E
            v.tensor_reduce(qb1[0:nr, :], exp16(src_ap), AX.X, ALU.max,
                            apply_absolute_value=True)
            v.reciprocal(sacc[0:nr, 0:1], ts_ap[0:nr, 0:1])
            v.drain()
            v.tensor_scalar(qb1[0:nr, :], qb1[0:nr, :], sacc[0:nr, 0:1], 1.0 / 6.0,
                            ALU.mult, ALU.mult)
            v.drain()
            return v.tensor_scalar(qb1[0:nr, :], qb1[0:nr, :], 448.0, 2.0 ** -20,
                                   ALU.min, ALU.max)
        Vc(vq1)
        wait("s", "c_v")
        Sc(lambda E: E.activation(qb2[0:nr, :], qb1[0:nr, :], AF.Ln))
        wait("v", "c_s")
        def vq2(E):
            v = E
            v.tensor_scalar(qb2[0:nr, :], qb2[0:nr, :], 1.0 / LN2, 16.0, ALU.mult, ALU.add)
            v.drain()
            v.tensor_scalar(qbi[0:nr, :], qb2[0:nr, :], 0.0, None, ALU.add)
            v.drain()
            v.scalar_tensor_tensor(qb3[0:nr, :], qb2[0:nr, :], 0.0, qbi[0:nr, :],
                                   ALU.add, ALU.is_lt)
            v.drain()
            v.tensor_tensor(qb2[0:nr, :], qbi[0:nr, :], qb3[0:nr, :], ALU.subtract)
            v.drain()
            return v.tensor_scalar(qb2[0:nr, :], qb2[0:nr, :], 10.0, None, ALU.max)
        Vc(vq2)
        wait("s", "c_v")
        def sq2(E):
            E.activation(qb3[0:nr, :], qb2[0:nr, :], AF.Exp, scale=-LN2, bias=cconst[0:nr, 0:1])
            E.drain()
            return E.activation(qb2[0:nr, :], qb2[0:nr, :], AF.Exp, scale=LN2,
                                bias=cconst[0:nr, 1:2])
        Sc(sq2)
        wait("v", "c_s")
        def vq3(E):
            v = E
            v.tensor_tensor(qb1[0:nr, :], qb1[0:nr, :], qb3[0:nr, :], ALU.mult)
            v.drain()
            rup(v, qb3[0:nr, :], qb1[0:nr, :], qbi[0:nr, :], qb1[0:nr, :])
            v.drain()
            v.tensor_tensor(qb3[0:nr, :], qb3[0:nr, :], qb2[0:nr, :], ALU.mult)
            v.drain()
            v.tensor_scalar(qb3[0:nr, :], qb3[0:nr, :], 512.0, 0.25, ALU.mult, ALU.add)
            v.drain()
            rup(v, qb2[0:nr, :], qb3[0:nr, :], qbi[0:nr, :], qb1[0:nr, :])
            v.drain()
            v.tensor_scalar(qb1[0:nr, :], qb2[0:nr, :], 1.0 / 512.0, None, ALU.mult)
            v.drain()
            v.tensor_scalar(qb2[0:nr, :], qb1[0:nr, :], 0.0, None, ALU.is_equal)
            v.drain()
            v.tensor_tensor(qb2[0:nr, :], qb2[0:nr, :], qb1[0:nr, :], ALU.add)
            v.drain()
            v.tensor_scalar(qb2[0:nr, :], qb2[0:nr, :], ts_ap[0:nr, 0:1], None, ALU.mult)
            v.drain()
            v.reciprocal(qb2[0:nr, :], qb2[0:nr, :])
            v.drain()
            return v.tensor_tensor(exp16(qs1[0:nr, :]), exp16(src_ap), bc16(qb2[0:nr, :]),
                                   ALU.mult)
        Vc(vq3)
        wait("s", "c_v")
        def sq3(E):
            E.activation(qs2[0:nr, :], qs1[0:nr, :], AF.Abs)
            E.drain()
            return E.activation(qs3[0:nr, :], qs1[0:nr, :], AF.Sign)
        Sc(sq3)
        wait("v", "c_s")
        def vq4(E):
            v = E
            v.tensor_scalar(qs1[0:nr, :], qs2[0:nr, :], 2.0, None, ALU.mult)
            v.drain()
            v.scalar_tensor_tensor(qs1[0:nr, :], qs2[0:nr, :], 2.0, qs1[0:nr, :],
                                   ALU.add, ALU.min)
            v.drain()
            v.tensor_scalar(qs2[0:nr, :], qs2[0:nr, :], 0.5, 4.0, ALU.mult, ALU.add)
            v.drain()
            v.tensor_tensor(qs1[0:nr, :], qs1[0:nr, :], qs2[0:nr, :], ALU.min)
            v.drain()
            v.tensor_scalar(qs1[0:nr, :], qs1[0:nr, :], 7.0, None, ALU.min)
            v.drain()
            rup(v, qs2[0:nr, :], qs1[0:nr, :], qsi[0:nr, :], qs1[0:nr, :])
            v.drain()
            v.tensor_scalar(qs1[0:nr, :], qs2[0:nr, :], 0.5, None, ALU.mult)
            v.drain()
            v.scalar_tensor_tensor(qs1[0:nr, :], qs2[0:nr, :], -2.0, qs1[0:nr, :],
                                   ALU.add, ALU.max)
            v.drain()
            v.tensor_scalar(qs2[0:nr, :], qs2[0:nr, :], 2.0, -8.0, ALU.mult, ALU.add)
            v.drain()
            v.tensor_tensor(qs1[0:nr, :], qs1[0:nr, :], qs2[0:nr, :], ALU.max)
            v.drain()
            v.tensor_tensor(qs1[0:nr, :], qs1[0:nr, :], qs3[0:nr, :], ALU.mult)
            v.drain()
            return v.tensor_tensor(exp16(qdq16[0:nr, :]), exp16(qs1[0:nr, :]),
                                   bc16(qb1[0:nr, :]), ALU.mult)
        Vc(vq4)

    wqt_sb = alloc("wqt_sb", [PD, KC * 256], BF16)
    for wi in range(2):
        wait("y", "c_v")
        dma("y", wfbuf[:, :], wffs[wi * PD:(wi + 1) * PD, :], "sLD")
        wait("v", "sLD")
        emit_quant(wfbuf[:, :], tsw_col)
        wait("t", "c_v")
        for half in range(2):
            def twq(E, half=half, q16=QB["qdq16"]):
                last = None
                for k in range(8):
                    kc = half * 8 + k
                    last = E.transpose(p16b[0:PD, k * PD:(k + 1) * PD],
                                       q16[:, kc * PD:(kc + 1) * PD], ident16[:, :])
                return last
            Tc(twq)
            wait("v", "c_t")
            def vwq(E, wi=wi, half=half):
                last = None
                for k in range(8):
                    kc = half * 8 + k
                    last = E.tensor_copy(
                        wqt_sb[:, kc * 256 + wi * PD:kc * 256 + (wi + 1) * PD],
                        p16b[0:PD, k * PD:(k + 1) * PD])
                return last
            Vc(vwq)
            wait("t", "c_v")
    wait("y", "c_v")
    dma("y", cc_wq_in[:, :], wqt_sb[:, :], "sST")
    wait("g", "sST")
    Gc(lambda E: E.collective_compute("AllGather", ALU.bypass, RG,
                                      [cc_wq_in[:, :]], [cc_wq_out[:, :, :]]))
    free(wqt_sb)
    free_quant()
    barrier()

    wait("y", "c_g")
    for u in range(NC):
        dma("y", ldg[0:NH, u:u + 1], cc_st_out[u, 0:NH, DI:DI + 1], "sLD")
    dma("y", pmk[0:1, :], pmask[:, :], "sLD")
    wait("t", "sLD")
    def tpm(E):
        return E.matmul(p32b[0:NH, 0:NC], ones_colf[0:1, 0:NH], pmk[0:1, :],
                        start=True, stop=True)
    Tc(tpm)
    wait("v", "c_t")
    Vc(lambda E: E.tensor_copy(pmk[0:NH, :], p32b[0:NH, 0:NC]))
    wait("v", "sLD")
    def vwm2(E):
        v = E
        v.tensor_tensor(ldg[0:NH, 0:NC], ldg[0:NH, 0:NC], pmk[0:NH, :], ALU.mult)
        v.drain()
        v.tensor_tensor_scan(tmp_s[0:NH, 0:NC], ldg[0:NH, 0:NC], ldg[0:NH, 0:NC],
                             0.0, ALU.add, ALU.bypass)
        v.drain()
        v.tensor_reduce(ldg[0:NH, NC:NC + 1], ldg[0:NH, 0:NC], AX.X, ALU.add)
        v.drain()
        return v.scalar_tensor_tensor(tmp_s[0:NH, 0:NC], tmp_s[0:NH, 0:NC], -1.0,
                                      ldg[0:NH, NC:NC + 1].broadcast_to([NH, NC]),
                                      ALU.mult, ALU.add)
    Vc(vwm2)
    wait("s", "c_v")
    Sc(lambda E: E.activation(tmp_s[0:NH, 0:NC], tmp_s[0:NH, 0:NC], AF.Exp))
    wait("v", "c_s")
    Vc(lambda E: E.tensor_tensor(tmp_s[0:NH, 0:NC], tmp_s[0:NH, 0:NC], pmk[0:NH, :],
                                 ALU.mult))
    wait("y", "c_v")
    dma("y", srow[0:1, 0:NH * NC], tmp_s[0:NH, 0:NC], "sLD")
    wait("t", "sLD")
    wait("t", "c_v")
    def twm2(E):
        return E.matmul(p32b[:, 0:256], ones_colf[:, :], srow[0:1, 0:256],
                        start=True, stop=True)
    Tc(twm2)
    wait("v", "c_t")
    Vc(lambda E: E.tensor_copy(wm_cols[:, 0:256], p32b[:, 0:256]))
    wait("t", "c_v")

    s_run = alloc("s_run", [PD, DI], F32)
    s16 = alloc("s16", [PD, DI], BF16)
    sgu = alloc("sgu", [PD, 2 * DI], F32)
    sgu_sems = ["sLD", "sDT", "sXT", "sFF"]
    vdone = []
    for u in range(NC):
        slot = u % 2
        if u >= 2:
            wait("y", "c_v", vdone[u - 2])
        for j in range(8):
            dma("y", sgu[:, slot * DI + j * 512:slot * DI + (j + 1) * 512],
                cc_st_out[u, :, j * 512:(j + 1) * 512], sgu_sems[j % 4])
        for s_ in sgu_sems:
            wait("v", s_)
        for h in range(NH):
            if u == 0:
                def vch0(E, h=h, slot=slot):
                    return E.tensor_scalar(s_run[:, h * PD:(h + 1) * PD],
                                           sgu[:, slot * DI + h * PD:slot * DI + (h + 1) * PD],
                                           wm_cols[:, h * NC:h * NC + 1], None, ALU.mult)
                Vc(vch0)
            else:
                def vchu(E, h=h, u=u, slot=slot):
                    return E.scalar_tensor_tensor(s_run[:, h * PD:(h + 1) * PD],
                                                  sgu[:, slot * DI + h * PD:slot * DI + (h + 1) * PD],
                                                  wm_cols[:, h * NC + u:h * NC + u + 1],
                                                  s_run[:, h * PD:(h + 1) * PD],
                                                  ALU.mult, ALU.add)
                Vc(vchu)
        vdone.append(cnt["c_v"])
    Vc(lambda E: E.tensor_copy(s16[:, :], s_run[:, :]))
    if DEBUG:
        wait("y", "c_v")
    dbg_dump("sinit", s_run[:, :], [PD, DI], F32)
    dbg_dump("wm", wm_cols[:, :], [PD, NC * NH], F32)
    free(sgu)
    free(p16b)
    free(p32b)
    barrier()

    crow = alloc("crow", [1, 2 * 1024], F32)
    dfrow = alloc("dfrow", [1, 1024], F32)
    sgbuf = alloc("sgbuf", [PD, 2 * NH * PD], BF16)
    x3c = alloc("x3c", [PD, 1024], BF16)
    x2c = alloc("x2c", [PD, 1024], BF16)
    xdc = alloc("xdc", [PD, 1024], BF16)
    wpre = alloc("wpre", [PD, NH * PD], BF16)
    cbm = alloc("cbm", [PD, PD], BF16)
    cdd16 = alloc("cdd16", [PD, 8 * PD], BF16)
    y3st = alloc("y3st", [PD, NH * PD], BF16)
    psY = palloc("psY", [PD, 1024], F32)
    psZ = palloc("psZ", [PD, 1024], F32)
    psR = palloc("psR", [PD, 1024], F32)
    psCB = palloc("psCB", [PD, PD], F32)

    def sg_load(ci, slot):
        dst = sgbuf[:, slot * NH * PD:(slot + 1) * NH * PD] \
            .rearrange("p (h t) -> p h t", t=PD)
        src = sgd[:, ci * PD:(ci + 1) * PD].rearrange("(h p) t -> p h t", p=PD)
        dma("y", dst, src, "sSG")

    sg_load(0, 0)

    vmarks = []
    for ci in range(NCH):
        slot = ci % 2
        if ci + 1 < NCH:
            if ci >= 1:
                wait("y", "c_v", vmarks[ci - 1])
            sg_load(ci + 1, (ci + 1) % 2)
        wait("v", "c_t")
        wait("v", "sY3")
        def tcb(E, ci=ci):
            return E.matmul(psCB[:, :], b16[:, ci * CH:(ci + 1) * CH],
                            c16[:, ci * CH:(ci + 1) * CH], start=True, stop=True)
        Tc(tcb)
        wait("v", "c_t")
        Vc(lambda E: E.tensor_tensor(cbm[:, :], psCB[:, :], tri01[:, :], ALU.mult))
        for hg in range(4):
            if hg % 2 == 0:
                wait("y", "c_t")
                dma("y", crow[0:1, :],
                    cum_fm[hg * 8:(hg + 2) * 8, ci * CH:(ci + 1) * CH], "sDT")
            wait("t", "c_v")
            wait("t", "c_s")
            wait("t", "sDT")
            def tcr(E, ci=ci, hg=hg):
                last = None
                for j in (0, 4):
                    last = E.matmul(psR[:, j * PD:(j + 4) * PD], ones_colf[:, :],
                                    crow[0:1, (hg % 2) * 1024 + j * CH:
                                         (hg % 2) * 1024 + (j + 4) * CH],
                                    start=True, stop=True)
                return last
            Tc(tcr)
            wait("s", "c_t")
            def swp(E, ci=ci, hg=hg):
                s = E
                for j in range(8):
                    h = hg * 8 + j
                    s.activation(wpre[:, h * PD:(h + 1) * PD], psR[:, j * PD:(j + 1) * PD],
                                 AF.Relu, scale=-1.0,
                                 bias=cum_tm[:, ci * NH + h:ci * NH + h + 1])
                s.drain()
                last = None
                for j in range(8):
                    h = hg * 8 + j
                    last = s.activation(wpre[:, h * PD:(h + 1) * PD],
                                        wpre[:, h * PD:(h + 1) * PD], AF.Exp, scale=-1.0)
                return last
            Sc(swp)
        wait("v", "c_s")
        def vwp(E):
            return E.tensor_tensor(wpre[:, :].rearrange("p (h q) -> p h q", q=PD),
                                   wpre[:, :].rearrange("p (h q) -> p h q", q=PD),
                                   cbm[:, :].unsqueeze(1).broadcast_to([PD, NH, PD]),
                                   ALU.mult)
        Vc(vwp)
        wait("v", "sSG", 16 * (ci + 1))
        for hg in range(4):
            wait("y", "c_t")
            dma("y", dfrow[0:1, :], df_fm[hg * 8:(hg + 1) * 8, ci * CH:(ci + 1) * CH], "sLD")
            wait("t", "c_v")
            wait("t", "sLD")
            def tdf(E, ci=ci, hg=hg):
                last = None
                for j in (0, 4):
                    last = E.matmul(psR[:, j * PD:(j + 4) * PD], ones_colf[:, :],
                                    dfrow[0:1, j * CH:(j + 4) * CH],
                                    start=True, stop=True)
                return last
            Tc(tdf)
            wait("v", "c_t")
            def vcd(E, ci=ci):
                return E.tensor_tensor(
                    cdd16[:, :].rearrange("p (j q) -> p j q", q=PD),
                    c16[:, ci * CH:(ci + 1) * CH].unsqueeze(1).broadcast_to([PD, 8, PD]),
                    psR[:, 0:1024].rearrange("p (j q) -> p j q", q=PD), ALU.mult)
            Vc(vcd)
            def vxh(E, ci=ci, hg=hg):
                v = E
                xc = x_tm[:, ci * DI + hg * 1024:ci * DI + (hg + 1) * 1024] \
                    .rearrange("p (h q) -> p h q", q=PD)
                dcol = dt_tm[:, ci * NH + hg * 8:ci * NH + (hg + 1) * 8]
                ecol = dte_tm[:, ci * NH + hg * 8:ci * NH + (hg + 1) * 8]
                v.tensor_tensor(x3c[:, :].rearrange("p (h q) -> p h q", q=PD), xc,
                                dcol.unsqueeze(2).broadcast_to([PD, 8, PD]), ALU.mult)
                v.drain()
                v.tensor_tensor(x2c[:, :].rearrange("p (h q) -> p h q", q=PD),
                                x3c[:, :].rearrange("p (h q) -> p h q", q=PD),
                                ecol.unsqueeze(2).broadcast_to([PD, 8, PD]), ALU.mult)
                v.drain()
                return v.tensor_tensor(xdc[:, :].rearrange("p (h q) -> p h q", q=PD), xc,
                                       d16[:, hg * 8:(hg + 1) * 8].unsqueeze(2)
                                       .broadcast_to([PD, 8, PD]), ALU.mult)
            Vc(vxh)
            wait("t", "c_v")
            def ty(E, hg=hg):
                last = None
                for j in range(8):
                    h = hg * 8 + j
                    E.matmul(psY[:, j * PD:(j + 1) * PD], x3c[:, j * PD:(j + 1) * PD],
                             wpre[:, h * PD:(h + 1) * PD], start=(j in (0, 4)), stop=False)
                    E.matmul(psY[:, j * PD:(j + 1) * PD], xdc[:, j * PD:(j + 1) * PD],
                             ident16[:, :], start=False, stop=False)
                    last = E.matmul(psY[:, j * PD:(j + 1) * PD], s16[:, h * PD:(h + 1) * PD],
                                    cdd16[:, j * PD:(j + 1) * PD], start=False,
                                    stop=(j in (3, 7)))
                return last
            Tc(ty)
            def tz2(E, ci=ci, hg=hg):
                last = None
                for j in range(8):
                    last = E.matmul(psZ[:, j * PD:(j + 1) * PD],
                                    btm16[:, ci * PD:(ci + 1) * PD],
                                    x2c[:, j * PD:(j + 1) * PD],
                                    start=(j in (0, 4)), stop=(j in (3, 7)))
                return last
            Tc(tz2)
            wait("v", "c_t")
            def vy(E, ci=ci, hg=hg, slot=slot):
                v = E
                sgv = sgbuf[:, slot * NH * PD + hg * 1024:slot * NH * PD + (hg + 1) * 1024]
                v.tensor_tensor(y3st[:, hg * 1024:(hg + 1) * 1024], psY[:, 0:1024], sgv,
                                ALU.mult)
                v.drain()
                for j in range(8):
                    h = hg * 8 + j
                    v.scalar_tensor_tensor(s_run[:, h * PD:(h + 1) * PD],
                                           s_run[:, h * PD:(h + 1) * PD],
                                           dc_cols[:, h * NCH + ci:h * NCH + ci + 1],
                                           psZ[:, j * PD:(j + 1) * PD], ALU.mult, ALU.add)
                v.drain()
                return v.tensor_copy(s16[:, hg * 1024:(hg + 1) * 1024],
                                     s_run[:, hg * 1024:(hg + 1) * 1024])
            Vc(vy)
        vmarks.append(cnt["c_v"])
        wait("y", "c_v", cnt["c_v"])
        def y3w(E, ci=ci):
            dst = y3d[:, ci * PD:(ci + 1) * PD].rearrange("(h p) t -> p h t", p=PD)
            return E.dma_start(dst, y3st[:, :].rearrange("p (h t) -> p h t", t=PD))
        h_ = sems["sY3"]
        cnt["sY3"] += 16
        emit("y", lambda E, h_=h_, y3w=y3w, ci=ci: y3w(E, ci).then_inc(h_, 16))

    free(psCB)
    free(psR)
    free(psZ)
    free(psY)
    free(y3st)
    free(cdd16)
    free(cbm)
    free(wpre)
    free(xdc)
    free(x2c)
    free(x3c)
    free(sgbuf)
    free(dfrow)
    free(crow)
    free(s16)
    free(s_run)
    free(x_tm)
    barrier()

    free(tmp_s2)
    free(tmp_s)
    free(dtraw)
    free(c_fm)
    free(b_fm)
    free(btm16)
    free(c16)
    free(b16)
    free(df_fm)
    free(cum_fm)
    free(dt_fm)
    barrier()
    xt2 = alloc("xtile2", [PD, D], F32)
    s_full = alloc("s_full", [PD, NCH * D], F32)
    y3sb = alloc("y3sb", [PD, NH * TC], BF16)
    w2b = alloc("w2b", [PD, 2 * 512], BF16)
    psO = palloc("psO", [PD, 4096], F32)

    wait("y", "sY3")
    dma("y", y3sb[:, :].rearrange("p (h t) -> p h t", t=TC),
        y3d[:, :].rearrange("(h p) t -> p h t", p=PD), "sLD")
    w2_marks = []

    def w2_load(di, dq, slot):
        dst = w2b[:, slot * 512:(slot + 1) * 512]
        src = w2[di * PD:(di + 1) * PD, dq * 512:(dq + 1) * 512]
        dma("g", dst, src, "sW2")
        w2_marks.append(cnt["sW2"])

    pairs = [(dq, di) for dq in range(4) for di in range(NH)]
    w2_load(pairs[0][1], pairs[0][0], 0)
    w2_load(pairs[1][1], pairs[1][0], 1)
    wait("t", "sLD")
    for j in range(len(pairs)):
        dq, di = pairs[j]
        if True:
            wait("t", "sW2", w2_marks[j])
            slot = j % 2
            def top(E, dq=dq, di=di, slot=slot):
                last = None
                for tc in range(NCH):
                    last = E.matmul(psO[:, tc * 512:(tc + 1) * 512],
                                    y3sb[:, di * TC + tc * PD:di * TC + (tc + 1) * PD],
                                    w2b[:, slot * 512:(slot + 1) * 512],
                                    start=(di == 0), stop=False)
                return last
            Tc(top)
            if j + 2 < len(pairs):
                wait("g", "c_t")
                nd, ni = pairs[j + 2]
                w2_load(ni, nd, j % 2)
        if di == NH - 1:
            emit_tail = True
        else:
            continue
        def tmr(E, dq=dq):
            last = None
            for tc in range(NCH):
                last = E.matmul(psO[:, tc * 512:(tc + 1) * 512],
                                statrow[0:1, tc * PD:(tc + 1) * PD],
                                ones_row[0:1, 0:512], start=False, stop=True)
            return last
        Tc(tmr)
        for tc in range(NCH):
            r0 = HALO + tc * PD
            wait("y", "c_v")
            dma("y", xt2[:, 0:512], xs[r0:r0 + PD, dq * 512:(dq + 1) * 512], "sXT")
            wait("v", "sXT")
            wait("v", "c_t")
            st = stats[:, (tc + 1) * 8:(tc + 2) * 8]
            def vs5(E, tc=tc, dq=dq, st=st):
                return E.scalar_tensor_tensor(
                    s_full[:, tc * D + dq * 512:tc * D + (dq + 1) * 512],
                    xt2[:, 0:512], st[:, 1:2], psO[:, tc * 512:(tc + 1) * 512],
                    ALU.mult, ALU.add)
            Vc(vs5)
        wait("t", "c_v")
    free(psO)
    free(w2b)
    free(y3sb)
    barrier()

    if DEBUG:
        for e in ("v", "s", "t"):
            wait("y", "c_" + e)
    dbg_dump("sfull", s_full[:, :], [PD, NCH * D], F32)
    dbg_dump("y3", y3d[:, :], [DI, TC], BF16)
    dbg_dump("sg", sgd[:, :], [DI, TC], BF16)

    ffb_row = alloc("ffb_row", [1, D], F32)
    ffb2 = alloc("ffb2", [1, D], F32)
    os_col = alloc("os_col", [PD, 2], F32)
    dma("y", ffb_row[0:1, :], ffb.ap().unsqueeze(0), "sLD")
    wait("v", "sLD")
    def vos(E):
        v = E
        v.tensor_tensor(os_col[:, 0:1], tsx_col[:, 0:1], tsw_col[:, 0:1], ALU.mult)
        v.drain()
        v.reciprocal(os_col[:, 1:2], os_col[:, 0:1])
        v.drain()
        return v.tensor_scalar(ffb2[0:1, :], ffb_row[0:1, :], os_col[0:1, 1:2], None,
                               ALU.mult)
    Vc(vos)
    xdq = alloc("xdq", [PD, KC * TC], BF16)
    alloc_quant()
    p16c = palloc("pt16c", [PD, 1024], BF16)
    for tc in range(NCH):
        r0 = HALO + tc * PD
        wait("y", "c_s")
        dma("y", xt2[:, :], xs[r0:r0 + PD, :], "sXT")
        wait("s", "sXT")
        wait("s", "c_v")
        st = stats[:, (tc + 1) * 8:(tc + 2) * 8]
        Sc(lambda E, st=st, q1=QB["qs1"]: E.activation(q1[:, :], xt2[:, :], AF.Identity,
                                                       bias=st[:, 6:7], scale=st[:, 1:2]))
        wait("v", "c_s")
        Vc(lambda E, q1=QB["qs1"], q2=QB["qs2"]: E.tensor_copy(q2[:, :], q1[:, :]))
        emit_quant(QB["qs2"][:, :], tsx_col)
        wait("t", "c_v")
        for half in range(2):
            def txq(E, half=half, q16=QB["qdq16"]):
                last = None
                for k in range(8):
                    kc = half * 8 + k
                    last = E.transpose(p16c[0:PD, k * PD:(k + 1) * PD],
                                       q16[:, kc * PD:(kc + 1) * PD], ident16[:, :])
                return last
            Tc(txq)
            wait("v", "c_t")
            def vxq(E, tc=tc, half=half):
                last = None
                for k in range(8):
                    kc = half * 8 + k
                    last = E.tensor_copy(xdq[:, kc * TC + tc * PD:kc * TC + (tc + 1) * PD],
                                         p16c[0:PD, k * PD:(k + 1) * PD])
                return last
            Vc(vxq)
            wait("t", "c_v")

    wdqb = alloc("wdqb", [PD, KC * 512], BF16)
    psG = palloc("psG", [PD, 512], F32)
    gtile = alloc("gtile", [PD, 512], F32)
    ftile = alloc("ftile", [PD, 512], F32)
    for ng in range(4):
        for r in range(2):
            dst = wdqb[:, :].rearrange("p (kc n) -> p kc n", n=512)[:, :, r * 256:(r + 1) * 256]
            src = cc_wq_out[2 * ng + r, :, :].rearrange("p (kc n) -> p kc n", n=256)
            dma("y", dst, src, "sFF")
        wait("t", "sFF")
        for tc in range(NCH):
            def tff(E, tc=tc, ng=ng):
                for kc in range(KC):
                    E.matmul(psG[:, 0:512], xdq[:, kc * TC + tc * PD:kc * TC + (tc + 1) * PD],
                             wdqb[:, :].rearrange("p (kc n) -> p kc n", n=512)[:, kc, :],
                             start=(kc == 0), stop=False)
                return E.matmul(psG[:, 0:512], ones_colf[0:1, 0:PD],
                                ffb2[0:1, ng * 512:(ng + 1) * 512], start=False, stop=True)
            Tc(tff)
            wait("s", "c_t")
            Sc(lambda E: E.activation(gtile[:, :], psG[:, 0:512], AF.Silu,
                                      scale=os_col[:, 0:1]))
            r0 = HALO + tc * PD
            wait("y", "c_v")
            dma("y", xt2[:, 0:512], xs[r0:r0 + PD, ng * 512:(ng + 1) * 512], "sXT")
            wait("v", "c_s")
            wait("v", "sXT")
            def vf(E, tc=tc, ng=ng):
                v = E
                v.tensor_tensor(ftile[:, :], s_full[:, tc * D + ng * 512:tc * D + (ng + 1) * 512],
                                gtile[:, :], ALU.mult)
                v.drain()
                return v.tensor_tensor(ftile[:, :], ftile[:, :], xt2[:, 0:512], ALU.add)
            Vc(vf)
            wait("t", "c_v")
            wait("y", "c_v")
            dma("y", out[tc * PD:(tc + 1) * PD, ng * 512:(ng + 1) * 512], ftile[:, :], "sOUT")
            wait("v", "sOUT")

    if DEBUG:
        for e in ("v", "s", "t"):
            wait("y", "c_" + e)
    dbg_dump("xdq", xdq[:, :], [PD, KC * TC], BF16)
    dbg_dump("wq", cc_wq_out[:, :, :].rearrange("a p c -> (a p) c"), [NC * PD, KC * 256], BF16)
    wait("y", "sOUT")

    with nc.Block() as block:
        block.tensor(lambda E: [f(E) for f in ops["t"]])
        block.vector(lambda E: [f(E) for f in ops["v"]])
        block.scalar(lambda E: [f(E) for f in ops["s"]])
        block.gpsimd(lambda E: [f(E) for f in ops["g"]])
        block.sync(lambda E: [f(E) for f in ops["y"]])
    es.close()
    return nc


_NC_CACHE = None


def kernel(**inputs):
    global _NC_CACHE
    x = np.asarray(inputs["L0_attn_x"], np.float32).reshape(8192, D)
    ns = np.ascontiguousarray(np.asarray(inputs["L0_attn_norm_scale"], np.float32))
    w1 = np.ascontiguousarray(np.asarray(inputs["L0_attn_in_proj_w"], np.float32))
    convw = np.ascontiguousarray(np.asarray(inputs["L0_attn_conv_weight"], np.float32))
    convb = np.ascontiguousarray(np.asarray(inputs["L0_attn_conv_bias"], np.float32))
    alog = np.ascontiguousarray(np.asarray(inputs["L0_attn_A_log"], np.float32))
    dtb = np.ascontiguousarray(np.asarray(inputs["L0_attn_dt_bias"], np.float32))
    dvec = np.ascontiguousarray(np.asarray(inputs["L0_attn_D"], np.float32))
    w2 = np.ascontiguousarray(np.asarray(inputs["L0_attn_out_proj_w"], np.float32))
    wff = np.ascontiguousarray(np.asarray(inputs["L0_ffn_weight"], np.float32))
    ffb = np.ascontiguousarray(np.asarray(inputs["L0_ffn_bias"], np.float32))
    xts = np.asarray(inputs["L0_ffn_x_tensor_scale"], np.float32).reshape(1, 1)

    if _NC_CACHE is None:
        _NC_CACHE = build_nc()
    nc = _NC_CACHE

    in_maps = []
    for k in range(NC):
        lo = k * TC
        if k == 0:
            xsk = np.concatenate([np.zeros((HALO, D), np.float32), x[0:TC]], axis=0)
        else:
            xsk = x[lo - HALO:lo + TC]
        pm = np.zeros((1, NC), np.float32)
        pm[0, :k] = 1.0
        in_maps.append(dict(
            xs=np.ascontiguousarray(xsk), ns_in=ns, w1=w1, convw=convw, convb=convb,
            alog=alog, dtb=dtb, dvec=dvec, w2=w2,
            wffs=np.ascontiguousarray(wff[k * 256:(k + 1) * 256, :]),
            ffb=ffb, xts=xts, pmask=pm,
        ))
    res = run_bass_kernel_spmd(nc, in_maps, core_ids=list(range(NC)))
    outs = [np.asarray(res.results[k]["out"]) for k in range(NC)]
    return np.concatenate(outs, axis=0).reshape(1, 8192, D).astype(np.float32)
